# version 03
# baseline (speedup 1.0000x reference)
import numpy as np
import ml_dtypes
import concourse.bass as bass
import concourse.mybir as mybir
from concourse.bass_utils import run_bass_kernel_spmd
from concourse.bass_isa_rust import ReduceOp

F32 = mybir.dt.float32
BF16 = mybir.dt.bfloat16
AX = mybir.AxisListType
OP = mybir.AluOpType
ACT = mybir.ActivationFunctionType

D = 1024
NCORE = 8
CH = 64
NP = 8
FF = 4096
SEG = 256
SEGB = 256
BASE_LR = 0.001
BASE_WD = 0.9


class Plan:

    NROT = 8
    ENGS = ("t", "v", "s", "g", "y")
    SEMS = tuple(["t", "v", "s", "g"]
                 + [f"dy{i}" for i in range(8)] + [f"dg{i}" for i in range(8)])

    def __init__(self):
        self.ops = {e: [] for e in self.ENGS}
        self.cnt = {sm: 0 for sm in self.SEMS}
        self.dma_i = {"y": 0, "g": 0}
        self._waited = {e: {sm: 0 for sm in self.SEMS} for e in self.ENGS}

    def op(self, eng, fn, waits=(), inc=True, dma=False):
        w = list(waits)
        if dma:
            i = self.dma_i[eng]
            self.dma_i[eng] += 1
            sem_name = f"d{eng}{i % self.NROT}"
            amt = 16
            if self.cnt[sem_name] > 0:
                w.append((sem_name, self.cnt[sem_name]))
        else:
            sem_name = eng
            amt = 1
        tok = None
        if inc:
            self.cnt[sem_name] += amt
            tok = self.cnt[sem_name]
        ww = []
        for (sm, c) in w:
            if c is None or c <= 0:
                continue
            if self._waited[eng][sm] >= c:
                continue
            self._waited[eng][sm] = c
            ww.append((sm, c))
        self.ops[eng].append((fn, tuple(ww), sem_name if inc else None, amt))
        return tok

    def barrier_tokens(self):
        return dict(self.cnt)


def emit_engine(plan, eng, engobj, sems):
    for fn, waits, inc_sem, amt in plan.ops[eng]:
        for (sm, c) in waits:
            engobj.wait_ge(sems[sm], c)
        inst = fn(engobj)
        if inc_sem is not None:
            assert inst is not None, f"op on {eng} must return instruction to inc"
            inst.then_inc(sems[inc_sem], amt)


def host_constants():
    c = {}
    c["c_ident"] = np.eye(128, dtype=ml_dtypes.bfloat16)
    c["c_identf"] = np.eye(128, dtype=np.float32)
    sel2 = np.zeros((2, 128), dtype=ml_dtypes.bfloat16)
    sel2[0, 0:64] = 1.0
    sel2[1, 64:128] = 1.0
    c["c_sel2"] = sel2
    mask2 = np.zeros((128, 2), dtype=ml_dtypes.bfloat16)
    mask2[0:64, 0] = 1.0
    mask2[64:128, 1] = 1.0
    c["c_mask2"] = mask2
    lsc = np.zeros((64, 1), dtype=np.float32)
    lsc[0:32, 0] = BASE_LR / CH
    lsc[32:64, 0] = BASE_WD / CH
    c["c_lrwdscale"] = lsc
    c["c_two"] = np.full((1, 128), 2.0, dtype=ml_dtypes.bfloat16)
    c["c_pool"] = np.tile(np.array([[1e-5, -0.5, 0.5, 1.0 / (FF * D), 1e-8]],
                                   dtype=np.float32), (128, 1))
    selbig = np.zeros((64, 32, 128), dtype=ml_dtypes.bfloat16)
    for p in range(NP):
        for q in range(4):
            idx = q * 8 + p
            selbig[q * 16 + 2 * p, idx, 0:64] = 1.0
            selbig[q * 16 + 2 * p + 1, idx, 64:128] = 1.0
    c["c_selbig"] = selbig.reshape(64, 32 * 128)
    return c


def build_program(L=4096, dbg=(), skip_passB=False):
    NSEG = L // SEG
    NCH = SEG // CH
    nc = bass.Bass("TRN2", target_bir_lowering=False)

    dr = {}
    dr["x"] = nc.dram_tensor("x", [L, D], F32, kind="ExternalInput").ap()
    for nm, shp in [("wq", [D, D]), ("wk", [D, D]), ("wv", [D, D]),
                    ("wo", [D, D]), ("wlr", [32, D]), ("wbeta", [32, D]),
                    ("winit", [64, 16 * 64]), ("woutit", [64, 16 * 64]),
                    ("w1", [FF, D]), ("b1", [FF]), ("w2", [D, FF]), ("b2", [D])]:
        dr[nm] = nc.dram_tensor(nm, shp, F32, kind="ExternalInput").ap()
    consts = host_constants()
    for nm, arr in consts.items():
        dt = BF16 if arr.dtype == ml_dtypes.bfloat16 else F32
        dr[nm] = nc.dram_tensor(nm, list(arr.shape), dt, kind="ExternalInput").ap()
    dr["out"] = nc.dram_tensor("out", [L, D], F32, kind="ExternalOutput").ap()
    dr["s_sp"] = nc.dram_tensor("s_sp", [L, D], BF16, kind="Internal").ap()
    dr["xn_sp"] = nc.dram_tensor("xn_sp", [8, 128, L], BF16, kind="Internal").ap()
    dbg_dr = {}
    for nm, shp, dt in dbg:
        dbg_dr[nm] = nc.dram_tensor(nm, shp, dt, kind="ExternalOutput").ap()

    plan = Plan()

    import contextlib
    with contextlib.ExitStack() as st:
        def T(*args, **kw):
            return st.enter_context(nc.sbuf_tensor(*args, **kw))

        def PS(*args, **kw):
            return st.enter_context(nc.psum_tensor(*args, **kw))

        ident = T("ident", [128, 128], BF16)
        identf = T("identf", [128, 128], F32)
        sel2 = T("sel2", [2, 128], BF16)
        mask2 = T("mask2", [128, 2], BF16)
        lrwdscale = T("lrwdscale", [64, 1], F32)
        two_row = T("two_row", [1, 128], BF16)
        cpool = T("cpool", [128, 8], F32)
        nc.const_aps.register(F32, 1e-5, cpool[:, 0:1])
        nc.const_aps.register(F32, -0.5, cpool[:, 1:2])
        nc.const_aps.register(F32, 0.5, cpool[:, 2:3])
        nc.const_aps.register(F32, 1.0 / (FF * D), cpool[:, 3:4])
        nc.const_aps.register(F32, 1e-8, cpool[:, 4:5])
        xt = T("xt", [128, 4, D], F32)
        outt = T("outt", [128, 2, D], F32)
        abf = T("abf", [128, 76 * 1024], BF16)
        af32 = T("af32", [128, 5 * 1024], F32)
        pb = [PS("pb0", [128, 512], F32), PS("pb1", [128, 512], F32),
              PS("pb2", [128, 512], F32), PS("pb3", [128, 512], F32),
              PS("pb4", [128, 512], F32), PS("pb5", [128, 1024], BF16),
              PS("pb6", [128, 1024], BF16), PS("pb7", [128, 512], F32)]
        sems = {nm: st.enter_context(nc.semaphore("sem_" + nm))
                for nm in Plan.SEMS}
        block = st.enter_context(nc.Block())
        tb = dict(ident=ident, identf=identf, sel2=sel2, mask2=mask2,
                  lrwdscale=lrwdscale, two_row=two_row, cpool=cpool,
                  xt=xt, outt=outt, abf=abf, af32=af32, pb=pb)

        emit_all(nc, plan, dr, dbg_dr, tb, L, NSEG, NCH, skip_passB)

        @block.tensor
        def _(t):
            emit_engine(plan, "t", t, sems)

        @block.vector
        def _(v):
            emit_engine(plan, "v", v, sems)

        @block.scalar
        def _(s):
            emit_engine(plan, "s", s, sems)

        @block.gpsimd
        def _(g):
            emit_engine(plan, "g", g, sems)

        @block.sync
        def _(y):
            emit_engine(plan, "y", y, sems)

    return nc


class Arena:

    def __init__(self, tensor, n_elems):
        self.t = tensor
        self.n = n_elems
        self.off = 0
        self.marks = {}
        self.slots = {}

    def alloc(self, name, shape):
        n = int(np.prod(shape[1:]))
        assert self.off + n <= self.n, f"arena overflow at {name}: {self.off}+{n}>{self.n}"
        base = self.off
        self.off += n
        self.slots[name] = (base, shape)
        return self.view(name)

    def view(self, name):
        base, shape = self.slots[name]
        ap = self.t[:shape[0], base:base + int(np.prod(shape[1:]))]
        if len(shape) > 2:
            pat = "p (" + " ".join(f"d{i}" for i in range(1, len(shape))) + ") -> p " + \
                  " ".join(f"d{i}" for i in range(1, len(shape)))
            kw = {f"d{i}": shape[i] for i in range(2, len(shape))}
            ap = ap.rearrange(pat, **kw)
        return ap

    def mark(self, tag):
        self.marks[tag] = self.off

    def reset_to(self, tag):
        self.off = self.marks[tag]



Z0 = ("v", 0)


DEBUG_ARENAS = {}


def emit_all(nc, plan, dr, dbg_dr, tb, L, NSEG, NCH, skip_passB=False):
    def P(eng, fn, waits=(), dma=False):
        if dma:
            sem_name = f"d{eng}{plan.dma_i[eng] % plan.NROT}"
            tok = plan.op(eng, fn, waits=waits, inc=True, dma=True)
            return (sem_name, tok)
        tok = plan.op(eng, fn, waits=waits, inc=True, dma=False)
        return (eng, tok)

    pb = tb["pb"]
    A = Arena(tb["abf"], 76 * 1024)
    A4 = Arena(tb["af32"], 5 * 1024)
    DEBUG_ARENAS["A"] = A
    DEBUG_ARENAS["A4"] = A4
    ident = tb["ident"]
    identf = tb["identf"]

    b2bf = A.alloc("b2bf", [1, 1024])
    A.mark("postperm")
    selbig = A.alloc("selbig", [64, 32, 128])
    wqT = A.alloc("wqT", [128, 8, 1024])
    wkT = A.alloc("wkT", [128, 8, 1024])
    wvT = A.alloc("wvT", [128, 8, 1024])
    woT = A.alloc("woT", [128, 8, 1024])
    wlrbT = A.alloc("wlrbT", [128, 8, 64])
    winT = A.alloc("winT", [128, 8, 64])
    wout = A.alloc("wout", [128, 8, 64])
    woutT = A.alloc("woutT", [128, 8, 64])
    A.mark("postAW")
    wnat = A.alloc("wnat", [128, 8, 1024])
    winit_nat = A.alloc("winit_nat", [64, 1024])
    woutit_nat = A.alloc("woutit_nat", [64, 1024])
    wlrb_nat = A.alloc("wlrb_nat", [64, 1024])
    A.reset_to("postAW")

    b1nat = A4.alloc("b1nat", [32, 128])
    b1cols = A4.alloc("b1cols", [128, 32])
    lrwd_f = A4.alloc("lrwd_f", [64, 2, 256])
    lrwdc = A4.alloc("lrwdc", [64, 2, 4])
    lrwdB = A4.alloc("lrwdB", [128, 2, 4, 8, 4])
    SUMS = A4.alloc("SUMS", [128, 16])
    RECIP = A4.alloc("RECIP", [128, 16])
    CLI = A4.alloc("CLI", [128, 8])
    CLO = A4.alloc("CLO", [128, 8])
    nmq = A4.alloc("nmq", [128, 2])
    bnst = A4.alloc("bnst", [128, 12])
    mv = A4.alloc("mv", [128, 2])
    rstd = A4.alloc("rstd", [128, 1])
    lntmp = A4.alloc("lntmp", [128, 1])
    xt = tb["xt"]
    outt = tb["outt"]

    pb5w = pb[5].ap().rearrange("p (a b) -> p a b", b=128)
    pb5v = pb[5].ap().rearrange("p (a b) -> p a b", b=64)
    pb6w = pb[6].ap().rearrange("p (a b) -> p a b", b=128)
    pb6v = pb[6].ap().rearrange("p (a b) -> p a b", b=64)
    pb7v = pb[7].ap().rearrange("p (a b) -> p a b", b=64)

    dstmap = {"c_ident": ident, "c_identf": identf, "c_sel2": tb["sel2"],
              "c_mask2": tb["mask2"], "c_lrwdscale": tb["lrwdscale"],
              "c_two": tb["two_row"]}
    cy_toks = []
    for nm, dst in dstmap.items():
        cy_toks.append(P("y", lambda e, dst=dst, nm=nm: e.dma_start(dst[:], dr[nm]),
                         dma=True))
    cy_toks.append(P("y", lambda e: e.dma_start(tb["cpool"][:, 0:5], dr["c_pool"]),
                    dma=True))
    cy_toks.append(P("y", lambda e: e.dma_start(selbig.rearrange("p a b -> p (a b)"),
                                                dr["c_selbig"]), dma=True))
    cy_toks.append(P("y", lambda e: e.dma_start(b1nat[:],
                                                dr["b1"].rearrange("(a b) -> a b", b=128)),
                    dma=True))
    cg_toks2 = [
        P("g", lambda e: e.dma_start(winit_nat[:], dr["winit"]), dma=True),
        P("g", lambda e: e.dma_start(woutit_nat[:], dr["woutit"]), dma=True),
        P("g", lambda e: e.dma_start(wlrb_nat[0:32, :], dr["wlr"]), dma=True),
        P("g", lambda e: e.dma_start(wlrb_nat[32:64, :], dr["wbeta"]), dma=True),
    ]
    cg_toks = [P("g", lambda e: e.dma_start(b2bf[:],
                                            dr["b2"].rearrange("(a b) -> a b", a=1)),
                dma=True)]

    t_tok = P("t", lambda e: e.transpose(pb[7][0:128, 0:32], b1nat[:],
                                         identf[0:32, 0:32]),
              waits=cy_toks)
    tok_b1 = P("v", lambda e: e.tensor_copy(b1cols[:], pb[7][0:128, 0:32]),
               waits=[t_tok])

    for h in range(16):
        p_, hh = h // 2, h % 2
        sl = slice(h * 64, (h + 1) * 64)
        hsl = slice(hh * 64, (hh + 1) * 64)
        t_tok = P("t", (lambda e, sl=sl, p_=p_, hh=hh, hsl=hsl:
                        e.transpose(pb5v[hsl, p_, :], winit_nat[:, sl],
                                    ident[0:64, 0:64], tile_position=(0, 64 * hh))),
                  waits=cg_toks + cg_toks2 + cy_toks + [tok_b1])
        t_tok = P("t", (lambda e, sl=sl, p_=p_, hh=hh, hsl=hsl:
                        e.transpose(pb5v[hsl, 8 + p_, :], woutit_nat[:, sl],
                                    ident[0:64, 0:64], tile_position=(0, 64 * hh))))
        t_tok = P("t", (lambda e, sl=sl, p_=p_, hh=hh, hsl=hsl:
                        e.matmul(pb7v[hsl, p_, :], ident[0:64, 0:64],
                                 woutit_nat[:, sl], start=True, stop=True,
                                 skip_group_check=True, tile_position=(0, 64 * hh))))
    P("v", lambda e: e.tensor_copy(winT[:], pb5v[:, 0:8, :]), waits=[t_tok])
    P("v", lambda e: e.tensor_copy(woutT[:], pb5v[:, 8:16, :]))
    tok_state0 = P("v", lambda e: e.tensor_copy(wout[:], pb7v[:, 0:8, :]))

    for fc in range(8):
        t_tok = P("t", (lambda e, fc=fc:
                        e.transpose(pb6v[0:128, 8 + fc, :],
                                    wlrb_nat[:, fc * 128:(fc + 1) * 128],
                                    ident[0:64, 0:64])),
                  waits=[tok_state0])
    tok_evac = P("v", lambda e: e.tensor_copy(wlrbT[:], pb6v[:, 8:16, :]),
                 waits=[t_tok])

    for wname, wT in (("wq", wqT), ("wk", wkT), ("wv", wvT), ("wo", woT)):
        wnat_free = ("t", plan.cnt["t"])
        tok_wload = P("g", (lambda e, wname=wname:
                            e.dma_start(wnat[:],
                                        dr[wname].rearrange("(c p) f -> p c f", p=128))),
                      waits=[wnat_free], dma=True)
        for fc in range(8):
            for c in range(8):
                t_tok = P("t", (lambda e, fc=fc, c=c:
                                e.transpose(pb5w[:, c, :],
                                            wnat[:, c, fc * 128:(fc + 1) * 128],
                                            ident[:])),
                          waits=[tok_wload, tok_evac])
            tok_evac = P("v", (lambda e, wT=wT, fc=fc:
                               e.tensor_copy(wT[:, fc, :],
                                             pb5w.rearrange("p a b -> p (a b)"))),
                         waits=[t_tok])

    qkv = A.alloc("qkv", [128, 2, 8, NCH, 192])
    xn_seg = A.alloc("xn_seg", [128, 2, 8, SEG])
    o_seg = A.alloc("o_seg", [128, 2, 8, SEG])
    s_seg = A.alloc("s_seg", [128, 2, 2, 1024])
    E_sb = A.alloc("E_sb", [128, 8, 128])
    p_q = A.alloc("p_q", [128, 8, 64])
    p_li = A.alloc("p_li", [128, 8, 64])
    m_sb = A.alloc("m_sb", [128, 8, 64])
    a_sb = A.alloc("a_sb", [128, 8, 64])
    t_sb = A.alloc("t_sb", [128, 8, 64])
    gzp = A.alloc("gzp", [128, 8, 128])
    s2_sb = A.alloc("s2_sb", [2, 8, 64])
    rk2 = A.alloc("rk2", [128, 8, 2])
    kvT_sb = A.alloc("kvT_sb", [128, 8, 128])
    gzpT_sb = A.alloc("gzpT_sb", [128, 8, 128])
    xn_tok = A.alloc("xn_tok", [128, 2, 1024])
    lrwdc_bf = A.alloc("lrwdc_bf", [64, 2, 4])
    A.mark("passA_end")

    pbv = [pb[i].ap().rearrange("p (a b) -> p a b", b=64) for i in range(5)]
    views = dict(qkv=qkv, xn_seg=xn_seg, o_seg=o_seg, s_seg=s_seg, E_sb=E_sb,
                 p_q=p_q, p_li=p_li, m_sb=m_sb, a_sb=a_sb, t_sb=t_sb,
                 gzp=gzp, s2_sb=s2_sb, rk2=rk2, kvT_sb=kvT_sb,
                 gzpT_sb=gzpT_sb, winT=winT, wout=wout, woutT=woutT,
                 SUMS=SUMS, RECIP=RECIP, CLI=CLI, CLO=CLO, nmq=nmq,
                 lrwdB=lrwdB, pbv=pbv, pb5w=pb5w, pb6w=pb6w,
                 wqT=wqT, wkT=wkT, wvT=wvT, woT=woT, wlrbT=wlrbT,
                 lrwd_f=lrwd_f, lrwdc=lrwdc, bnst=bnst, mv=mv, rstd=rstd,
                 lntmp=lntmp, xn_tok=xn_tok, selbig=selbig,
                 lrwdc_bf=lrwdc_bf)

    wstream = A4.alloc("wstream", [128, 2, 1024])
    acc1 = A4.alloc("accw1", [128, 1])
    acc2 = A4.alloc("accw2", [128, 1])
    tabs = A4.alloc("tabsw", [128, 1])
    t1bufs = dict(wstream=wstream, acc1=acc1, acc2=acc2, tabs=tabs)

    ck = {
        "t1_i": 0, "t1_ws": [Z0, Z0], "t1_tabs": Z0,
        "upd_win": tok_state0, "upd_wout": tok_state0, "wt_evac": tok_state0,
        "b0": Z0, "b1": Z0, "b2": Z0, "b3": tok_b1, "b4": Z0,
        "b5": tok_evac, "b6": tok_evac, "b7": Z0,
        "eq": Z0, "ek": Z0, "pq": Z0, "pli": Z0, "plo": Z0, "m": Z0, "a": Z0,
        "gz": Z0, "rk2": Z0, "s2": Z0, "kvt": Z0, "gzpt": Z0,
        "nm": Z0, "oseg": Z0, "xt": [Z0, Z0, Z0, Z0], "xntok": [Z0, Z0],
        "mv": Z0, "ln_ev": Z0, "proj_last": ("s", 0), "lrwd_lc": Z0,
        "lrwdc_r": Z0, "o_evac": ("s", 0), "qkv_ready": ("s", 0),
        "lrwdB_ready": Z0,
        "lrwdB_ready_p": [Z0, Z0], "lrwdB_r_p": [Z0, Z0],
        "oseg_p": [Z0, Z0], "xnseg_p": [Z0, Z0], "qkv_w": [Z0, Z0],
        "sseg_p": [Z0, Z0], "lrwdf_p": [Z0, Z0], "xnsp_dma_p": [Z0, Z0],
    }

    views.update(t1bufs)
    for stage in range(NSEG + 1):
        emit_stage(plan, P, dr, tb, views, ck, stage, NSEG, NCH)

    barA = plan.barrier_tokens()
    bar_toks = [(sm, c) for sm, c in barA.items() if c > 0]
    if not skip_passB:
        A.reset_to("postperm")
        emit_passB(nc, plan, P, dr, dbg_dr, tb, A, A4, bar_toks, L,
                   b2bf, b1cols, ident, identf, ck, t1bufs)

    fin = [(sm, plan.cnt[sm]) for sm in Plan.SEMS
           if sm.startswith("d") and plan.cnt[sm] > 0]
    plan.op("y", lambda e: None, waits=fin, inc=False)


def emit_stage(plan, P, dr, tb, V, ck, stage, NSEG, NCH):
    pb = tb["pb"]
    pb5w = V["pb5w"]
    ident = tb["ident"]
    xt = tb["xt"]

    fillers = []
    if stage < NSEG:
        seg = stage
        par = seg % 2
        qkv = V["qkv"][:, par]
        xn_seg = V["xn_seg"][:, par]

        def ln_sub(tsub, seg=seg, par=par, xn_seg=xn_seg):
            gidx = seg * 2 + tsub
            xpar = gidx % 2
            tok_x = P("y", (lambda e, gidx=gidx, xpar=xpar:
                            e.dma_start(xt[:, xpar, :],
                                        dr["x"][gidx * 128:(gidx + 1) * 128, :])),
                      waits=[ck["xt"][xpar]], dma=True)
            tok_bn = P("v", (lambda e, xpar=xpar:
                             e.bn_stats(V["bnst"][:, 0:6], xt[:, xpar, 0:512])),
                       waits=[tok_x, ck["mv"]])
            tok_bn = P("v", (lambda e, xpar=xpar:
                             e.bn_stats(V["bnst"][:, 6:12], xt[:, xpar, 512:1024])))
            tok_mv = P("v", lambda e: e.bn_aggr(V["mv"][:], V["bnst"][:, 0:12]),
                       waits=[tok_bn])
            tok_ln = P("s", (lambda e: e.activation(V["lntmp"][:], V["mv"][:, 1:2],
                                                    ACT.Ln, bias=1e-5)),
                      waits=[tok_mv])
            tok_rstd = P("s", (lambda e: e.activation(V["rstd"][:], V["lntmp"][:],
                                                      ACT.Exp, scale=-0.5)),
                         waits=[tok_ln])
            ck["mv"] = tok_rstd
            tok_xntok = P("v", (lambda e, xpar=xpar:
                                e.tensor_scalar(V["xn_tok"][:, xpar, :],
                                                xt[:, xpar, :],
                                                V["mv"][:, 0:1], V["rstd"][:],
                                                op0=OP.subtract, op1=OP.mult)),
                          waits=[tok_rstd, tok_mv, ck["xntok"][xpar]])
            ck["xt"][xpar] = tok_xntok
            for fc in range(8):
                t_tok = P("t", (lambda e, xpar=xpar, fc=fc:
                                e.transpose(pb5w[:, fc, :],
                                            V["xn_tok"][:, xpar, fc * 128:(fc + 1) * 128],
                                            ident[:])),
                          waits=[tok_xntok, ck["b5"]])
            ck["xntok"][xpar] = t_tok
            tok_ev = P("v", (lambda e, tsub=tsub, xn_seg=xn_seg:
                             e.tensor_copy(xn_seg[:, :, tsub * 128:(tsub + 1) * 128],
                                           pb5w[:, :, :])),
                       waits=[t_tok, ck["xnseg_p"][par], ck["xnsp_dma_p"][par]])
            ck["b5"] = tok_ev
            ck["ln_ev"] = tok_ev

        fillers.append(lambda: ln_sub(0))
        fillers.append(lambda: ln_sub(1))

        def proj_tile(wTname, m, coff, actf, seg=seg, par=par, qkv=qkv,
                      xn_seg=xn_seg):
            wT = V[wTname]
            for fc in range(8):
                t_tok = P("t", (lambda e, wT=wT, m=m, fc=fc:
                                e.matmul(pb[3][:, 0:SEG],
                                         wT[:, fc, m * 128:(m + 1) * 128],
                                         xn_seg[:, fc, :],
                                         start=(fc == 0), stop=(fc == 7),
                                         skip_group_check=True)),
                          waits=[ck["ln_ev"], ck["b3"]])
            bview = pb[3].ap().rearrange("p (a b) -> p a b", b=64)
            tok_evac = P("s", (lambda e, bview=bview, m=m, coff=coff, actf=actf,
                               qkv=qkv:
                               e.activation(qkv[:, m, :, coff:coff + 64],
                                            bview[:, 0:NCH, :], actf)),
                         waits=[t_tok, ck["qkv_w"][par]])
            ck["b3"] = tok_evac
            ck["proj_last"] = tok_evac

        for wTname, coff, actf in (("wqT", 0, ACT.Silu), ("wkT", 64, ACT.Silu),
                                   ("wvT", 128, ACT.Copy)):
            for m in range(8):
                fillers.append(lambda wTname=wTname, m=m, coff=coff, actf=actf:
                               proj_tile(wTname, m, coff, actf))

        def lrwd_proj(seg=seg, par=par, xn_seg=xn_seg):
            for fc in range(8):
                t_tok = P("t", (lambda e, fc=fc:
                                e.matmul(pb[3][0:64, 0:SEG], V["wlrbT"][:, fc, :],
                                         xn_seg[:, fc, :], start=(fc == 0),
                                         stop=(fc == 7), skip_group_check=True)),
                          waits=[ck["ln_ev"], ck["b3"]])
            tok_sg = P("s", (lambda e, par=par:
                             e.activation(V["lrwd_f"][:, par, :], pb[3][0:64, 0:SEG],
                                          ACT.Sigmoid)),
                       waits=[t_tok, ck["lrwdf_p"][par]])
            ck["b3"] = tok_sg
            tok_rd = P("v", (lambda e, par=par:
                             e.reduce_sum(V["lrwdc"][:, par, :],
                                          V["lrwd_f"][:, par]
                                          .rearrange("p (a b) -> p a b", b=64),
                                          axis=AX.X)),
                      waits=[tok_sg, ck["lrwdc_r"]])
            ck["lrwdf_p"][par] = tok_rd
            tok_lc = P("v", (lambda e, par=par:
                             e.tensor_scalar(V["lrwdc_bf"][:, par, :],
                                             V["lrwdc"][:, par, :],
                                             tb["lrwdscale"][:], None,
                                             op0=OP.mult)), waits=[tok_rd])
            ck["lrwd_lc"] = tok_lc
            ck["xnseg_p"][par] = ("t", plan.cnt["t"])

        def lrwd_bcast(seg=seg, par=par):
            pb3b = pb[3].ap().rearrange("p (a b) -> p a b", b=4)
            for idx in range(32):
                t_tok = P("t", (lambda e, idx=idx, par=par:
                                e.matmul(pb3b[:, idx, :],
                                         V["selbig"][:, idx, :],
                                         V["lrwdc_bf"][:, par, :], start=True,
                                         stop=True, skip_group_check=True)),
                          waits=[ck["lrwd_lc"], ck["b3"]])
            tok_lB = P("v", (lambda e, par=par:
                             e.tensor_copy(V["lrwdB"][:, par]
                                           .rearrange("p a b c -> p (a b c)"),
                                           pb[3][:, 0:128])),
                       waits=[t_tok, ck["lrwdB_r_p"][par]])
            ck["b3"] = tok_lB
            ck["lrwdB_ready_p"][par] = tok_lB
            ck["lrwdc_r"] = ck["lrwd_lc"]

        fillers.append(lrwd_proj)
        fillers.append(lrwd_bcast)

        def xn_spill(seg=seg, par=par, xn_seg=xn_seg):
            tok = P("g", (lambda e, seg=seg, xn_seg=xn_seg:
                          e.dma_start(dr["xn_sp"][:, :, seg * SEG:(seg + 1) * SEG]
                                      .rearrange("a p t -> p a t"),
                                      xn_seg[:])),
                    waits=[ck["ln_ev"]], dma=True)
            ck["xnsp_dma_p"][par] = tok

        fillers.append(xn_spill)
        n_t1 = (64 + NSEG - 1) // NSEG
        for _ in range(n_t1):
            fillers.append(lambda: emit_t1_tiles(plan, P, dr, V, ck, 1))

    if stage == 0:
        for f in fillers:
            f()
        ck["qkv_ready"] = ck["proj_last"]
        return

    seg = stage - 1
    par = seg % 2
    Vp = dict(V)
    Vp["qkv"] = V["qkv"][:, par]
    Vp["o_seg"] = V["o_seg"][:, par]
    Vp["lrwdB"] = V["lrwdB"][:, par]
    ck["lrwdB_ready"] = ck["lrwdB_ready_p"][par]
    ck["oseg"] = ck["oseg_p"][par]
    nf = len(fillers)
    per = max(1, (nf + NCH - 1) // NCH) if nf else 0
    fi = 0
    for ci in range(NCH):
        emit_scan_chunk(plan, P, tb, Vp, ck, ci)
        for _ in range(per):
            if fi < nf:
                fillers[fi]()
                fi += 1
    while fi < nf:
        fillers[fi]()
        fi += 1
    if stage < NSEG:
        ck["qkv_ready"] = ck["proj_last"]
    ck["qkv_w"][par] = ("t", plan.cnt["t"])
    ck["lrwdB_r_p"][par] = ck["upd_win"]

    o_seg = V["o_seg"][:, par]
    s_seg = V["s_seg"][:, par]
    tok_sev = ("s", 0)
    for tsub in range(2):
        for half in range(2):
            for fc in range(8):
                t_tok = P("t", (lambda e, tsub=tsub, half=half, fc=fc,
                                o_seg=o_seg:
                                e.matmul(pb[3][:, :],
                                         o_seg[:, fc, tsub * 128:(tsub + 1) * 128],
                                         V["woT"][:, fc, half * 512:(half + 1) * 512],
                                         start=(fc == 0), stop=(fc == 7),
                                         skip_group_check=True)),
                          waits=[ck["o_evac"], ck["b3"]])
            tok_sev = P("s", (lambda e, tsub=tsub, half=half, s_seg=s_seg:
                              e.activation(s_seg[:, tsub, half * 512:(half + 1) * 512],
                                           pb[3][:, :], ACT.Copy)),
                        waits=[t_tok, ck["sseg_p"][par]])
            ck["b3"] = tok_sev
    ck["oseg_p"][par] = ("t", plan.cnt["t"])
    tok_sdma = P("g", (lambda e, seg=seg, s_seg=s_seg:
                       e.dma_start(dr["s_sp"][seg * SEG:(seg + 1) * SEG, :]
                                   .rearrange("(a p) f -> p a f", p=128),
                                   s_seg[:])),
                 waits=[tok_sev], dma=True)
    ck["sseg_p"][par] = tok_sdma


def emit_scan_chunk(plan, P, tb, V, ck, ci):
    pb = tb["pb"]
    pbv = V["pbv"]
    pb5w, pb6w = V["pb5w"], V["pb6w"]
    ident = tb["ident"]
    qkv, winT, wout, woutT = V["qkv"], V["winT"], V["wout"], V["woutT"]
    E_sb, p_q, p_li = V["E_sb"], V["p_q"], V["p_li"]
    m_sb, a_sb, t_sb, gzp = V["m_sb"], V["a_sb"], V["t_sb"], V["gzp"]
    s2_sb, rk2 = V["s2_sb"], V["rk2"]
    kvT_sb, gzpT_sb = V["kvT_sb"], V["gzpT_sb"]
    SUMS, RECIP, CLI, CLO = V["SUMS"], V["RECIP"], V["CLI"], V["CLO"]
    nmq, lrwdB = V["nmq"], V["lrwdB"]
    pb5WT = pb[5].ap().rearrange("p (a b) -> p a b", b=64)
    pbz = [pb[0].ap().rearrange("p (a b) -> p a b", b=128),
           pb[1].ap().rearrange("p (a b) -> p a b", b=128)]

    def heads():
        for p in range(NP):
            for hh in range(2):
                yield p, hh, slice(hh * 64, (hh + 1) * 64), (0 if hh == 0 else 64)

    for p, hh, hsl, off in heads():
        t_tok = P("t", (lambda e, p=p, hh=hh, hsl=hsl, off=off:
                        e.matmul(pbz[p // 4][hsl, p % 4, :], winT[hsl, p, :],
                                 qkv[hsl, p, ci, 0:128], start=True, stop=True,
                                 skip_group_check=True, tile_position=(off, off))),
                  waits=[ck["upd_win"], ck["qkv_ready"],
                         ck["b0"] if p < 4 else ck["b1"]])
    tok_z = t_tok

    for p in range(NP):
        t_tok = P("t", (lambda e, p=p:
                        e.transpose(pb5w[:, p, :], qkv[:, p, ci, 64:192],
                                    ident[:])),
                  waits=[ck["b5"]])
    tok_trkv = t_tok
    tok_kvt = P("v", lambda e: e.tensor_copy(kvT_sb[:], pb5w[:, 0:8, :]),
                waits=[tok_trkv, ck["kvt"]])
    ck["b5"] = tok_kvt

    for p, hh, hsl, off in heads():
        t_tok = P("t", (lambda e, p=p, hh=hh, hsl=hsl, off=off:
                        e.matmul(pbv[2][hsl, p, :], woutT[hsl, p, :],
                                 qkv[hsl, p, ci, 128:192], start=True, stop=True,
                                 skip_group_check=True, tile_position=(off, off))),
                  waits=[ck["wt_evac"], ck["b2"]])
    tok_amm = t_tok

    tok_nmq = P("v", lambda e: e.reduce_max(nmq[:, 0:1], pb[0][:, :], axis=AX.X,
                                            negate=True),
                waits=[tok_z, ck["nm"]])
    tok_nmk = P("v", (lambda e: e.reduce_max(nmq[:, 1:2], pb[1][:, :], axis=AX.X,
                                             negate=True)))
    tok_nm2 = P("v", lambda e: e.tensor_tensor(nmq[:, 0:1], nmq[:, 0:1],
                                               nmq[:, 1:2], OP.min),
                waits=[tok_nmk])
    tok_eqa = P("s", (lambda e: e.activation(E_sb[:, 0:4, :], pb[0][:, :],
                                             ACT.Exp, bias=nmq[:, 0:1])),
                waits=[tok_nm2, ck["eq"]])
    tok_eqb = P("s", (lambda e: e.activation(E_sb[:, 4:8, :], pb[1][:, :],
                                             ACT.Exp, bias=nmq[:, 0:1])))
    ck["b0"] = tok_eqa
    ck["b1"] = tok_eqb
    ck["nm"] = tok_eqb
    tok_sums = P("v", (lambda e:
                       e.reduce_sum(SUMS[:], E_sb.rearrange("p a (b c) -> p (a b) c", c=64),
                                    axis=AX.X)),
                 waits=[tok_eqb])
    tok_rec = P("v", lambda e: e.reciprocal(RECIP[:], SUMS[:]),
                waits=[tok_sums])
    rk_cols = RECIP.rearrange("p (a b) -> p a b", b=2)[:, :, 1]
    tok_cli = P("v", (lambda e:
                      e.tensor_tensor(CLI[:], rk_cols,
                                      lrwdB[:, 0, :, ci], OP.mult)),
                waits=[ck["lrwdB_ready"], tok_rec])
    tok_clo = P("v", (lambda e:
                      e.tensor_tensor(CLO[:], rk_cols,
                                      lrwdB[:, 1, :, ci], OP.mult)))
    for p in range(NP):
        tok_pq = P("v", (lambda e, p=p:
                         e.tensor_scalar(p_q[:, p, :], E_sb[:, p, 0:64],
                                         RECIP[:, 2 * p:2 * p + 1], None,
                                         op0=OP.mult)),
                   waits=[ck["pq"], tok_rec])
        tok_pli = P("v", (lambda e, p=p:
                          e.tensor_scalar(p_li[:, p, :], E_sb[:, p, 64:128],
                                          CLI[:, p:p + 1], None, op0=OP.mult)),
                    waits=[ck["pli"], tok_clo])
        tok_plo = P("v", (lambda e, p=p:
                          e.tensor_scalar(gzp[:, p, 64:128], E_sb[:, p, 64:128],
                                          CLO[:, p:p + 1], None, op0=OP.mult)),
                    waits=[ck["plo"]])
        tok_rk2 = P("v", (lambda e, p=p:
                          e.tensor_scalar(rk2[:, p, :], tb["mask2"][:],
                                          RECIP[:, 2 * p + 1:2 * p + 2], None,
                                          op0=OP.mult)),
                    waits=[ck["rk2"]])
    tok_eqfree = tok_rk2

    for p, hh, hsl, off in heads():
        t_tok = P("t", (lambda e, p=p, hh=hh, hsl=hsl, off=off:
                        e.matmul(pbv[3][hsl, p, :], wout[hsl, p, :],
                                 p_q[hsl, p, :], start=True, stop=True,
                                 skip_group_check=True, tile_position=(off, off))),
                  waits=[ck["upd_wout"], ck["b3"], tok_pq])
    tok_omm = t_tok
    ck["pq"] = tok_omm
    tok_oev = P("s", (lambda e, ci=ci:
                      e.activation(V["o_seg"][:, :, ci * 64:(ci + 1) * 64],
                                   pbv[3][:, 0:8, :], ACT.Copy)),
                waits=[tok_omm, ck["oseg"]])
    ck["b3"] = tok_oev
    ck["o_evac"] = tok_oev

    tok_aev = P("s", lambda e: e.activation(a_sb[:], pbv[2][:, 0:8, :], ACT.Copy),
                waits=[tok_amm, ck["a"]])
    tok_m = P("v", (lambda e:
                    e.tensor_tensor(m_sb[:],
                                    E_sb[:, :, 64:128], pbv[2][:, 0:8, :],
                                    OP.mult)),
              waits=[tok_eqb, tok_aev, ck["m"]])
    ck["b2"] = tok_m
    ck["eq"] = tok_eqfree
    ck["ek"] = tok_m
    for p in range(NP):
        t_tok = P("t", (lambda e, p=p:
                        e.matmul(pbv[4][0:2, p, :], rk2[:, p, :], m_sb[:, p, :],
                                 start=True, stop=True, skip_group_check=True)),
                  waits=[tok_m, tok_rk2, ck["b4"]])
    tok_s2mm = t_tok
    ck["rk2"] = tok_s2mm
    ck["m"] = tok_s2mm
    tok_s2ev = P("s", lambda e: e.activation(s2_sb[:], pbv[4][0:2, 0:8, :],
                                             ACT.Copy),
                 waits=[tok_s2mm, ck["s2"]])
    tok_sb = P("t", (lambda e:
                     e.matmul(pb[4][:, :], tb["sel2"][:],
                              s2_sb.rearrange("p a b -> p (a b)"),
                              start=True, stop=True, skip_group_check=True)),
              waits=[tok_s2ev])
    ck["s2"] = tok_sb
    tok_t = P("v", lambda e: e.tensor_tensor(t_sb[:], pb[4].ap()
                                             .rearrange("p (a b) -> p a b", b=64),
                                             a_sb[:], OP.subtract),
              waits=[tok_sb, tok_aev])
    ck["b4"] = tok_t
    ck["a"] = tok_t
    tok_gz = P("v", (lambda e:
                     e.tensor_tensor(gzp[:, :, 0:64],
                                     t_sb[:], p_li[:], OP.mult)),
               waits=[tok_pli, tok_t, ck["gz"]])
    ck["pli"] = tok_gz

    for p in range(NP):
        t_tok = P("t", (lambda e, p=p:
                        e.transpose(pb6w[:, p, :], gzp[:, p, :], ident[:])),
                  waits=[tok_gz, tok_plo, ck["b6"]])
    tok_trgzp = t_tok
    ck["gz"] = tok_trgzp
    ck["plo"] = tok_trgzp
    tok_gzpt = P("s", lambda e: e.activation(gzpT_sb[:], pb6w[:, 0:8, :],
                                             ACT.Copy),
                 waits=[tok_trgzp, ck["gzpt"]])
    ck["b6"] = tok_gzpt

    for p, hh, hsl, off in heads():
        t_tok = P("t", (lambda e, p=p, hh=hh, hsl=hsl, off=off:
                        e.matmul(pbv[0][hsl, p, :],
                                 gzpT_sb[64:128, p, hh * 64:(hh + 1) * 64],
                                 kvT_sb[64:128, p, hh * 64:(hh + 1) * 64],
                                 start=True, stop=True, skip_group_check=True,
                                 tile_position=(64, off))),
                  waits=[tok_gzpt, tok_kvt, ck["b0"]])
    tok_g11 = t_tok
    for p, hh, hsl, off in heads():
        t_tok = P("t", (lambda e, p=p, hh=hh, hsl=hsl, off=off:
                        e.matmul(pbv[1][hsl, p, :],
                                 kvT_sb[0:64, p, hh * 64:(hh + 1) * 64],
                                 gzpT_sb[0:64, p, hh * 64:(hh + 1) * 64],
                                 start=True, stop=True, skip_group_check=True,
                                 tile_position=(0, off))),
                  waits=[ck["b1"]])
    tok_g13 = t_tok
    ck["kvt"] = tok_g13
    ck["gzpt"] = tok_g13

    for p in range(NP):
        tok_uo = P("v", (lambda e, p=p:
                         e.scalar_tensor_tensor(wout[:, p, :], wout[:, p, :],
                                                lrwdB[:, 3, p, ci:ci + 1],
                                                pbv[0][:, p, :], op0=OP.mult,
                                                op1=OP.add)),
                   waits=[tok_g11, tok_omm, ck["upd_wout"], ck["lrwdB_ready"]])
        tok_ui = P("v", (lambda e, p=p:
                         e.scalar_tensor_tensor(winT[:, p, :], winT[:, p, :],
                                                lrwdB[:, 2, p, ci:ci + 1],
                                                pbv[1][:, p, :], op0=OP.mult,
                                                op1=OP.subtract)),
                   waits=[tok_g13, tok_z, ck["upd_win"]])
    ck["upd_win"] = tok_ui
    ck["upd_wout"] = tok_uo
    ck["b0"] = tok_uo
    ck["b1"] = tok_ui

    for p, hh, hsl, off in heads():
        t_tok = P("t", (lambda e, p=p, hh=hh, hsl=hsl, off=off:
                        e.transpose(pb5WT[hsl, p, :], wout[hsl, p, :],
                                    ident[hsl, hsl], tile_position=(off, off))),
                  waits=[tok_uo, ck["b5"]])
    tok_trw = t_tok
    tok_wt = P("s", lambda e: e.activation(woutT[:], pb5WT[:, 0:8, :], ACT.Copy),
               waits=[tok_trw])
    ck["b5"] = tok_wt
    ck["wt_evac"] = tok_wt


def w2_tile_src_g(dr, c):
    rt, cc = c // 4, c % 4
    return dr["w2"][rt * 128:(rt + 1) * 128, cc * 1024:(cc + 1) * 1024]


def emit_t1_tiles(plan, P, dr, V, ck, n):
    wstream, tabs = V["wstream"], V["tabs"]
    for _ in range(n):
        i = ck["t1_i"]
        if i >= 64:
            return
        ck["t1_i"] += 1
        par = i % 2
        if i < 32:
            srcap = dr["w1"][i * 128:(i + 1) * 128, :]
            acc = V["acc1"]
        else:
            srcap = w2_tile_src_g(dr, i - 32)
            acc = V["acc2"]
        tok_l = P("y", (lambda e, srcap=srcap, par=par:
                        e.dma_start(wstream[:, par, :], srcap)),
                  waits=[ck["t1_ws"][par]], dma=True)
        tok_tb = P("v", (lambda e, par=par:
                         e.reduce_sum(tabs[:], wstream[:, par, :], axis=AX.X,
                                      apply_absolute_value=True)),
                   waits=[tok_l, ck["t1_tabs"]])
        if i % 32 == 0:
            tok_a = P("v", (lambda e, acc=acc: e.tensor_copy(acc[:], tabs[:])),
                      waits=[tok_tb])
        else:
            tok_a = P("v", (lambda e, acc=acc:
                            e.tensor_tensor(acc[:], acc[:], tabs[:], OP.add)),
                      waits=[tok_tb])
        ck["t1_ws"][par] = tok_a
        ck["t1_tabs"] = tok_a
    return


def emit_passB(nc, plan, P, dr, dbg_dr, tb, A, A4, bar_toks, L,
               b2bf, b1cols, ident, identf, ck, t1bufs):
    pb = tb["pb"]
    xt = tb["xt"]
    outt = tb["outt"]
    pb5w = pb[5].ap().rearrange("p (a b) -> p a b", b=128)
    pb6w = pb[6].ap().rearrange("p (a b) -> p a b", b=128)
    NBT = L // SEGB
    wstream = t1bufs["wstream"]
    acc1, acc2, tabs = t1bufs["acc1"], t1bufs["acc2"], t1bufs["tabs"]

    for e in Plan.ENGS:
        plan.op(e, lambda _e: None, waits=bar_toks, inc=False)

    w1T = A.alloc("w1T", [128, 8, FF])
    w2T = A.alloc("w2T", [128, 32, 1024])
    A.mark("tern_tmp")
    w12scr = A.alloc("w12scr", [128, 1024])
    sgA = A.alloc("sgA", [128, 1024])
    A.reset_to("tern_tmp")
    hT = A.alloc("hT", [128, 2, SEGB])
    xnB = A.alloc("xnB", [128, 2, 8, SEGB])
    s_rd = A.alloc("s_rd", [128, 2, 1024])
    g_sb = A.alloc("g_sb", [128, 2, 1024])

    tot = A4.alloc("totw", [1, 1])
    inv1 = A4.alloc("invw1", [128, 1])
    inv2 = A4.alloc("invw2", [128, 1])
    ones_col = A4.alloc("ones_col", [128, 1])
    ones_row = A4.alloc("ones_row", [1, 128])


    tok_o1 = P("v", lambda e: e.memset(ones_col[:], 1.0))
    tok_o2 = P("v", lambda e: e.memset(ones_row[:], 1.0))
    tabs_free = ck["t1_tabs"]
    for acc, inv in ((acc1, inv1), (acc2, inv2)):
        t_tok = P("t", (lambda e, acc=acc:
                        e.matmul(pb[7][0:1, 0:1], ones_col[:], acc[:],
                                 start=True, stop=True, skip_group_check=True)),
                  waits=[ck["t1_tabs"], tok_o1])
        tok_cp = P("v", lambda e: e.tensor_copy(tot[0:1, :], pb[7][0:1, 0:1]),
                   waits=[t_tok])
        t_tok = P("t", (lambda e:
                        e.matmul(pb[7][0:128, 1:2], ones_row[:], tot[0:1, :],
                                 start=True, stop=True, skip_group_check=True)),
                  waits=[tok_cp, tok_o2])
        tok_t1 = P("v", (lambda e: e.tensor_scalar(tabs[:], pb[7][0:128, 1:2],
                                                   1.0 / (FF * D), None,
                                                   op0=OP.mult)),
                   waits=[t_tok, tabs_free])
        tok_t2 = P("v", (lambda e: e.tensor_scalar(tabs[:], tabs[:], 1e-8, None,
                                                   op0=OP.max)), waits=[tok_t1])
        tok_i = P("v", (lambda e, inv=inv: e.reciprocal(inv[:], tabs[:])),
                  waits=[tok_t2])
        tabs_free = tok_i

    ws_free = list(ck["t1_ws"])
    scr_free = Z0
    tr_banks = [pb5w, pb6w]
    bank_ev = [Z0, Z0]
    tok_trev = Z0
    for wname, wT, inv in (("w1", w1T, inv1), ("w2", w2T, inv2)):
        for c in range(32):
            par = c % 2
            srcap = (dr["w1"][c * 128:(c + 1) * 128, :] if wname == "w1"
                     else w2_tile_src_g(dr, c))
            tok_l = P("y", (lambda e, srcap=srcap, par=par:
                            e.dma_start(wstream[:, par, :], srcap)),
                      waits=[ws_free[par]], dma=True)
            tok_sa = P("s", (lambda e, par=par, inv=inv:
                             e.activation(w12scr[:], wstream[:, par, :], ACT.Sign,
                                          bias=-0.5, scale=inv[:])),
                       waits=[tok_l, tok_i, scr_free])
            tok_sb_ = P("s", (lambda e, par=par, inv=inv:
                              e.activation(sgA[:], wstream[:, par, :], ACT.Sign,
                                           bias=0.5, scale=inv[:])))
            ws_free[par] = tok_sb_
            tok_w = P("v", lambda e: e.tensor_tensor(w12scr[:], w12scr[:], sgA[:],
                                                     OP.add),
                      waits=[tok_sb_])
            pbx = tr_banks[par]
            for j in range(8):
                t_tok = P("t", (lambda e, j=j, pbx=pbx:
                                e.transpose(pbx[:, j, :],
                                            w12scr[:, j * 128:(j + 1) * 128],
                                            ident[:])),
                          waits=[tok_w, bank_ev[par]])
            scr_free = t_tok
            if wname == "w1":
                dst = w1T[:, :, c * 128:(c + 1) * 128]
            else:
                rt, cc = c // 4, c % 4
                dst = w2T[:, cc * 8:(cc + 1) * 8, rt * 128:(rt + 1) * 128]
            tok_trev = P("v", (lambda e, dst=dst, pbx=pbx:
                               e.tensor_copy(dst, pbx[:, :, :])),
                         waits=[t_tok])
            bank_ev[par] = tok_trev
    tok_tern = tok_trev
    tok_tern_t = ("t", plan.cnt["t"])

    xt_free = [Z0, Z0, Z0, Z0]
    h_free = [Z0, Z0]
    bank_free = {i: Z0 for i in range(4)}
    mm1_free = {0: Z0, 1: tabs_free}
    g_free = [Z0, Z0]
    srd_free = Z0
    out_free = [Z0, Z0]
    xnB_free = [Z0, Z0]
    xnB_tok = [Z0, Z0]
    srd_tok = Z0

    def dma_xnB(bt):
        par = bt % 2
        tok = P("y", (lambda e, bt=bt, par=par:
                      e.dma_start(xnB[:, par, :, :],
                                  dr["xn_sp"][:, :, bt * SEGB:(bt + 1) * SEGB]
                                  .rearrange("a p t -> p a t"))),
                waits=[xnB_free[par], tok_tern], dma=True)
        xnB_tok[par] = tok

    dma_xnB(0)
    for bt in range(NBT):
        par = bt % 2
        if bt + 1 < NBT:
            dma_xnB(bt + 1)
        srd_tok = P("y", (lambda e, bt=bt:
                          e.dma_start(s_rd[:, 0:2, :],
                                      dr["s_sp"][bt * SEGB:(bt + 1) * SEGB, :]
                                      .rearrange("(a p) f -> p a f", p=128))),
                    waits=[srd_free, tok_tern], dma=True)
        for tsub in range(2):
            gidx = bt * 2 + tsub
            slot = gidx % 4
            xt_tok = P("y", (lambda e, gidx=gidx, slot=slot:
                             e.dma_start(xt[:, slot, :],
                                         dr["x"][gidx * 128:(gidx + 1) * 128, :])),
                       waits=[xt_free[slot]], dma=True)
            if tsub == 0:
                xt_tok0 = xt_tok
            else:
                xt_tok1 = xt_tok
        for f1c in range(32):
            hpar = f1c % 2
            bank = pb[4] if hpar == 0 else pb[7]
            for fc in range(8):
                t_tok = P("t", (lambda e, f1c=f1c, fc=fc, bank=bank, par=par:
                                e.matmul(bank[:, 0:SEGB],
                                         w1T[:, fc, f1c * 128:(f1c + 1) * 128],
                                         xnB[:, par, fc, :], start=(fc == 0),
                                         stop=(fc == 7), skip_group_check=True)),
                          waits=[xnB_tok[par], mm1_free[hpar], tok_tern_t])
            tok_mm1 = t_tok
            if f1c > 0:
                pf1c, phpar = f1c - 1, (f1c - 1) % 2
                for tsub in range(2):
                    for half in range(2):
                        j = tsub * 2 + half
                        t_tok = P("t", (lambda e, pf1c=pf1c, phpar=phpar,
                                        tsub=tsub, half=half, j=j:
                                        e.matmul(pb[j][:, :],
                                                 hT[:, phpar, tsub * 128:(tsub + 1) * 128],
                                                 w2T[:, pf1c, half * 512:(half + 1) * 512],
                                                 start=(pf1c == 0), stop=False,
                                                 skip_group_check=True)),
                                  waits=[tok_gelu_prev, bank_free[j]])
                mm1_free[phpar] = t_tok
                h_free[phpar] = t_tok
            tok_gelu_prev = P("s", (lambda e, hpar=hpar, f1c=f1c, bank=bank:
                                    e.activation(hT[:, hpar, :], bank[:, 0:SEGB],
                                                 ACT.Gelu, scale=0.5,
                                                 bias=b1cols[:, f1c:f1c + 1])),
                              waits=[tok_mm1, h_free[hpar], tok_tern])
        pf1c, phpar = 31, 31 % 2
        for tsub in range(2):
            for half in range(2):
                j = tsub * 2 + half
                t_tok = P("t", (lambda e, pf1c=pf1c, phpar=phpar, tsub=tsub,
                                half=half, j=j:
                                e.matmul(pb[j][:, :],
                                         hT[:, phpar, tsub * 128:(tsub + 1) * 128],
                                         w2T[:, pf1c, half * 512:(half + 1) * 512],
                                         start=False, stop=False,
                                         skip_group_check=True)),
                          waits=[tok_gelu_prev, bank_free[j]])
        mm1_free[phpar] = t_tok
        h_free[phpar] = t_tok
        for tsub in range(2):
            for half in range(2):
                j = tsub * 2 + half
                t_tok = P("t", (lambda e, j=j, half=half:
                                e.matmul(pb[j][:, :], tb["two_row"][:],
                                         b2bf[0:1, half * 512:(half + 1) * 512],
                                         start=False, stop=True,
                                         skip_group_check=True)))
                tok_g = P("s", (lambda e, j=j, tsub=tsub, half=half:
                                e.activation(g_sb[:, tsub, half * 512:(half + 1) * 512],
                                             pb[j][:, :], ACT.Silu, scale=0.5)),
                          waits=[t_tok, g_free[tsub]])
                bank_free[j] = tok_g
            gidx = bt * 2 + tsub
            slot = gidx % 4
            opar = gidx % 2
            xt_tok = xt_tok0 if tsub == 0 else xt_tok1
            tok_c1 = P("v", (lambda e, tsub=tsub, opar=opar:
                             e.tensor_tensor(outt[:, opar, :], s_rd[:, tsub, :],
                                             g_sb[:, tsub, :], OP.mult)),
                       waits=[tok_g, srd_tok, out_free[opar]])
            tok_c2 = P("v", (lambda e, slot=slot, opar=opar:
                             e.tensor_tensor(outt[:, opar, :], outt[:, opar, :],
                                             xt[:, slot, :], OP.add)),
                       waits=[tok_c1, xt_tok])
            g_free[tsub] = tok_c2
            xt_free[slot] = tok_c2
            tok_od = P("g", (lambda e, gidx=gidx, opar=opar:
                             e.dma_start(dr["out"][gidx * 128:(gidx + 1) * 128, :],
                                         outt[:, opar, :])),
                       waits=[tok_c2], dma=True)
            out_free[opar] = tok_od
            if tsub == 1:
                srd_free = tok_c1
        xnB_free[par] = ("t", plan.cnt["t"])


_NC_CACHE = {}


def _get_program(L=4096):
    if L not in _NC_CACHE:
        _NC_CACHE[L] = build_program(L=L)
    return _NC_CACHE[L]


def kernel(L0_attn_x, L0_attn_Wq, L0_attn_Wk, L0_attn_Wv, L0_attn_Wlr,
           L0_attn_Wbeta, L0_attn_Wo, L0_attn_W_in_init, L0_attn_W_out_init,
           L0_ffn_W1, L0_ffn_b1, L0_ffn_W2, L0_ffn_b2):
    x = np.ascontiguousarray(np.asarray(L0_attn_x, dtype=np.float32))
    B, L, _ = x.shape
    base = {
        "wq": np.ascontiguousarray(np.asarray(L0_attn_Wq, np.float32)),
        "wk": np.ascontiguousarray(np.asarray(L0_attn_Wk, np.float32)),
        "wv": np.ascontiguousarray(np.asarray(L0_attn_Wv, np.float32)),
        "wo": np.ascontiguousarray(np.asarray(L0_attn_Wo, np.float32)),
        "wlr": np.ascontiguousarray(np.asarray(L0_attn_Wlr, np.float32)),
        "wbeta": np.ascontiguousarray(np.asarray(L0_attn_Wbeta, np.float32)),
        "winit": np.ascontiguousarray(
            np.asarray(L0_attn_W_in_init, np.float32).reshape(64, 16 * 64)),
        "woutit": np.ascontiguousarray(
            np.asarray(L0_attn_W_out_init, np.float32).reshape(64, 16 * 64)),
        "w1": np.ascontiguousarray(np.asarray(L0_ffn_W1, np.float32)),
        "b1": np.ascontiguousarray(np.asarray(L0_ffn_b1, np.float32)),
        "w2": np.ascontiguousarray(np.asarray(L0_ffn_W2, np.float32)),
        "b2": np.ascontiguousarray(np.asarray(L0_ffn_b2, np.float32)),
    }
    base.update(host_constants())
    nc = _get_program(L=L)
    in_maps = [dict(base, x=x[i]) for i in range(B)]
    res = run_bass_kernel_spmd(nc, in_maps, core_ids=list(range(B)))
    return np.stack([res.results[i]["out"] for i in range(B)], axis=0)


# version 04
# speedup vs baseline: 1.0442x; 1.0442x over previous
import numpy as np
import ml_dtypes
import concourse.bass as bass
import concourse.mybir as mybir
from concourse.bass_utils import run_bass_kernel_spmd
from concourse.bass_isa_rust import ReduceOp

F32 = mybir.dt.float32
BF16 = mybir.dt.bfloat16
AX = mybir.AxisListType
OP = mybir.AluOpType
ACT = mybir.ActivationFunctionType

D = 1024
NCORE = 8
CH = 64
NP = 8
FF = 4096
SEG = 256
SEGB = 256
BASE_LR = 0.001
BASE_WD = 0.9


class Plan:

    NROT = 8
    ENGS = ("t", "v", "s", "g", "y")
    SEMS = tuple(["t", "v", "s", "g"]
                 + [f"dy{i}" for i in range(8)] + [f"dg{i}" for i in range(8)])

    def __init__(self):
        self.ops = {e: [] for e in self.ENGS}
        self.cnt = {sm: 0 for sm in self.SEMS}
        self.dma_i = {"y": 0, "g": 0}
        self._waited = {e: {sm: 0 for sm in self.SEMS} for e in self.ENGS}

    def op(self, eng, fn, waits=(), inc=True, dma=False):
        w = list(waits)
        if dma:
            i = self.dma_i[eng]
            self.dma_i[eng] += 1
            sem_name = f"d{eng}{i % self.NROT}"
            amt = 16
            if self.cnt[sem_name] > 0:
                w.append((sem_name, self.cnt[sem_name]))
        else:
            sem_name = eng
            amt = 1
        tok = None
        if inc:
            self.cnt[sem_name] += amt
            tok = self.cnt[sem_name]
        ww = []
        for (sm, c) in w:
            if c is None or c <= 0:
                continue
            if self._waited[eng][sm] >= c:
                continue
            self._waited[eng][sm] = c
            ww.append((sm, c))
        self.ops[eng].append((fn, tuple(ww), sem_name if inc else None, amt))
        return tok

    def barrier_tokens(self):
        return dict(self.cnt)


def emit_engine(plan, eng, engobj, sems):
    for fn, waits, inc_sem, amt in plan.ops[eng]:
        for (sm, c) in waits:
            engobj.wait_ge(sems[sm], c)
        inst = fn(engobj)
        if inc_sem is not None:
            assert inst is not None, f"op on {eng} must return instruction to inc"
            inst.then_inc(sems[inc_sem], amt)


def host_constants():
    c = {}
    c["c_ident"] = np.eye(128, dtype=ml_dtypes.bfloat16)
    c["c_identf"] = np.eye(128, dtype=np.float32)
    sel2 = np.zeros((2, 128), dtype=ml_dtypes.bfloat16)
    sel2[0, 0:64] = 1.0
    sel2[1, 64:128] = 1.0
    c["c_sel2"] = sel2
    mask2 = np.zeros((128, 2), dtype=ml_dtypes.bfloat16)
    mask2[0:64, 0] = 1.0
    mask2[64:128, 1] = 1.0
    c["c_mask2"] = mask2
    lsc = np.zeros((64, 1), dtype=np.float32)
    lsc[0:32, 0] = BASE_LR / CH
    lsc[32:64, 0] = BASE_WD / CH
    c["c_lrwdscale"] = lsc
    c["c_two"] = np.full((1, 128), 2.0, dtype=ml_dtypes.bfloat16)
    c["c_pool"] = np.tile(np.array([[1e-5, -0.5, 0.5, 1.0 / (FF * D), 1e-8,
                                     -16.0]], dtype=np.float32), (128, 1))
    selbig = np.zeros((64, 32, 128), dtype=ml_dtypes.bfloat16)
    for p in range(NP):
        for q in range(4):
            idx = q * 8 + p
            selbig[q * 16 + 2 * p, idx, 0:64] = 1.0
            selbig[q * 16 + 2 * p + 1, idx, 64:128] = 1.0
    c["c_selbig"] = selbig.reshape(64, 32 * 128)
    return c


def build_program(L=4096, dbg=(), skip_passB=False):
    NSEG = L // SEG
    NCH = SEG // CH
    nc = bass.Bass("TRN2", target_bir_lowering=False)

    dr = {}
    dr["x"] = nc.dram_tensor("x", [L, D], F32, kind="ExternalInput").ap()
    for nm, shp in [("wq", [D, D]), ("wk", [D, D]), ("wv", [D, D]),
                    ("wo", [D, D]), ("wlr", [32, D]), ("wbeta", [32, D]),
                    ("winit", [64, 16 * 64]), ("woutit", [64, 16 * 64]),
                    ("w1", [FF, D]), ("b1", [FF]), ("w2", [D, FF]), ("b2", [D])]:
        dr[nm] = nc.dram_tensor(nm, shp, F32, kind="ExternalInput").ap()
    consts = host_constants()
    for nm, arr in consts.items():
        dt = BF16 if arr.dtype == ml_dtypes.bfloat16 else F32
        dr[nm] = nc.dram_tensor(nm, list(arr.shape), dt, kind="ExternalInput").ap()
    dr["out"] = nc.dram_tensor("out", [L, D], F32, kind="ExternalOutput").ap()
    dr["s_sp"] = nc.dram_tensor("s_sp", [L, D], BF16, kind="Internal").ap()
    dr["xn_sp"] = nc.dram_tensor("xn_sp", [8, 128, L], BF16, kind="Internal").ap()
    dbg_dr = {}
    for nm, shp, dt in dbg:
        dbg_dr[nm] = nc.dram_tensor(nm, shp, dt, kind="ExternalOutput").ap()

    plan = Plan()

    import contextlib
    with contextlib.ExitStack() as st:
        def T(*args, **kw):
            return st.enter_context(nc.sbuf_tensor(*args, **kw))

        def PS(*args, **kw):
            return st.enter_context(nc.psum_tensor(*args, **kw))

        ident = T("ident", [128, 128], BF16)
        identf = T("identf", [128, 128], F32)
        sel2 = T("sel2", [2, 128], BF16)
        mask2 = T("mask2", [128, 2], BF16)
        lrwdscale = T("lrwdscale", [64, 1], F32)
        two_row = T("two_row", [1, 128], BF16)
        cpool = T("cpool", [128, 8], F32)
        nc.const_aps.register(F32, 1e-5, cpool[:, 0:1])
        nc.const_aps.register(F32, -0.5, cpool[:, 1:2])
        nc.const_aps.register(F32, 0.5, cpool[:, 2:3])
        nc.const_aps.register(F32, 1.0 / (FF * D), cpool[:, 3:4])
        nc.const_aps.register(F32, 1e-8, cpool[:, 4:5])
        nc.const_aps.register(F32, -16.0, cpool[:, 5:6])
        xt = T("xt", [128, 4, D], F32)
        outt = T("outt", [128, 2, D], F32)
        abf = T("abf", [128, 76 * 1024], BF16)
        af32 = T("af32", [128, 6 * 1024], F32)
        pb = [PS("pb0", [128, 512], F32), PS("pb1", [128, 512], F32),
              PS("pb2", [128, 512], F32), PS("pb3", [128, 512], F32),
              PS("pb4", [128, 512], F32), PS("pb5", [128, 1024], BF16),
              PS("pb6", [128, 1024], BF16), PS("pb7", [128, 512], F32)]
        sems = {nm: st.enter_context(nc.semaphore("sem_" + nm))
                for nm in Plan.SEMS}
        block = st.enter_context(nc.Block())
        tb = dict(ident=ident, identf=identf, sel2=sel2, mask2=mask2,
                  lrwdscale=lrwdscale, two_row=two_row, cpool=cpool,
                  xt=xt, outt=outt, abf=abf, af32=af32, pb=pb)

        emit_all(nc, plan, dr, dbg_dr, tb, L, NSEG, NCH, skip_passB)

        @block.tensor
        def _(t):
            emit_engine(plan, "t", t, sems)

        @block.vector
        def _(v):
            emit_engine(plan, "v", v, sems)

        @block.scalar
        def _(s):
            emit_engine(plan, "s", s, sems)

        @block.gpsimd
        def _(g):
            emit_engine(plan, "g", g, sems)

        @block.sync
        def _(y):
            emit_engine(plan, "y", y, sems)

    return nc


class Arena:

    def __init__(self, tensor, n_elems):
        self.t = tensor
        self.n = n_elems
        self.off = 0
        self.marks = {}
        self.slots = {}

    def alloc(self, name, shape):
        n = int(np.prod(shape[1:]))
        assert self.off + n <= self.n, f"arena overflow at {name}: {self.off}+{n}>{self.n}"
        base = self.off
        self.off += n
        self.slots[name] = (base, shape)
        return self.view(name)

    def view(self, name):
        base, shape = self.slots[name]
        ap = self.t[:shape[0], base:base + int(np.prod(shape[1:]))]
        if len(shape) > 2:
            pat = "p (" + " ".join(f"d{i}" for i in range(1, len(shape))) + ") -> p " + \
                  " ".join(f"d{i}" for i in range(1, len(shape)))
            kw = {f"d{i}": shape[i] for i in range(2, len(shape))}
            ap = ap.rearrange(pat, **kw)
        return ap

    def mark(self, tag):
        self.marks[tag] = self.off

    def reset_to(self, tag):
        self.off = self.marks[tag]



Z0 = ("v", 0)


DEBUG_ARENAS = {}


def emit_all(nc, plan, dr, dbg_dr, tb, L, NSEG, NCH, skip_passB=False):
    def P(eng, fn, waits=(), dma=False):
        if dma:
            sem_name = f"d{eng}{plan.dma_i[eng] % plan.NROT}"
            tok = plan.op(eng, fn, waits=waits, inc=True, dma=True)
            return (sem_name, tok)
        tok = plan.op(eng, fn, waits=waits, inc=True, dma=False)
        return (eng, tok)

    pb = tb["pb"]
    A = Arena(tb["abf"], 76 * 1024)
    A4 = Arena(tb["af32"], 6 * 1024)
    DEBUG_ARENAS["A"] = A
    DEBUG_ARENAS["A4"] = A4
    ident = tb["ident"]
    identf = tb["identf"]

    b2bf = A.alloc("b2bf", [1, 1024])
    A.mark("postperm")
    selbig = A.alloc("selbig", [64, 32, 128])
    wqT = A.alloc("wqT", [128, 8, 1024])
    wkT = A.alloc("wkT", [128, 8, 1024])
    wvT = A.alloc("wvT", [128, 8, 1024])
    woT = A.alloc("woT", [128, 8, 1024])
    wlrbT = A.alloc("wlrbT", [128, 8, 64])
    winT = A.alloc("winT", [128, 8, 64])
    wout = A.alloc("wout", [128, 8, 64])
    woutT = A.alloc("woutT", [128, 8, 64])
    A.mark("postAW")
    wnat = A.alloc("wnat", [128, 8, 1024])
    winit_nat = A.alloc("winit_nat", [64, 1024])
    woutit_nat = A.alloc("woutit_nat", [64, 1024])
    wlrb_nat = A.alloc("wlrb_nat", [64, 1024])
    A.reset_to("postAW")

    b1nat = A4.alloc("b1nat", [32, 128])
    b1cols = A4.alloc("b1cols", [128, 32])
    lrwd_f = A4.alloc("lrwd_f", [64, 2, 256])
    lrwdc = A4.alloc("lrwdc", [64, 2, 4])
    lrwdB = A4.alloc("lrwdB", [128, 2, 4, 8, 4])
    SUMS = A4.alloc("SUMS", [128, 16])
    RECIP = A4.alloc("RECIP", [128, 16])
    CLI = A4.alloc("CLI", [128, 8])
    CLO = A4.alloc("CLO", [128, 8])
    nmq = A4.alloc("nmq", [128, 2])
    bnst = A4.alloc("bnst", [128, 12])
    mv = A4.alloc("mv", [128, 2])
    rstd = A4.alloc("rstd", [128, 1])
    lntmp = A4.alloc("lntmp", [128, 1])
    xt = tb["xt"]
    outt = tb["outt"]

    pb5w = pb[5].ap().rearrange("p (a b) -> p a b", b=128)
    pb5v = pb[5].ap().rearrange("p (a b) -> p a b", b=64)
    pb6w = pb[6].ap().rearrange("p (a b) -> p a b", b=128)
    pb6v = pb[6].ap().rearrange("p (a b) -> p a b", b=64)
    pb7v = pb[7].ap().rearrange("p (a b) -> p a b", b=64)

    dstmap = {"c_ident": ident, "c_identf": identf, "c_sel2": tb["sel2"],
              "c_mask2": tb["mask2"], "c_lrwdscale": tb["lrwdscale"],
              "c_two": tb["two_row"]}
    cy_toks = []
    for nm, dst in dstmap.items():
        cy_toks.append(P("y", lambda e, dst=dst, nm=nm: e.dma_start(dst[:], dr[nm]),
                         dma=True))
    cy_toks.append(P("y", lambda e: e.dma_start(tb["cpool"][:, 0:6], dr["c_pool"]),
                    dma=True))
    cy_toks.append(P("y", lambda e: e.dma_start(selbig.rearrange("p a b -> p (a b)"),
                                                dr["c_selbig"]), dma=True))
    cy_toks.append(P("y", lambda e: e.dma_start(b1nat[:],
                                                dr["b1"].rearrange("(a b) -> a b", b=128)),
                    dma=True))
    cg_toks2 = [
        P("g", lambda e: e.dma_start(winit_nat[:], dr["winit"]), dma=True),
        P("g", lambda e: e.dma_start(woutit_nat[:], dr["woutit"]), dma=True),
        P("g", lambda e: e.dma_start(wlrb_nat[0:32, :], dr["wlr"]), dma=True),
        P("g", lambda e: e.dma_start(wlrb_nat[32:64, :], dr["wbeta"]), dma=True),
    ]
    cg_toks = [P("g", lambda e: e.dma_start(b2bf[:],
                                            dr["b2"].rearrange("(a b) -> a b", a=1)),
                dma=True)]

    t_tok = P("t", lambda e: e.transpose(pb[7][0:128, 0:32], b1nat[:],
                                         identf[0:32, 0:32]),
              waits=cy_toks)
    tok_b1 = P("v", lambda e: e.tensor_copy(b1cols[:], pb[7][0:128, 0:32]),
               waits=[t_tok])

    for h in range(16):
        p_, hh = h // 2, h % 2
        sl = slice(h * 64, (h + 1) * 64)
        hsl = slice(hh * 64, (hh + 1) * 64)
        t_tok = P("t", (lambda e, sl=sl, p_=p_, hh=hh, hsl=hsl:
                        e.transpose(pb5v[hsl, p_, :], winit_nat[:, sl],
                                    ident[0:64, 0:64], tile_position=(0, 64 * hh))),
                  waits=cg_toks + cg_toks2 + cy_toks + [tok_b1])
        t_tok = P("t", (lambda e, sl=sl, p_=p_, hh=hh, hsl=hsl:
                        e.transpose(pb5v[hsl, 8 + p_, :], woutit_nat[:, sl],
                                    ident[0:64, 0:64], tile_position=(0, 64 * hh))))
        t_tok = P("t", (lambda e, sl=sl, p_=p_, hh=hh, hsl=hsl:
                        e.matmul(pb7v[hsl, p_, :], ident[0:64, 0:64],
                                 woutit_nat[:, sl], start=True, stop=True,
                                 skip_group_check=True, tile_position=(0, 64 * hh))))
    P("v", lambda e: e.tensor_copy(winT[:], pb5v[:, 0:8, :]), waits=[t_tok])
    P("v", lambda e: e.tensor_copy(woutT[:], pb5v[:, 8:16, :]))
    tok_state0 = P("v", lambda e: e.tensor_copy(wout[:], pb7v[:, 0:8, :]))

    for fc in range(8):
        t_tok = P("t", (lambda e, fc=fc:
                        e.transpose(pb6v[0:128, 8 + fc, :],
                                    wlrb_nat[:, fc * 128:(fc + 1) * 128],
                                    ident[0:64, 0:64])),
                  waits=[tok_state0])
    tok_evac = P("v", lambda e: e.tensor_copy(wlrbT[:], pb6v[:, 8:16, :]),
                 waits=[t_tok])

    tok_wload = Z0
    for wname, wT in (("wq", wqT), ("wk", wkT), ("wv", wvT), ("wo", woT)):
        wnat_free = ("t", plan.cnt["t"])
        tok_wload = P("g", (lambda e, wname=wname:
                            e.dma_start(wnat[:],
                                        dr[wname].rearrange("(c p) f -> p c f", p=128))),
                      waits=[wnat_free, tok_wload], dma=True)
        for fc in range(8):
            for c in range(8):
                t_tok = P("t", (lambda e, fc=fc, c=c:
                                e.transpose(pb5w[:, c, :],
                                            wnat[:, c, fc * 128:(fc + 1) * 128],
                                            ident[:])),
                          waits=[tok_wload, tok_evac])
            tok_evac = P("v", (lambda e, wT=wT, fc=fc:
                               e.tensor_copy(wT[:, fc, :],
                                             pb5w.rearrange("p a b -> p (a b)"))),
                         waits=[t_tok])

    qkv = A.alloc("qkv", [128, 2, 8, NCH, 192])
    xn_seg = A.alloc("xn_seg", [128, 2, 8, SEG])
    o_seg = A.alloc("o_seg", [128, 2, 8, SEG])
    s_seg = A.alloc("s_seg", [128, 2, 2, 1024])
    E_sb = A.alloc("E_sb", [128, 8, 128])
    p_q = A.alloc("p_q", [128, 8, 64])
    p_li = A.alloc("p_li", [128, 8, 64])
    m_sb = A.alloc("m_sb", [128, 8, 64])
    a_sb = A.alloc("a_sb", [128, 8, 64])
    t_sb = A.alloc("t_sb", [128, 8, 64])
    gzp = A.alloc("gzp", [128, 8, 128])
    s2_sb = A.alloc("s2_sb", [2, 8, 64])
    rk2 = A.alloc("rk2", [128, 8, 2])
    kvT_sb = A.alloc("kvT_sb", [128, 8, 128])
    gzpT_sb = A.alloc("gzpT_sb", [128, 8, 128])
    xn_tok = A.alloc("xn_tok", [128, 2, 1024])
    lrwdc_bf = A.alloc("lrwdc_bf", [64, 2, 4])
    A.mark("passA_end")

    pbv = [pb[i].ap().rearrange("p (a b) -> p a b", b=64) for i in range(5)]
    views = dict(qkv=qkv, xn_seg=xn_seg, o_seg=o_seg, s_seg=s_seg, E_sb=E_sb,
                 p_q=p_q, p_li=p_li, m_sb=m_sb, a_sb=a_sb, t_sb=t_sb,
                 gzp=gzp, s2_sb=s2_sb, rk2=rk2, kvT_sb=kvT_sb,
                 gzpT_sb=gzpT_sb, winT=winT, wout=wout, woutT=woutT,
                 SUMS=SUMS, RECIP=RECIP, CLI=CLI, CLO=CLO, nmq=nmq,
                 lrwdB=lrwdB, pbv=pbv, pb5w=pb5w, pb6w=pb6w,
                 wqT=wqT, wkT=wkT, wvT=wvT, woT=woT, wlrbT=wlrbT,
                 lrwd_f=lrwd_f, lrwdc=lrwdc, bnst=bnst, mv=mv, rstd=rstd,
                 lntmp=lntmp, xn_tok=xn_tok, selbig=selbig,
                 lrwdc_bf=lrwdc_bf)

    wstream = A4.alloc("wstream", [128, 4, 1024])
    acc1 = A4.alloc("accw1", [128, 1])
    acc2 = A4.alloc("accw2", [128, 1])
    tabs = A4.alloc("tabsw", [128, 1])
    t1bufs = dict(wstream=wstream, acc1=acc1, acc2=acc2, tabs=tabs)

    ck = {
        "t1_i": 0, "t1_pf": 0, "t1_ws": [Z0, Z0, Z0, Z0],
        "t1_dma": [Z0, Z0, Z0, Z0], "t1_tabs": Z0, "xdma": [Z0, Z0, Z0, Z0],
        "upd_win": tok_state0, "upd_wout": tok_state0, "wt_evac": tok_state0,
        "b0": Z0, "b1": Z0, "b2": Z0, "b3": tok_b1, "b4": Z0,
        "b5": tok_evac, "b6": tok_evac, "b7": Z0,
        "eq": Z0, "ek": Z0, "pq": Z0, "pli": Z0, "plo": Z0, "m": Z0, "a": Z0,
        "gz": Z0, "rk2": Z0, "s2": Z0, "kvt": Z0, "gzpt": Z0,
        "nm": Z0, "oseg": Z0, "xt": [Z0, Z0, Z0, Z0], "xntok": [Z0, Z0],
        "mv": Z0, "ln_ev": Z0, "proj_last": ("s", 0), "lrwd_lc": Z0,
        "lrwdc_r": Z0, "o_evac": ("s", 0), "qkv_ready": ("s", 0),
        "lrwdB_ready": Z0,
        "lrwdB_ready_p": [Z0, Z0], "lrwdB_r_p": [Z0, Z0],
        "oseg_p": [Z0, Z0], "xnseg_p": [Z0, Z0], "qkv_w": [Z0, Z0],
        "sseg_p": [Z0, Z0], "lrwdf_p": [Z0, Z0], "xnsp_dma_p": [Z0, Z0],
    }

    views.update(t1bufs)
    for stage in range(NSEG + 1):
        emit_stage(plan, P, dr, tb, views, ck, stage, NSEG, NCH)

    barA = plan.barrier_tokens()
    bar_toks = [(sm, c) for sm, c in barA.items() if c > 0]
    if not skip_passB:
        A.reset_to("postperm")
        emit_passB(nc, plan, P, dr, dbg_dr, tb, A, A4, bar_toks, L,
                   b2bf, b1cols, ident, identf, ck, t1bufs)

    fin = [(sm, plan.cnt[sm]) for sm in Plan.SEMS
           if sm.startswith("d") and plan.cnt[sm] > 0]
    plan.op("y", lambda e: None, waits=fin, inc=False)


def emit_stage(plan, P, dr, tb, V, ck, stage, NSEG, NCH):
    pb = tb["pb"]
    pb5w = V["pb5w"]
    ident = tb["ident"]
    xt = tb["xt"]

    fillers = []
    if stage < NSEG:
        seg = stage
        par = seg % 2
        qkv = V["qkv"][:, par]
        xn_seg = V["xn_seg"][:, par]

        for tsub in range(2):
            gidx = seg * 2 + tsub
            slot = gidx % 4
            ck["xdma"][slot] = P("y", (lambda e, gidx=gidx, slot=slot:
                                       e.dma_start(xt[:, slot, :],
                                                   dr["x"][gidx * 128:(gidx + 1) * 128, :])),
                                 waits=[ck["xt"][slot], ck["xdma"][slot]],
                                 dma=True)
        if stage == 0:
            emit_t1_prefetch(plan, P, dr, V, ck, 4)

        def ln_sub(tsub, seg=seg, par=par, xn_seg=xn_seg):
            gidx = seg * 2 + tsub
            xpar = gidx % 4
            tok_x = ck["xdma"][xpar]
            tok_bn = P("v", (lambda e, xpar=xpar:
                             e.bn_stats(V["bnst"][:, 0:6], xt[:, xpar, 0:512])),
                       waits=[tok_x, ck["mv"]])
            tok_bn = P("v", (lambda e, xpar=xpar:
                             e.bn_stats(V["bnst"][:, 6:12], xt[:, xpar, 512:1024])))
            tok_mv = P("v", lambda e: e.bn_aggr(V["mv"][:], V["bnst"][:, 0:12]),
                       waits=[tok_bn])
            tok_ln = P("s", (lambda e: e.activation(V["lntmp"][:], V["mv"][:, 1:2],
                                                    ACT.Ln, bias=1e-5)),
                      waits=[tok_mv])
            tok_rstd = P("s", (lambda e: e.activation(V["rstd"][:], V["lntmp"][:],
                                                      ACT.Exp, scale=-0.5)),
                         waits=[tok_ln])
            ck["mv"] = tok_rstd
            npar = gidx % 2
            tok_xntok = P("v", (lambda e, xpar=xpar, npar=npar:
                                e.tensor_scalar(V["xn_tok"][:, npar, :],
                                                xt[:, xpar, :],
                                                V["mv"][:, 0:1], V["rstd"][:],
                                                op0=OP.subtract, op1=OP.mult)),
                          waits=[tok_rstd, tok_mv, ck["xntok"][npar]])
            ck["xt"][xpar] = tok_xntok
            for fc in range(8):
                t_tok = P("t", (lambda e, npar=npar, fc=fc:
                                e.transpose(pb5w[:, fc, :],
                                            V["xn_tok"][:, npar, fc * 128:(fc + 1) * 128],
                                            ident[:])),
                          waits=[tok_xntok, ck["b5"]])
            ck["xntok"][npar] = t_tok
            tok_ev = P("v", (lambda e, tsub=tsub, xn_seg=xn_seg:
                             e.tensor_copy(xn_seg[:, :, tsub * 128:(tsub + 1) * 128],
                                           pb5w[:, :, :])),
                       waits=[t_tok, ck["xnseg_p"][par], ck["xnsp_dma_p"][par]])
            ck["b5"] = tok_ev
            ck["ln_ev"] = tok_ev

        fillers.append(lambda: ln_sub(0))
        fillers.append(lambda: ln_sub(1))

        def proj_tile(wTname, m, coff, actf, seg=seg, par=par, qkv=qkv,
                      xn_seg=xn_seg):
            wT = V[wTname]
            for fc in range(8):
                t_tok = P("t", (lambda e, wT=wT, m=m, fc=fc:
                                e.matmul(pb[3][:, 0:SEG],
                                         wT[:, fc, m * 128:(m + 1) * 128],
                                         xn_seg[:, fc, :],
                                         start=(fc == 0), stop=(fc == 7),
                                         skip_group_check=True)),
                          waits=[ck["ln_ev"], ck["b3"]])
            bview = pb[3].ap().rearrange("p (a b) -> p a b", b=64)
            tok_evac = P("s", (lambda e, bview=bview, m=m, coff=coff, actf=actf,
                               qkv=qkv:
                               e.activation(qkv[:, m, :, coff:coff + 64],
                                            bview[:, 0:NCH, :], actf)),
                         waits=[t_tok, ck["qkv_w"][par]])
            ck["b3"] = tok_evac
            ck["proj_last"] = tok_evac

        for wTname, coff, actf in (("wqT", 0, ACT.Silu), ("wkT", 64, ACT.Silu),
                                   ("wvT", 128, ACT.Copy)):
            for m in range(8):
                fillers.append(lambda wTname=wTname, m=m, coff=coff, actf=actf:
                               proj_tile(wTname, m, coff, actf))

        def lrwd_proj(seg=seg, par=par, xn_seg=xn_seg):
            for fc in range(8):
                t_tok = P("t", (lambda e, fc=fc:
                                e.matmul(pb[3][0:64, 0:SEG], V["wlrbT"][:, fc, :],
                                         xn_seg[:, fc, :], start=(fc == 0),
                                         stop=(fc == 7), skip_group_check=True)),
                          waits=[ck["ln_ev"], ck["b3"]])
            tok_sg = P("s", (lambda e, par=par:
                             e.activation(V["lrwd_f"][:, par, :], pb[3][0:64, 0:SEG],
                                          ACT.Sigmoid)),
                       waits=[t_tok, ck["lrwdf_p"][par]])
            ck["b3"] = tok_sg
            tok_rd = P("v", (lambda e, par=par:
                             e.reduce_sum(V["lrwdc"][:, par, :],
                                          V["lrwd_f"][:, par]
                                          .rearrange("p (a b) -> p a b", b=64),
                                          axis=AX.X)),
                      waits=[tok_sg, ck["lrwdc_r"]])
            ck["lrwdf_p"][par] = tok_rd
            tok_lc = P("v", (lambda e, par=par:
                             e.tensor_scalar(V["lrwdc_bf"][:, par, :],
                                             V["lrwdc"][:, par, :],
                                             tb["lrwdscale"][:], None,
                                             op0=OP.mult)), waits=[tok_rd])
            ck["lrwd_lc"] = tok_lc
            ck["xnseg_p"][par] = ("t", plan.cnt["t"])

        def lrwd_bcast(seg=seg, par=par):
            pb3b = pb[3].ap().rearrange("p (a b) -> p a b", b=4)
            for idx in range(32):
                t_tok = P("t", (lambda e, idx=idx, par=par:
                                e.matmul(pb3b[:, idx, :],
                                         V["selbig"][:, idx, :],
                                         V["lrwdc_bf"][:, par, :], start=True,
                                         stop=True, skip_group_check=True)),
                          waits=[ck["lrwd_lc"], ck["b3"]])
            tok_lB = P("v", (lambda e, par=par:
                             e.tensor_copy(V["lrwdB"][:, par]
                                           .rearrange("p a b c -> p (a b c)"),
                                           pb[3][:, 0:128])),
                       waits=[t_tok, ck["lrwdB_r_p"][par]])
            ck["b3"] = tok_lB
            ck["lrwdB_ready_p"][par] = tok_lB
            ck["lrwdc_r"] = ck["lrwd_lc"]

        fillers.append(lrwd_proj)
        fillers.append(lrwd_bcast)

        def xn_spill(seg=seg, par=par, xn_seg=xn_seg):
            tok = P("g", (lambda e, seg=seg, xn_seg=xn_seg:
                          e.dma_start(dr["xn_sp"][:, :, seg * SEG:(seg + 1) * SEG]
                                      .rearrange("a p t -> p a t"),
                                      xn_seg[:])),
                    waits=[ck["ln_ev"]], dma=True)
            ck["xnsp_dma_p"][par] = tok

        fillers.append(xn_spill)
        n_t1 = (64 + NSEG - 1) // NSEG
        for _ in range(n_t1):
            fillers.append(lambda: emit_t1_compute(plan, P, dr, V, ck, 1))

    if stage == 0:
        for f in fillers:
            f()
        ck["qkv_ready"] = ck["proj_last"]
        return

    seg = stage - 1
    par = seg % 2
    Vp = dict(V)
    Vp["qkv"] = V["qkv"][:, par]
    Vp["o_seg"] = V["o_seg"][:, par]
    Vp["lrwdB"] = V["lrwdB"][:, par]
    ck["lrwdB_ready"] = ck["lrwdB_ready_p"][par]
    ck["oseg"] = ck["oseg_p"][par]
    nf = len(fillers)
    per = max(1, (nf + NCH - 1) // NCH) if nf else 0
    fi = 0
    for ci in range(NCH):
        emit_scan_chunk(plan, P, tb, Vp, ck, ci)
        for _ in range(per):
            if fi < nf:
                fillers[fi]()
                fi += 1
    while fi < nf:
        fillers[fi]()
        fi += 1
    if stage < NSEG:
        ck["qkv_ready"] = ck["proj_last"]
    ck["qkv_w"][par] = ("t", plan.cnt["t"])
    ck["lrwdB_r_p"][par] = ck["upd_win"]

    o_seg = V["o_seg"][:, par]
    s_seg = V["s_seg"][:, par]
    tok_sev = ("s", 0)
    for tsub in range(2):
        for half in range(2):
            for fc in range(8):
                t_tok = P("t", (lambda e, tsub=tsub, half=half, fc=fc,
                                o_seg=o_seg:
                                e.matmul(pb[3][:, :],
                                         o_seg[:, fc, tsub * 128:(tsub + 1) * 128],
                                         V["woT"][:, fc, half * 512:(half + 1) * 512],
                                         start=(fc == 0), stop=(fc == 7),
                                         skip_group_check=True)),
                          waits=[ck["o_evac"], ck["b3"]])
            tok_sev = P("s", (lambda e, tsub=tsub, half=half, s_seg=s_seg:
                              e.activation(s_seg[:, tsub, half * 512:(half + 1) * 512],
                                           pb[3][:, :], ACT.Copy)),
                        waits=[t_tok, ck["sseg_p"][par]])
            ck["b3"] = tok_sev
    ck["oseg_p"][par] = ("t", plan.cnt["t"])
    tok_sdma = P("g", (lambda e, seg=seg, s_seg=s_seg:
                       e.dma_start(dr["s_sp"][seg * SEG:(seg + 1) * SEG, :]
                                   .rearrange("(a p) f -> p a f", p=128),
                                   s_seg[:])),
                 waits=[tok_sev], dma=True)
    ck["sseg_p"][par] = tok_sdma


def emit_scan_chunk(plan, P, tb, V, ck, ci):
    pb = tb["pb"]
    pbv = V["pbv"]
    pb5w, pb6w = V["pb5w"], V["pb6w"]
    ident = tb["ident"]
    qkv, winT, wout, woutT = V["qkv"], V["winT"], V["wout"], V["woutT"]
    E_sb, p_q, p_li = V["E_sb"], V["p_q"], V["p_li"]
    m_sb, a_sb, t_sb, gzp = V["m_sb"], V["a_sb"], V["t_sb"], V["gzp"]
    s2_sb, rk2 = V["s2_sb"], V["rk2"]
    kvT_sb, gzpT_sb = V["kvT_sb"], V["gzpT_sb"]
    SUMS, RECIP, CLI, CLO = V["SUMS"], V["RECIP"], V["CLI"], V["CLO"]
    nmq, lrwdB = V["nmq"], V["lrwdB"]
    pb5WT = pb[5].ap().rearrange("p (a b) -> p a b", b=64)
    pbz = [pb[0].ap().rearrange("p (a b) -> p a b", b=128),
           pb[1].ap().rearrange("p (a b) -> p a b", b=128)]

    def heads():
        for p in range(NP):
            for hh in range(2):
                yield p, hh, slice(hh * 64, (hh + 1) * 64), (0 if hh == 0 else 64)

    for p, hh, hsl, off in heads():
        t_tok = P("t", (lambda e, p=p, hh=hh, hsl=hsl, off=off:
                        e.matmul(pbz[p // 4][hsl, p % 4, :], winT[hsl, p, :],
                                 qkv[hsl, p, ci, 0:128], start=True, stop=True,
                                 skip_group_check=True, tile_position=(off, off))),
                  waits=[ck["upd_win"], ck["qkv_ready"],
                         ck["b0"] if p < 4 else ck["b1"]])
    tok_z = t_tok

    for p in range(NP):
        t_tok = P("t", (lambda e, p=p:
                        e.transpose(pb5w[:, p, :], qkv[:, p, ci, 64:192],
                                    ident[:])),
                  waits=[ck["b5"]])
    tok_trkv = t_tok
    tok_kvt = P("v", lambda e: e.tensor_copy(kvT_sb[:], pb5w[:, 0:8, :]),
                waits=[tok_trkv, ck["kvt"]])
    ck["b5"] = tok_kvt

    for p, hh, hsl, off in heads():
        t_tok = P("t", (lambda e, p=p, hh=hh, hsl=hsl, off=off:
                        e.matmul(pbv[2][hsl, p, :], woutT[hsl, p, :],
                                 qkv[hsl, p, ci, 128:192], start=True, stop=True,
                                 skip_group_check=True, tile_position=(off, off))),
                  waits=[ck["wt_evac"], ck["b2"]])
    tok_amm = t_tok

    tok_eqa = P("s", (lambda e: e.activation(E_sb[:, 0:4, :], pb[0][:, :],
                                             ACT.Exp, bias=-16.0)),
                waits=[tok_z, ck["eq"]])
    tok_eqb = P("s", (lambda e: e.activation(E_sb[:, 4:8, :], pb[1][:, :],
                                             ACT.Exp, bias=-16.0)))
    ck["b0"] = tok_eqa
    ck["b1"] = tok_eqb
    tok_sums = P("v", (lambda e:
                       e.reduce_sum(SUMS[:], E_sb.rearrange("p a (b c) -> p (a b) c", c=64),
                                    axis=AX.X)),
                 waits=[tok_eqb])
    tok_rec = P("v", lambda e: e.reciprocal(RECIP[:], SUMS[:]),
                waits=[tok_sums])
    rk_cols = RECIP.rearrange("p (a b) -> p a b", b=2)[:, :, 1]
    tok_cli = P("v", (lambda e:
                      e.tensor_tensor(CLI[:], rk_cols,
                                      lrwdB[:, 0, :, ci], OP.mult)),
                waits=[ck["lrwdB_ready"], tok_rec])
    tok_clo = P("v", (lambda e:
                      e.tensor_tensor(CLO[:], rk_cols,
                                      lrwdB[:, 1, :, ci], OP.mult)))
    for p in range(NP):
        tok_pq = P("v", (lambda e, p=p:
                         e.tensor_scalar(p_q[:, p, :], E_sb[:, p, 0:64],
                                         RECIP[:, 2 * p:2 * p + 1], None,
                                         op0=OP.mult)),
                   waits=[ck["pq"], tok_rec])
        tok_pli = P("v", (lambda e, p=p:
                          e.tensor_scalar(p_li[:, p, :], E_sb[:, p, 64:128],
                                          CLI[:, p:p + 1], None, op0=OP.mult)),
                    waits=[ck["pli"], tok_clo])
        tok_plo = P("v", (lambda e, p=p:
                          e.tensor_scalar(gzp[:, p, 64:128], E_sb[:, p, 64:128],
                                          CLO[:, p:p + 1], None, op0=OP.mult)),
                    waits=[ck["plo"]])
        tok_rk2 = P("v", (lambda e, p=p:
                          e.tensor_scalar(rk2[:, p, :], tb["mask2"][:],
                                          RECIP[:, 2 * p + 1:2 * p + 2], None,
                                          op0=OP.mult)),
                    waits=[ck["rk2"]])
    tok_eqfree = tok_rk2

    for p, hh, hsl, off in heads():
        t_tok = P("t", (lambda e, p=p, hh=hh, hsl=hsl, off=off:
                        e.matmul(pbv[3][hsl, p, :], wout[hsl, p, :],
                                 p_q[hsl, p, :], start=True, stop=True,
                                 skip_group_check=True, tile_position=(off, off))),
                  waits=[ck["upd_wout"], ck["b3"], tok_pq])
    tok_omm = t_tok
    ck["pq"] = tok_omm
    tok_oev = P("s", (lambda e, ci=ci:
                      e.activation(V["o_seg"][:, :, ci * 64:(ci + 1) * 64],
                                   pbv[3][:, 0:8, :], ACT.Copy)),
                waits=[tok_omm, ck["oseg"]])
    ck["b3"] = tok_oev
    ck["o_evac"] = tok_oev

    tok_aev = P("s", lambda e: e.activation(a_sb[:], pbv[2][:, 0:8, :], ACT.Copy),
                waits=[tok_amm, ck["a"]])
    tok_m = P("v", (lambda e:
                    e.tensor_tensor(m_sb[:],
                                    E_sb[:, :, 64:128], pbv[2][:, 0:8, :],
                                    OP.mult)),
              waits=[tok_eqb, tok_aev, ck["m"]])
    ck["b2"] = tok_m
    ck["eq"] = tok_eqfree
    ck["ek"] = tok_m
    for p in range(NP):
        t_tok = P("t", (lambda e, p=p:
                        e.matmul(pbv[4][0:2, p, :], rk2[:, p, :], m_sb[:, p, :],
                                 start=True, stop=True, skip_group_check=True)),
                  waits=[tok_m, tok_rk2, ck["b4"]])
    tok_s2mm = t_tok
    ck["rk2"] = tok_s2mm
    ck["m"] = tok_s2mm
    tok_s2ev = P("s", lambda e: e.activation(s2_sb[:], pbv[4][0:2, 0:8, :],
                                             ACT.Copy),
                 waits=[tok_s2mm, ck["s2"]])
    tok_sb = P("t", (lambda e:
                     e.matmul(pb[4][:, :], tb["sel2"][:],
                              s2_sb.rearrange("p a b -> p (a b)"),
                              start=True, stop=True, skip_group_check=True)),
              waits=[tok_s2ev])
    ck["s2"] = tok_sb
    tok_t = P("v", lambda e: e.tensor_tensor(t_sb[:], pb[4].ap()
                                             .rearrange("p (a b) -> p a b", b=64),
                                             a_sb[:], OP.subtract),
              waits=[tok_sb, tok_aev])
    ck["b4"] = tok_t
    ck["a"] = tok_t
    tok_gz = P("v", (lambda e:
                     e.tensor_tensor(gzp[:, :, 0:64],
                                     t_sb[:], p_li[:], OP.mult)),
               waits=[tok_pli, tok_t, ck["gz"]])
    ck["pli"] = tok_gz

    for p in range(NP):
        t_tok = P("t", (lambda e, p=p:
                        e.transpose(pb6w[:, p, :], gzp[:, p, :], ident[:])),
                  waits=[tok_gz, tok_plo, ck["b6"]])
    tok_trgzp = t_tok
    ck["gz"] = tok_trgzp
    ck["plo"] = tok_trgzp
    tok_gzpt = P("s", lambda e: e.activation(gzpT_sb[:], pb6w[:, 0:8, :],
                                             ACT.Copy),
                 waits=[tok_trgzp, ck["gzpt"]])
    ck["b6"] = tok_gzpt

    for p, hh, hsl, off in heads():
        t_tok = P("t", (lambda e, p=p, hh=hh, hsl=hsl, off=off:
                        e.matmul(pbv[0][hsl, p, :],
                                 gzpT_sb[64:128, p, hh * 64:(hh + 1) * 64],
                                 kvT_sb[64:128, p, hh * 64:(hh + 1) * 64],
                                 start=True, stop=True, skip_group_check=True,
                                 tile_position=(64, off))),
                  waits=[tok_gzpt, tok_kvt, ck["b0"]])
    tok_g11 = t_tok
    for p, hh, hsl, off in heads():
        t_tok = P("t", (lambda e, p=p, hh=hh, hsl=hsl, off=off:
                        e.matmul(pbv[1][hsl, p, :],
                                 kvT_sb[0:64, p, hh * 64:(hh + 1) * 64],
                                 gzpT_sb[0:64, p, hh * 64:(hh + 1) * 64],
                                 start=True, stop=True, skip_group_check=True,
                                 tile_position=(0, off))),
                  waits=[ck["b1"]])
    tok_g13 = t_tok
    ck["kvt"] = tok_g13
    ck["gzpt"] = tok_g13

    for p in range(NP):
        tok_uo = P("v", (lambda e, p=p:
                         e.scalar_tensor_tensor(wout[:, p, :], wout[:, p, :],
                                                lrwdB[:, 3, p, ci:ci + 1],
                                                pbv[0][:, p, :], op0=OP.mult,
                                                op1=OP.add)),
                   waits=[tok_g11, tok_omm, ck["upd_wout"], ck["lrwdB_ready"]])
        tok_ui = P("v", (lambda e, p=p:
                         e.scalar_tensor_tensor(winT[:, p, :], winT[:, p, :],
                                                lrwdB[:, 2, p, ci:ci + 1],
                                                pbv[1][:, p, :], op0=OP.mult,
                                                op1=OP.subtract)),
                   waits=[tok_g13, tok_z, ck["upd_win"]])
    ck["upd_win"] = tok_ui
    ck["upd_wout"] = tok_uo
    ck["b0"] = tok_uo
    ck["b1"] = tok_ui

    for p, hh, hsl, off in heads():
        t_tok = P("t", (lambda e, p=p, hh=hh, hsl=hsl, off=off:
                        e.transpose(pb5WT[hsl, p, :], wout[hsl, p, :],
                                    ident[hsl, hsl], tile_position=(off, off))),
                  waits=[tok_uo, ck["b5"]])
    tok_trw = t_tok
    tok_wt = P("s", lambda e: e.activation(woutT[:], pb5WT[:, 0:8, :], ACT.Copy),
               waits=[tok_trw])
    ck["b5"] = tok_wt
    ck["wt_evac"] = tok_wt


def w2_tile_src_g(dr, c):
    rt, cc = c // 4, c % 4
    return dr["w2"][rt * 128:(rt + 1) * 128, cc * 1024:(cc + 1) * 1024]


def t1_src(dr, i):
    if i < 32:
        return dr["w1"][i * 128:(i + 1) * 128, :]
    return w2_tile_src_g(dr, i - 32)


def emit_t1_prefetch(plan, P, dr, V, ck, n):
    wstream = V["wstream"]
    for _ in range(n):
        i = ck["t1_pf"]
        if i >= 64:
            return
        ck["t1_pf"] += 1
        par = i % 4
        srcap = t1_src(dr, i)
        ck["t1_dma"][par] = P("y", (lambda e, srcap=srcap, par=par:
                                    e.dma_start(wstream[:, par, :], srcap)),
                              waits=[ck["t1_ws"][par], ck["t1_dma"][par]],
                              dma=True)


def emit_t1_compute(plan, P, dr, V, ck, n):
    wstream, tabs = V["wstream"], V["tabs"]
    for _ in range(n):
        i = ck["t1_i"]
        if i >= 64:
            return
        ck["t1_i"] += 1
        par = i % 4
        acc = V["acc1"] if i < 32 else V["acc2"]
        tok_tb = P("v", (lambda e, par=par:
                         e.reduce_sum(tabs[:], wstream[:, par, :], axis=AX.X,
                                      apply_absolute_value=True)),
                   waits=[ck["t1_dma"][par], ck["t1_tabs"]])
        if i % 32 == 0:
            tok_a = P("v", (lambda e, acc=acc: e.tensor_copy(acc[:], tabs[:])),
                      waits=[tok_tb])
        else:
            tok_a = P("v", (lambda e, acc=acc:
                            e.tensor_tensor(acc[:], acc[:], tabs[:], OP.add)),
                      waits=[tok_tb])
        ck["t1_ws"][par] = tok_a
        ck["t1_tabs"] = tok_a
        emit_t1_prefetch(plan, P, dr, V, ck, 1)
    return


def emit_passB(nc, plan, P, dr, dbg_dr, tb, A, A4, bar_toks, L,
               b2bf, b1cols, ident, identf, ck, t1bufs):
    pb = tb["pb"]
    xt = tb["xt"]
    outt = tb["outt"]
    pb5w = pb[5].ap().rearrange("p (a b) -> p a b", b=128)
    pb6w = pb[6].ap().rearrange("p (a b) -> p a b", b=128)
    NBT = L // SEGB
    wstream = t1bufs["wstream"]
    acc1, acc2, tabs = t1bufs["acc1"], t1bufs["acc2"], t1bufs["tabs"]

    for e in Plan.ENGS:
        plan.op(e, lambda _e: None, waits=bar_toks, inc=False)

    w1T = A.alloc("w1T", [128, 8, FF])
    w2T = A.alloc("w2T", [128, 32, 1024])
    A.mark("tern_tmp")
    w12scr = A.alloc("w12scr", [128, 1024])
    sgA = A.alloc("sgA", [128, 1024])
    A.reset_to("tern_tmp")
    hT = A.alloc("hT", [128, 2, SEGB])
    xnB = A.alloc("xnB", [128, 2, 8, SEGB])
    s_rd = A.alloc("s_rd", [128, 2, 1024])
    g_sb = A.alloc("g_sb", [128, 2, 1024])

    tot = A4.alloc("totw", [1, 1])
    inv1 = A4.alloc("invw1", [128, 1])
    inv2 = A4.alloc("invw2", [128, 1])
    ones_col = A4.alloc("ones_col", [128, 1])
    ones_row = A4.alloc("ones_row", [1, 128])


    tok_o1 = P("v", lambda e: e.memset(ones_col[:], 1.0))
    tok_o2 = P("v", lambda e: e.memset(ones_row[:], 1.0))
    tabs_free = ck["t1_tabs"]
    for acc, inv in ((acc1, inv1), (acc2, inv2)):
        t_tok = P("t", (lambda e, acc=acc:
                        e.matmul(pb[7][0:1, 0:1], ones_col[:], acc[:],
                                 start=True, stop=True, skip_group_check=True)),
                  waits=[ck["t1_tabs"], tok_o1])
        tok_cp = P("v", lambda e: e.tensor_copy(tot[0:1, :], pb[7][0:1, 0:1]),
                   waits=[t_tok])
        t_tok = P("t", (lambda e:
                        e.matmul(pb[7][0:128, 1:2], ones_row[:], tot[0:1, :],
                                 start=True, stop=True, skip_group_check=True)),
                  waits=[tok_cp, tok_o2])
        tok_t1 = P("v", (lambda e: e.tensor_scalar(tabs[:], pb[7][0:128, 1:2],
                                                   1.0 / (FF * D), None,
                                                   op0=OP.mult)),
                   waits=[t_tok, tabs_free])
        tok_t2 = P("v", (lambda e: e.tensor_scalar(tabs[:], tabs[:], 1e-8, None,
                                                   op0=OP.max)), waits=[tok_t1])
        tok_i = P("v", (lambda e, inv=inv: e.reciprocal(inv[:], tabs[:])),
                  waits=[tok_t2])
        tabs_free = tok_i

    ws_free = list(ck["t1_ws"])
    ws_dma = list(ck["t1_dma"])
    scr_free = Z0
    tr_banks = [pb5w, pb6w]
    bank_ev = [Z0, Z0]
    tok_trev = Z0
    for wname, wT, inv in (("w1", w1T, inv1), ("w2", w2T, inv2)):
        for c in range(32):
            par = (c + (32 if wname == "w2" else 0)) % 4
            srcap = (dr["w1"][c * 128:(c + 1) * 128, :] if wname == "w1"
                     else w2_tile_src_g(dr, c))
            tok_l = P("y", (lambda e, srcap=srcap, par=par:
                            e.dma_start(wstream[:, par, :], srcap)),
                      waits=[ws_free[par], ws_dma[par]], dma=True)
            ws_dma[par] = tok_l
            tok_sa = P("s", (lambda e, par=par, inv=inv:
                             e.activation(w12scr[:], wstream[:, par, :], ACT.Sign,
                                          bias=-0.5, scale=inv[:])),
                       waits=[tok_l, tok_i, scr_free])
            tok_sb_ = P("s", (lambda e, par=par, inv=inv:
                              e.activation(sgA[:], wstream[:, par, :], ACT.Sign,
                                           bias=0.5, scale=inv[:])))
            ws_free[par] = tok_sb_
            tok_w = P("v", lambda e: e.tensor_tensor(w12scr[:], w12scr[:], sgA[:],
                                                     OP.add),
                      waits=[tok_sb_])
            pbx = tr_banks[par % 2]
            for j in range(8):
                t_tok = P("t", (lambda e, j=j, pbx=pbx:
                                e.transpose(pbx[:, j, :],
                                            w12scr[:, j * 128:(j + 1) * 128],
                                            ident[:])),
                          waits=[tok_w, bank_ev[par % 2]])
            scr_free = t_tok
            if wname == "w1":
                dst = w1T[:, :, c * 128:(c + 1) * 128]
            else:
                rt, cc = c // 4, c % 4
                dst = w2T[:, cc * 8:(cc + 1) * 8, rt * 128:(rt + 1) * 128]
            tok_trev = P("v", (lambda e, dst=dst, pbx=pbx:
                               e.tensor_copy(dst, pbx[:, :, :])),
                         waits=[t_tok])
            bank_ev[par % 2] = tok_trev
    tok_tern = tok_trev
    tok_tern_t = ("t", plan.cnt["t"])

    xt_free = list(ck["xt"])
    xdmaB = list(ck["xdma"])
    h_free = [Z0, Z0]
    bank_free = {i: Z0 for i in range(4)}
    mm1_free = {0: Z0, 1: tabs_free}
    g_free = [Z0, Z0]
    srd_free = Z0
    out_free = [Z0, Z0]
    xnB_free = [Z0, Z0]
    xnB_tok = [Z0, Z0]
    srd_tok = Z0

    def dma_xnB(bt):
        par = bt % 2
        tok = P("y", (lambda e, bt=bt, par=par:
                      e.dma_start(xnB[:, par, :, :],
                                  dr["xn_sp"][:, :, bt * SEGB:(bt + 1) * SEGB]
                                  .rearrange("a p t -> p a t"))),
                waits=[xnB_free[par], xnB_tok[par], tok_tern], dma=True)
        xnB_tok[par] = tok

    dma_xnB(0)
    for bt in range(NBT):
        par = bt % 2
        if bt + 1 < NBT:
            dma_xnB(bt + 1)
        srd_tok = P("y", (lambda e, bt=bt:
                          e.dma_start(s_rd[:, 0:2, :],
                                      dr["s_sp"][bt * SEGB:(bt + 1) * SEGB, :]
                                      .rearrange("(a p) f -> p a f", p=128))),
                    waits=[srd_free, srd_tok, tok_tern], dma=True)
        for tsub in range(2):
            gidx = bt * 2 + tsub
            slot = gidx % 4
            xt_tok = P("y", (lambda e, gidx=gidx, slot=slot:
                             e.dma_start(xt[:, slot, :],
                                         dr["x"][gidx * 128:(gidx + 1) * 128, :])),
                       waits=[xt_free[slot], xdmaB[slot]], dma=True)
            xdmaB[slot] = xt_tok
            if tsub == 0:
                xt_tok0 = xt_tok
            else:
                xt_tok1 = xt_tok
        for f1c in range(32):
            hpar = f1c % 2
            bank = pb[4] if hpar == 0 else pb[7]
            for fc in range(8):
                t_tok = P("t", (lambda e, f1c=f1c, fc=fc, bank=bank, par=par:
                                e.matmul(bank[:, 0:SEGB],
                                         w1T[:, fc, f1c * 128:(f1c + 1) * 128],
                                         xnB[:, par, fc, :], start=(fc == 0),
                                         stop=(fc == 7), skip_group_check=True)),
                          waits=[xnB_tok[par], mm1_free[hpar], tok_tern_t])
            tok_mm1 = t_tok
            if f1c > 0:
                pf1c, phpar = f1c - 1, (f1c - 1) % 2
                for tsub in range(2):
                    for half in range(2):
                        j = tsub * 2 + half
                        t_tok = P("t", (lambda e, pf1c=pf1c, phpar=phpar,
                                        tsub=tsub, half=half, j=j:
                                        e.matmul(pb[j][:, :],
                                                 hT[:, phpar, tsub * 128:(tsub + 1) * 128],
                                                 w2T[:, pf1c, half * 512:(half + 1) * 512],
                                                 start=(pf1c == 0), stop=False,
                                                 skip_group_check=True)),
                                  waits=[tok_gelu_prev, bank_free[j]])
                mm1_free[phpar] = t_tok
                h_free[phpar] = t_tok
            tok_gelu_prev = P("s", (lambda e, hpar=hpar, f1c=f1c, bank=bank:
                                    e.activation(hT[:, hpar, :], bank[:, 0:SEGB],
                                                 ACT.Gelu, scale=0.5,
                                                 bias=b1cols[:, f1c:f1c + 1])),
                              waits=[tok_mm1, h_free[hpar], tok_tern])
        pf1c, phpar = 31, 31 % 2
        for tsub in range(2):
            for half in range(2):
                j = tsub * 2 + half
                t_tok = P("t", (lambda e, pf1c=pf1c, phpar=phpar, tsub=tsub,
                                half=half, j=j:
                                e.matmul(pb[j][:, :],
                                         hT[:, phpar, tsub * 128:(tsub + 1) * 128],
                                         w2T[:, pf1c, half * 512:(half + 1) * 512],
                                         start=False, stop=False,
                                         skip_group_check=True)),
                          waits=[tok_gelu_prev, bank_free[j]])
        mm1_free[phpar] = t_tok
        h_free[phpar] = t_tok
        for tsub in range(2):
            for half in range(2):
                j = tsub * 2 + half
                t_tok = P("t", (lambda e, j=j, half=half:
                                e.matmul(pb[j][:, :], tb["two_row"][:],
                                         b2bf[0:1, half * 512:(half + 1) * 512],
                                         start=False, stop=True,
                                         skip_group_check=True)))
                tok_g = P("s", (lambda e, j=j, tsub=tsub, half=half:
                                e.activation(g_sb[:, tsub, half * 512:(half + 1) * 512],
                                             pb[j][:, :], ACT.Silu, scale=0.5)),
                          waits=[t_tok, g_free[tsub]])
                bank_free[j] = tok_g
            gidx = bt * 2 + tsub
            slot = gidx % 4
            opar = gidx % 2
            xt_tok = xt_tok0 if tsub == 0 else xt_tok1
            tok_c1 = P("v", (lambda e, tsub=tsub, opar=opar:
                             e.tensor_tensor(outt[:, opar, :], s_rd[:, tsub, :],
                                             g_sb[:, tsub, :], OP.mult)),
                       waits=[tok_g, srd_tok, out_free[opar]])
            tok_c2 = P("v", (lambda e, slot=slot, opar=opar:
                             e.tensor_tensor(outt[:, opar, :], outt[:, opar, :],
                                             xt[:, slot, :], OP.add)),
                       waits=[tok_c1, xt_tok])
            g_free[tsub] = tok_c2
            xt_free[slot] = tok_c2
            tok_od = P("g", (lambda e, gidx=gidx, opar=opar:
                             e.dma_start(dr["out"][gidx * 128:(gidx + 1) * 128, :],
                                         outt[:, opar, :])),
                       waits=[tok_c2], dma=True)
            out_free[opar] = tok_od
            if tsub == 1:
                srd_free = tok_c1
        xnB_free[par] = ("t", plan.cnt["t"])


_NC_CACHE = {}


def _get_program(L=4096):
    if L not in _NC_CACHE:
        _NC_CACHE[L] = build_program(L=L)
    return _NC_CACHE[L]


def kernel(L0_attn_x, L0_attn_Wq, L0_attn_Wk, L0_attn_Wv, L0_attn_Wlr,
           L0_attn_Wbeta, L0_attn_Wo, L0_attn_W_in_init, L0_attn_W_out_init,
           L0_ffn_W1, L0_ffn_b1, L0_ffn_W2, L0_ffn_b2):
    x = np.ascontiguousarray(np.asarray(L0_attn_x, dtype=np.float32))
    B, L, _ = x.shape
    base = {
        "wq": np.ascontiguousarray(np.asarray(L0_attn_Wq, np.float32)),
        "wk": np.ascontiguousarray(np.asarray(L0_attn_Wk, np.float32)),
        "wv": np.ascontiguousarray(np.asarray(L0_attn_Wv, np.float32)),
        "wo": np.ascontiguousarray(np.asarray(L0_attn_Wo, np.float32)),
        "wlr": np.ascontiguousarray(np.asarray(L0_attn_Wlr, np.float32)),
        "wbeta": np.ascontiguousarray(np.asarray(L0_attn_Wbeta, np.float32)),
        "winit": np.ascontiguousarray(
            np.asarray(L0_attn_W_in_init, np.float32).reshape(64, 16 * 64)),
        "woutit": np.ascontiguousarray(
            np.asarray(L0_attn_W_out_init, np.float32).reshape(64, 16 * 64)),
        "w1": np.ascontiguousarray(np.asarray(L0_ffn_W1, np.float32)),
        "b1": np.ascontiguousarray(np.asarray(L0_ffn_b1, np.float32)),
        "w2": np.ascontiguousarray(np.asarray(L0_ffn_W2, np.float32)),
        "b2": np.ascontiguousarray(np.asarray(L0_ffn_b2, np.float32)),
    }
    base.update(host_constants())
    nc = _get_program(L=L)
    in_maps = [dict(base, x=x[i]) for i in range(B)]
    res = run_bass_kernel_spmd(nc, in_maps, core_ids=list(range(B)))
    return np.stack([res.results[i]["out"] for i in range(B)], axis=0)


# version 05
# speedup vs baseline: 1.0933x; 1.0470x over previous
import numpy as np
import ml_dtypes
import concourse.bass as bass
import concourse.mybir as mybir
from concourse.bass_utils import run_bass_kernel_spmd
from concourse.bass_isa_rust import ReduceOp

F32 = mybir.dt.float32
BF16 = mybir.dt.bfloat16
AX = mybir.AxisListType
OP = mybir.AluOpType
ACT = mybir.ActivationFunctionType

D = 1024
NCORE = 8
CH = 64
NP = 8
FF = 4096
SEG = 256
SEGB = 256
BASE_LR = 0.001
BASE_WD = 0.9


class Plan:

    NROT = 8
    ENGS = ("t", "v", "s", "g", "y")
    SEMS = tuple(["t", "v", "s", "g"]
                 + [f"dy{i}" for i in range(8)] + [f"dg{i}" for i in range(8)])

    def __init__(self):
        self.ops = {e: [] for e in self.ENGS}
        self.cnt = {sm: 0 for sm in self.SEMS}
        self.dma_i = {"y": 0, "g": 0}
        self._waited = {e: {sm: 0 for sm in self.SEMS} for e in self.ENGS}

    def op(self, eng, fn, waits=(), inc=True, dma=False):
        w = list(waits)
        if dma:
            i = self.dma_i[eng]
            self.dma_i[eng] += 1
            sem_name = f"d{eng}{i % self.NROT}"
            amt = 16
            if self.cnt[sem_name] > 0:
                w.append((sem_name, self.cnt[sem_name]))
        else:
            sem_name = eng
            amt = 1
        tok = None
        if inc:
            self.cnt[sem_name] += amt
            tok = self.cnt[sem_name]
        ww = []
        for (sm, c) in w:
            if c is None or c <= 0:
                continue
            if self._waited[eng][sm] >= c:
                continue
            self._waited[eng][sm] = c
            ww.append((sm, c))
        self.ops[eng].append((fn, tuple(ww), sem_name if inc else None, amt))
        return tok

    def barrier_tokens(self):
        return dict(self.cnt)


def emit_engine(plan, eng, engobj, sems):
    for fn, waits, inc_sem, amt in plan.ops[eng]:
        for (sm, c) in waits:
            engobj.wait_ge(sems[sm], c)
        inst = fn(engobj)
        if inc_sem is not None:
            assert inst is not None, f"op on {eng} must return instruction to inc"
            inst.then_inc(sems[inc_sem], amt)


def host_constants():
    c = {}
    c["c_ident"] = np.eye(128, dtype=ml_dtypes.bfloat16)
    c["c_identf"] = np.eye(128, dtype=np.float32)
    sel2 = np.zeros((2, 128), dtype=ml_dtypes.bfloat16)
    sel2[0, 0:64] = 1.0
    sel2[1, 64:128] = 1.0
    c["c_sel2"] = sel2
    mask2 = np.zeros((128, 2), dtype=ml_dtypes.bfloat16)
    mask2[0:64, 0] = 1.0
    mask2[64:128, 1] = 1.0
    c["c_mask2"] = mask2
    lsc = np.zeros((64, 1), dtype=np.float32)
    lsc[0:32, 0] = BASE_LR / CH
    lsc[32:64, 0] = BASE_WD / CH
    c["c_lrwdscale"] = lsc
    c["c_two"] = np.full((1, 128), 2.0, dtype=ml_dtypes.bfloat16)
    c["c_pool"] = np.tile(np.array([[1e-5, -0.5, 0.5, 1.0 / (FF * D), 1e-8,
                                     -16.0]], dtype=np.float32), (128, 1))
    selbig = np.zeros((64, 32, 128), dtype=ml_dtypes.bfloat16)
    for p in range(NP):
        for q in range(4):
            idx = q * 8 + p
            selbig[q * 16 + 2 * p, idx, 0:64] = 1.0
            selbig[q * 16 + 2 * p + 1, idx, 64:128] = 1.0
    c["c_selbig"] = selbig.reshape(64, 32 * 128)
    return c


def build_program(L=4096, dbg=(), skip_passB=False):
    NSEG = L // SEG
    NCH = SEG // CH
    nc = bass.Bass("TRN2", target_bir_lowering=False)

    dr = {}
    dr["x"] = nc.dram_tensor("x", [L, D], F32, kind="ExternalInput").ap()
    for nm, shp in [("wq", [D, D]), ("wk", [D, D]), ("wv", [D, D]),
                    ("wo", [D, D]), ("wlr", [32, D]), ("wbeta", [32, D]),
                    ("winit", [64, 16 * 64]), ("woutit", [64, 16 * 64]),
                    ("w1", [FF, D]), ("b1", [FF]), ("w2", [D, FF]), ("b2", [D])]:
        dr[nm] = nc.dram_tensor(nm, shp, F32, kind="ExternalInput").ap()
    consts = host_constants()
    for nm, arr in consts.items():
        dt = BF16 if arr.dtype == ml_dtypes.bfloat16 else F32
        dr[nm] = nc.dram_tensor(nm, list(arr.shape), dt, kind="ExternalInput").ap()
    dr["out"] = nc.dram_tensor("out", [L, D], F32, kind="ExternalOutput").ap()
    dr["s_sp"] = nc.dram_tensor("s_sp", [L, D], BF16, kind="Internal").ap()
    dr["xn_sp"] = nc.dram_tensor("xn_sp", [8, 128, L], BF16, kind="Internal").ap()
    dbg_dr = {}
    for nm, shp, dt in dbg:
        dbg_dr[nm] = nc.dram_tensor(nm, shp, dt, kind="ExternalOutput").ap()

    plan = Plan()

    import contextlib
    with contextlib.ExitStack() as st:
        def T(*args, **kw):
            return st.enter_context(nc.sbuf_tensor(*args, **kw))

        def PS(*args, **kw):
            return st.enter_context(nc.psum_tensor(*args, **kw))

        ident = T("ident", [128, 128], BF16)
        identf = T("identf", [128, 128], F32)
        sel2 = T("sel2", [2, 128], BF16)
        mask2 = T("mask2", [128, 2], BF16)
        lrwdscale = T("lrwdscale", [64, 1], F32)
        two_row = T("two_row", [1, 128], BF16)
        cpool = T("cpool", [128, 8], F32)
        nc.const_aps.register(F32, 1e-5, cpool[:, 0:1])
        nc.const_aps.register(F32, -0.5, cpool[:, 1:2])
        nc.const_aps.register(F32, 0.5, cpool[:, 2:3])
        nc.const_aps.register(F32, 1.0 / (FF * D), cpool[:, 3:4])
        nc.const_aps.register(F32, 1e-8, cpool[:, 4:5])
        nc.const_aps.register(F32, -16.0, cpool[:, 5:6])
        xt = T("xt", [128, 4, D], F32)
        outt = T("outt", [128, 2, D], F32)
        abf = T("abf", [128, 76 * 1024], BF16)
        af32 = T("af32", [128, 6 * 1024], F32)
        pb0big = PS("pb0", [128, 1024], F32)
        pb = [pb0big, pb0big,
              PS("pb2", [128, 512], F32), PS("pb3", [128, 512], F32),
              PS("pb4", [128, 512], F32), PS("pb5", [128, 1024], BF16),
              PS("pb6", [128, 1024], BF16), PS("pb7", [128, 512], F32)]
        sems = {nm: st.enter_context(nc.semaphore("sem_" + nm))
                for nm in Plan.SEMS}
        block = st.enter_context(nc.Block())
        tb = dict(ident=ident, identf=identf, sel2=sel2, mask2=mask2,
                  lrwdscale=lrwdscale, two_row=two_row, cpool=cpool,
                  xt=xt, outt=outt, abf=abf, af32=af32, pb=pb)

        emit_all(nc, plan, dr, dbg_dr, tb, L, NSEG, NCH, skip_passB)

        @block.tensor
        def _(t):
            emit_engine(plan, "t", t, sems)

        @block.vector
        def _(v):
            emit_engine(plan, "v", v, sems)

        @block.scalar
        def _(s):
            emit_engine(plan, "s", s, sems)

        @block.gpsimd
        def _(g):
            emit_engine(plan, "g", g, sems)

        @block.sync
        def _(y):
            emit_engine(plan, "y", y, sems)

    return nc


class Arena:

    def __init__(self, tensor, n_elems):
        self.t = tensor
        self.n = n_elems
        self.off = 0
        self.marks = {}
        self.slots = {}

    def alloc(self, name, shape):
        n = int(np.prod(shape[1:]))
        assert self.off + n <= self.n, f"arena overflow at {name}: {self.off}+{n}>{self.n}"
        base = self.off
        self.off += n
        self.slots[name] = (base, shape)
        return self.view(name)

    def view(self, name):
        base, shape = self.slots[name]
        ap = self.t[:shape[0], base:base + int(np.prod(shape[1:]))]
        if len(shape) > 2:
            pat = "p (" + " ".join(f"d{i}" for i in range(1, len(shape))) + ") -> p " + \
                  " ".join(f"d{i}" for i in range(1, len(shape)))
            kw = {f"d{i}": shape[i] for i in range(2, len(shape))}
            ap = ap.rearrange(pat, **kw)
        return ap

    def mark(self, tag):
        self.marks[tag] = self.off

    def reset_to(self, tag):
        self.off = self.marks[tag]



Z0 = ("v", 0)


DEBUG_ARENAS = {}


def emit_all(nc, plan, dr, dbg_dr, tb, L, NSEG, NCH, skip_passB=False):
    def P(eng, fn, waits=(), dma=False):
        if dma:
            sem_name = f"d{eng}{plan.dma_i[eng] % plan.NROT}"
            tok = plan.op(eng, fn, waits=waits, inc=True, dma=True)
            return (sem_name, tok)
        tok = plan.op(eng, fn, waits=waits, inc=True, dma=False)
        return (eng, tok)

    pb = tb["pb"]
    A = Arena(tb["abf"], 76 * 1024)
    A4 = Arena(tb["af32"], 6 * 1024)
    DEBUG_ARENAS["A"] = A
    DEBUG_ARENAS["A4"] = A4
    ident = tb["ident"]
    identf = tb["identf"]

    b2bf = A.alloc("b2bf", [1, 1024])
    A.mark("postperm")
    selbig = A.alloc("selbig", [64, 32, 128])
    wqT = A.alloc("wqT", [128, 8, 1024])
    wkT = A.alloc("wkT", [128, 8, 1024])
    wvT = A.alloc("wvT", [128, 8, 1024])
    woT = A.alloc("woT", [128, 8, 1024])
    wlrbT = A.alloc("wlrbT", [128, 8, 64])
    winT = A.alloc("winT", [128, 8, 64])
    wout = A.alloc("wout", [128, 8, 64])
    woutT = A.alloc("woutT", [128, 8, 64])
    A.mark("postAW")
    wnat = A.alloc("wnat", [128, 8, 1024])
    winit_nat = A.alloc("winit_nat", [64, 1024])
    woutit_nat = A.alloc("woutit_nat", [64, 1024])
    wlrb_nat = A.alloc("wlrb_nat", [64, 1024])
    A.reset_to("postAW")

    b1nat = A4.alloc("b1nat", [32, 128])
    b1cols = A4.alloc("b1cols", [128, 32])
    lrwd_f = A4.alloc("lrwd_f", [64, 2, 256])
    lrwdc = A4.alloc("lrwdc", [64, 2, 4])
    lrwdB = A4.alloc("lrwdB", [128, 2, 4, 8, 4])
    SUMS = A4.alloc("SUMS", [128, 16])
    RECIP = A4.alloc("RECIP", [128, 16])
    CLI = A4.alloc("CLI", [128, 8])
    CLO = A4.alloc("CLO", [128, 8])
    nmq = A4.alloc("nmq", [128, 2])
    bnst = A4.alloc("bnst", [128, 12])
    mv = A4.alloc("mv", [128, 2])
    rstd = A4.alloc("rstd", [128, 1])
    lntmp = A4.alloc("lntmp", [128, 1])
    xt = tb["xt"]
    outt = tb["outt"]

    pb5w = pb[5].ap().rearrange("p (a b) -> p a b", b=128)
    pb5v = pb[5].ap().rearrange("p (a b) -> p a b", b=64)
    pb6w = pb[6].ap().rearrange("p (a b) -> p a b", b=128)
    pb6v = pb[6].ap().rearrange("p (a b) -> p a b", b=64)
    pb7v = pb[7].ap().rearrange("p (a b) -> p a b", b=64)

    dstmap = {"c_ident": ident, "c_identf": identf, "c_sel2": tb["sel2"],
              "c_mask2": tb["mask2"], "c_lrwdscale": tb["lrwdscale"],
              "c_two": tb["two_row"]}
    cy_toks = []
    for nm, dst in dstmap.items():
        cy_toks.append(P("y", lambda e, dst=dst, nm=nm: e.dma_start(dst[:], dr[nm]),
                         dma=True))
    cy_toks.append(P("y", lambda e: e.dma_start(tb["cpool"][:, 0:6], dr["c_pool"]),
                    dma=True))
    cy_toks.append(P("y", lambda e: e.dma_start(selbig.rearrange("p a b -> p (a b)"),
                                                dr["c_selbig"]), dma=True))
    cy_toks.append(P("y", lambda e: e.dma_start(b1nat[:],
                                                dr["b1"].rearrange("(a b) -> a b", b=128)),
                    dma=True))
    cg_toks2 = [
        P("g", lambda e: e.dma_start(winit_nat[:], dr["winit"]), dma=True),
        P("g", lambda e: e.dma_start(woutit_nat[:], dr["woutit"]), dma=True),
        P("g", lambda e: e.dma_start(wlrb_nat[0:32, :], dr["wlr"]), dma=True),
        P("g", lambda e: e.dma_start(wlrb_nat[32:64, :], dr["wbeta"]), dma=True),
    ]
    cg_toks = [P("g", lambda e: e.dma_start(b2bf[:],
                                            dr["b2"].rearrange("(a b) -> a b", a=1)),
                dma=True)]

    t_tok = P("t", lambda e: e.transpose(pb[7][0:128, 0:32], b1nat[:],
                                         identf[0:32, 0:32]),
              waits=cy_toks)
    tok_b1 = P("v", lambda e: e.tensor_copy(b1cols[:], pb[7][0:128, 0:32]),
               waits=[t_tok])

    for h in range(16):
        p_, hh = h // 2, h % 2
        sl = slice(h * 64, (h + 1) * 64)
        hsl = slice(hh * 64, (hh + 1) * 64)
        t_tok = P("t", (lambda e, sl=sl, p_=p_, hh=hh, hsl=hsl:
                        e.transpose(pb5v[hsl, p_, :], winit_nat[:, sl],
                                    ident[0:64, 0:64], tile_position=(0, 64 * hh))),
                  waits=cg_toks + cg_toks2 + cy_toks + [tok_b1])
        t_tok = P("t", (lambda e, sl=sl, p_=p_, hh=hh, hsl=hsl:
                        e.transpose(pb5v[hsl, 8 + p_, :], woutit_nat[:, sl],
                                    ident[0:64, 0:64], tile_position=(0, 64 * hh))))
        t_tok = P("t", (lambda e, sl=sl, p_=p_, hh=hh, hsl=hsl:
                        e.matmul(pb7v[hsl, p_, :], ident[0:64, 0:64],
                                 woutit_nat[:, sl], start=True, stop=True,
                                 skip_group_check=True, tile_position=(0, 64 * hh))))
    P("v", lambda e: e.tensor_copy(winT[:], pb5v[:, 0:8, :]), waits=[t_tok])
    P("v", lambda e: e.tensor_copy(woutT[:], pb5v[:, 8:16, :]))
    tok_state0 = P("v", lambda e: e.tensor_copy(wout[:], pb7v[:, 0:8, :]))

    for fc in range(8):
        t_tok = P("t", (lambda e, fc=fc:
                        e.transpose(pb6v[0:128, 8 + fc, :],
                                    wlrb_nat[:, fc * 128:(fc + 1) * 128],
                                    ident[0:64, 0:64])),
                  waits=[tok_state0])
    tok_evac = P("v", lambda e: e.tensor_copy(wlrbT[:], pb6v[:, 8:16, :]),
                 waits=[t_tok])

    tok_wload = Z0
    for wname, wT in (("wq", wqT), ("wk", wkT), ("wv", wvT), ("wo", woT)):
        wnat_free = ("t", plan.cnt["t"])
        tok_wload = P("g", (lambda e, wname=wname:
                            e.dma_start(wnat[:],
                                        dr[wname].rearrange("(c p) f -> p c f", p=128))),
                      waits=[wnat_free, tok_wload], dma=True)
        for fc in range(8):
            for c in range(8):
                t_tok = P("t", (lambda e, fc=fc, c=c:
                                e.transpose(pb5w[:, c, :],
                                            wnat[:, c, fc * 128:(fc + 1) * 128],
                                            ident[:])),
                          waits=[tok_wload, tok_evac])
            tok_evac = P("v", (lambda e, wT=wT, fc=fc:
                               e.tensor_copy(wT[:, fc, :],
                                             pb5w.rearrange("p a b -> p (a b)"))),
                         waits=[t_tok])

    qkv = A.alloc("qkv", [128, 2, 8, NCH, 192])
    xn_seg = A.alloc("xn_seg", [128, 2, 8, SEG])
    o_seg = A.alloc("o_seg", [128, 2, 8, SEG])
    s_seg = A.alloc("s_seg", [128, 2, 2, 1024])
    E_sb = A.alloc("E_sb", [128, 8, 128])
    p_q = A.alloc("p_q", [128, 8, 64])
    p_li = A.alloc("p_li", [128, 8, 64])
    m_sb = A.alloc("m_sb", [128, 8, 64])
    a_sb = A.alloc("a_sb", [128, 8, 64])
    t_sb = A.alloc("t_sb", [128, 8, 64])
    gzp = A.alloc("gzp", [128, 8, 128])
    s2_sb = A.alloc("s2_sb", [2, 8, 64])
    rk2 = A.alloc("rk2", [128, 8, 2])
    kvT_sb = A.alloc("kvT_sb", [128, 8, 128])
    gzpT_sb = A.alloc("gzpT_sb", [128, 8, 128])
    xn_tok = A.alloc("xn_tok", [128, 2, 1024])
    lrwdc_bf = A.alloc("lrwdc_bf", [64, 2, 4])
    A.mark("passA_end")

    z16 = pb[0].ap().rearrange("p (a b) -> p a b", b=64)
    pbv = [z16[:, 0:8, :], z16[:, 8:16, :],
           pb[2].ap().rearrange("p (a b) -> p a b", b=64),
           pb[3].ap().rearrange("p (a b) -> p a b", b=64),
           pb[4].ap().rearrange("p (a b) -> p a b", b=64)]
    views = dict(qkv=qkv, xn_seg=xn_seg, o_seg=o_seg, s_seg=s_seg, E_sb=E_sb,
                 p_q=p_q, p_li=p_li, m_sb=m_sb, a_sb=a_sb, t_sb=t_sb,
                 gzp=gzp, s2_sb=s2_sb, rk2=rk2, kvT_sb=kvT_sb,
                 gzpT_sb=gzpT_sb, winT=winT, wout=wout, woutT=woutT,
                 SUMS=SUMS, RECIP=RECIP, CLI=CLI, CLO=CLO, nmq=nmq,
                 lrwdB=lrwdB, pbv=pbv, pb5w=pb5w, pb6w=pb6w,
                 wqT=wqT, wkT=wkT, wvT=wvT, woT=woT, wlrbT=wlrbT,
                 lrwd_f=lrwd_f, lrwdc=lrwdc, bnst=bnst, mv=mv, rstd=rstd,
                 lntmp=lntmp, xn_tok=xn_tok, selbig=selbig,
                 lrwdc_bf=lrwdc_bf)

    wstream = A4.alloc("wstream", [128, 4, 1024])
    acc1 = A4.alloc("accw1", [128, 1])
    acc2 = A4.alloc("accw2", [128, 1])
    tabs = A4.alloc("tabsw", [128, 1])
    t1bufs = dict(wstream=wstream, acc1=acc1, acc2=acc2, tabs=tabs)

    ck = {
        "t1_i": 0, "t1_pf": 0, "t1_ws": [Z0, Z0, Z0, Z0],
        "t1_dma": [Z0, Z0, Z0, Z0], "t1_tabs": Z0, "xdma": [Z0, Z0, Z0, Z0],
        "upd_win": tok_state0, "upd_wout": tok_state0, "wt_evac": tok_state0,
        "b0": Z0, "b1": Z0, "b2": Z0, "b3": tok_b1, "b4": Z0,
        "b5": tok_evac, "b6": tok_evac, "b7": Z0,
        "eq": Z0, "ek": Z0, "pq": Z0, "pli": Z0, "plo": Z0, "m": Z0, "a": Z0,
        "gz": Z0, "rk2": Z0, "s2": Z0, "kvt": Z0, "gzpt": Z0,
        "nm": Z0, "oseg": Z0, "xt": [Z0, Z0, Z0, Z0], "xntok": [Z0, Z0],
        "mv": Z0, "ln_ev": Z0, "proj_last": ("s", 0), "lrwd_lc": Z0,
        "lrwdc_r": Z0, "o_evac": ("s", 0), "qkv_ready": ("s", 0),
        "lrwdB_ready": Z0,
        "lrwdB_ready_p": [Z0, Z0], "lrwdB_r_p": [Z0, Z0],
        "oseg_p": [Z0, Z0], "xnseg_p": [Z0, Z0], "qkv_w": [Z0, Z0],
        "sseg_p": [Z0, Z0], "lrwdf_p": [Z0, Z0], "xnsp_dma_p": [Z0, Z0],
    }

    views.update(t1bufs)
    for stage in range(NSEG + 1):
        emit_stage(plan, P, dr, tb, views, ck, stage, NSEG, NCH)

    barA = plan.barrier_tokens()
    bar_toks = [(sm, c) for sm, c in barA.items() if c > 0]
    if not skip_passB:
        A.reset_to("postperm")
        emit_passB(nc, plan, P, dr, dbg_dr, tb, A, A4, bar_toks, L,
                   b2bf, b1cols, ident, identf, ck, t1bufs)

    fin = [(sm, plan.cnt[sm]) for sm in Plan.SEMS
           if sm.startswith("d") and plan.cnt[sm] > 0]
    plan.op("y", lambda e: None, waits=fin, inc=False)


def emit_stage(plan, P, dr, tb, V, ck, stage, NSEG, NCH):
    pb = tb["pb"]
    pb5w = V["pb5w"]
    ident = tb["ident"]
    xt = tb["xt"]

    fillers = []
    if stage < NSEG:
        seg = stage
        par = seg % 2
        qkv = V["qkv"][:, par]
        xn_seg = V["xn_seg"][:, par]

        for tsub in range(2):
            gidx = seg * 2 + tsub
            slot = gidx % 4
            ck["xdma"][slot] = P("y", (lambda e, gidx=gidx, slot=slot:
                                       e.dma_start(xt[:, slot, :],
                                                   dr["x"][gidx * 128:(gidx + 1) * 128, :])),
                                 waits=[ck["xt"][slot], ck["xdma"][slot]],
                                 dma=True)
        if stage == 0:
            emit_t1_prefetch(plan, P, dr, V, ck, 4)

        def ln_sub(tsub, seg=seg, par=par, xn_seg=xn_seg):
            gidx = seg * 2 + tsub
            xpar = gidx % 4
            tok_x = ck["xdma"][xpar]
            tok_bn = P("v", (lambda e, xpar=xpar:
                             e.bn_stats(V["bnst"][:, 0:6], xt[:, xpar, 0:512])),
                       waits=[tok_x, ck["mv"]])
            tok_bn = P("v", (lambda e, xpar=xpar:
                             e.bn_stats(V["bnst"][:, 6:12], xt[:, xpar, 512:1024])))
            tok_mv = P("v", lambda e: e.bn_aggr(V["mv"][:], V["bnst"][:, 0:12]),
                       waits=[tok_bn])
            tok_ln = P("s", (lambda e: e.activation(V["lntmp"][:], V["mv"][:, 1:2],
                                                    ACT.Ln, bias=1e-5)),
                      waits=[tok_mv])
            tok_rstd = P("s", (lambda e: e.activation(V["rstd"][:], V["lntmp"][:],
                                                      ACT.Exp, scale=-0.5)),
                         waits=[tok_ln])
            ck["mv"] = tok_rstd
            npar = gidx % 2
            tok_xntok = P("v", (lambda e, xpar=xpar, npar=npar:
                                e.tensor_scalar(V["xn_tok"][:, npar, :],
                                                xt[:, xpar, :],
                                                V["mv"][:, 0:1], V["rstd"][:],
                                                op0=OP.subtract, op1=OP.mult)),
                          waits=[tok_rstd, tok_mv, ck["xntok"][npar]])
            ck["xt"][xpar] = tok_xntok
            for fc in range(8):
                t_tok = P("t", (lambda e, npar=npar, fc=fc:
                                e.transpose(pb5w[:, fc, :],
                                            V["xn_tok"][:, npar, fc * 128:(fc + 1) * 128],
                                            ident[:])),
                          waits=[tok_xntok, ck["b5"]])
            ck["xntok"][npar] = t_tok
            tok_ev = P("v", (lambda e, tsub=tsub, xn_seg=xn_seg:
                             e.tensor_copy(xn_seg[:, :, tsub * 128:(tsub + 1) * 128],
                                           pb5w[:, :, :])),
                       waits=[t_tok, ck["xnseg_p"][par], ck["xnsp_dma_p"][par]])
            ck["b5"] = tok_ev
            ck["ln_ev"] = tok_ev

        fillers.append(lambda: ln_sub(0))
        fillers.append(lambda: ln_sub(1))

        def proj_tile(wTname, m, coff, actf, seg=seg, par=par, qkv=qkv,
                      xn_seg=xn_seg):
            wT = V[wTname]
            for fc in range(8):
                t_tok = P("t", (lambda e, wT=wT, m=m, fc=fc:
                                e.matmul(pb[3][:, 0:SEG],
                                         wT[:, fc, m * 128:(m + 1) * 128],
                                         xn_seg[:, fc, :],
                                         start=(fc == 0), stop=(fc == 7),
                                         skip_group_check=True)),
                          waits=[ck["ln_ev"], ck["b3"]])
            bview = pb[3].ap().rearrange("p (a b) -> p a b", b=64)
            tok_evac = P("s", (lambda e, bview=bview, m=m, coff=coff, actf=actf,
                               qkv=qkv:
                               e.activation(qkv[:, m, :, coff:coff + 64],
                                            bview[:, 0:NCH, :], actf)),
                         waits=[t_tok, ck["qkv_w"][par]])
            ck["b3"] = tok_evac
            ck["proj_last"] = tok_evac

        for wTname, coff, actf in (("wqT", 0, ACT.Silu), ("wkT", 64, ACT.Silu),
                                   ("wvT", 128, ACT.Copy)):
            for m in range(8):
                fillers.append(lambda wTname=wTname, m=m, coff=coff, actf=actf:
                               proj_tile(wTname, m, coff, actf))

        def lrwd_proj(seg=seg, par=par, xn_seg=xn_seg):
            for fc in range(8):
                t_tok = P("t", (lambda e, fc=fc:
                                e.matmul(pb[3][0:64, 0:SEG], V["wlrbT"][:, fc, :],
                                         xn_seg[:, fc, :], start=(fc == 0),
                                         stop=(fc == 7), skip_group_check=True)),
                          waits=[ck["ln_ev"], ck["b3"]])
            tok_sg = P("s", (lambda e, par=par:
                             e.activation(V["lrwd_f"][:, par, :], pb[3][0:64, 0:SEG],
                                          ACT.Sigmoid)),
                       waits=[t_tok, ck["lrwdf_p"][par]])
            ck["b3"] = tok_sg
            tok_rd = P("v", (lambda e, par=par:
                             e.reduce_sum(V["lrwdc"][:, par, :],
                                          V["lrwd_f"][:, par]
                                          .rearrange("p (a b) -> p a b", b=64),
                                          axis=AX.X)),
                      waits=[tok_sg, ck["lrwdc_r"]])
            ck["lrwdf_p"][par] = tok_rd
            tok_lc = P("v", (lambda e, par=par:
                             e.tensor_scalar(V["lrwdc_bf"][:, par, :],
                                             V["lrwdc"][:, par, :],
                                             tb["lrwdscale"][:], None,
                                             op0=OP.mult)), waits=[tok_rd])
            ck["lrwd_lc"] = tok_lc
            ck["xnseg_p"][par] = ("t", plan.cnt["t"])

        def lrwd_bcast(seg=seg, par=par):
            pb3b = pb[3].ap().rearrange("p (a b) -> p a b", b=4)
            for idx in range(32):
                t_tok = P("t", (lambda e, idx=idx, par=par:
                                e.matmul(pb3b[:, idx, :],
                                         V["selbig"][:, idx, :],
                                         V["lrwdc_bf"][:, par, :], start=True,
                                         stop=True, skip_group_check=True)),
                          waits=[ck["lrwd_lc"], ck["b3"]])
            tok_lB = P("v", (lambda e, par=par:
                             e.tensor_copy(V["lrwdB"][:, par]
                                           .rearrange("p a b c -> p (a b c)"),
                                           pb[3][:, 0:128])),
                       waits=[t_tok, ck["lrwdB_r_p"][par]])
            ck["b3"] = tok_lB
            ck["lrwdB_ready_p"][par] = tok_lB
            ck["lrwdc_r"] = ck["lrwd_lc"]

        fillers.append(lrwd_proj)
        fillers.append(lrwd_bcast)

        def xn_spill(seg=seg, par=par, xn_seg=xn_seg):
            tok = P("g", (lambda e, seg=seg, xn_seg=xn_seg:
                          e.dma_start(dr["xn_sp"][:, :, seg * SEG:(seg + 1) * SEG]
                                      .rearrange("a p t -> p a t"),
                                      xn_seg[:])),
                    waits=[ck["ln_ev"]], dma=True)
            ck["xnsp_dma_p"][par] = tok

        fillers.append(xn_spill)
        n_t1 = (64 + NSEG - 1) // NSEG
        for _ in range(n_t1):
            fillers.append(lambda: emit_t1_compute(plan, P, dr, V, ck, 1))

    if stage == 0:
        for f in fillers:
            f()
        ck["qkv_ready"] = ck["proj_last"]
        return

    seg = stage - 1
    par = seg % 2
    Vp = dict(V)
    Vp["qkv"] = V["qkv"][:, par]
    Vp["o_seg"] = V["o_seg"][:, par]
    Vp["lrwdB"] = V["lrwdB"][:, par]
    ck["lrwdB_ready"] = ck["lrwdB_ready_p"][par]
    ck["oseg"] = ck["oseg_p"][par]
    nf = len(fillers)
    per = max(1, (nf + NCH - 1) // NCH) if nf else 0
    fi = 0
    for ci in range(NCH):
        emit_scan_chunk(plan, P, tb, Vp, ck, ci)
        for _ in range(per):
            if fi < nf:
                fillers[fi]()
                fi += 1
    while fi < nf:
        fillers[fi]()
        fi += 1
    if stage < NSEG:
        ck["qkv_ready"] = ck["proj_last"]
    ck["qkv_w"][par] = ("t", plan.cnt["t"])
    ck["lrwdB_r_p"][par] = ck["upd_win"]

    o_seg = V["o_seg"][:, par]
    s_seg = V["s_seg"][:, par]
    tok_sev = ("s", 0)
    for tsub in range(2):
        for half in range(2):
            for fc in range(8):
                t_tok = P("t", (lambda e, tsub=tsub, half=half, fc=fc,
                                o_seg=o_seg:
                                e.matmul(pb[3][:, :],
                                         o_seg[:, fc, tsub * 128:(tsub + 1) * 128],
                                         V["woT"][:, fc, half * 512:(half + 1) * 512],
                                         start=(fc == 0), stop=(fc == 7),
                                         skip_group_check=True)),
                          waits=[ck["o_evac"], ck["b3"]])
            tok_sev = P("s", (lambda e, tsub=tsub, half=half, s_seg=s_seg:
                              e.activation(s_seg[:, tsub, half * 512:(half + 1) * 512],
                                           pb[3][:, :], ACT.Copy)),
                        waits=[t_tok, ck["sseg_p"][par]])
            ck["b3"] = tok_sev
    ck["oseg_p"][par] = ("t", plan.cnt["t"])
    tok_sdma = P("g", (lambda e, seg=seg, s_seg=s_seg:
                       e.dma_start(dr["s_sp"][seg * SEG:(seg + 1) * SEG, :]
                                   .rearrange("(a p) f -> p a f", p=128),
                                   s_seg[:])),
                 waits=[tok_sev], dma=True)
    ck["sseg_p"][par] = tok_sdma


def emit_scan_chunk(plan, P, tb, V, ck, ci):
    pb = tb["pb"]
    pbv = V["pbv"]
    pb5w, pb6w = V["pb5w"], V["pb6w"]
    ident = tb["ident"]
    qkv, winT, wout, woutT = V["qkv"], V["winT"], V["wout"], V["woutT"]
    E_sb, p_q, p_li = V["E_sb"], V["p_q"], V["p_li"]
    m_sb, a_sb, t_sb, gzp = V["m_sb"], V["a_sb"], V["t_sb"], V["gzp"]
    s2_sb, rk2 = V["s2_sb"], V["rk2"]
    kvT_sb, gzpT_sb = V["kvT_sb"], V["gzpT_sb"]
    SUMS, RECIP, CLI, CLO = V["SUMS"], V["RECIP"], V["CLI"], V["CLO"]
    nmq, lrwdB = V["nmq"], V["lrwdB"]
    pb5WT = pb[5].ap().rearrange("p (a b) -> p a b", b=64)
    pbz = pb[0].ap().rearrange("p (a b) -> p a b", b=128)

    def heads():
        for p in range(NP):
            for hh in range(2):
                yield p, hh, slice(hh * 64, (hh + 1) * 64), (0 if hh == 0 else 64)

    for p, hh, hsl, off in heads():
        t_tok = P("t", (lambda e, p=p, hh=hh, hsl=hsl, off=off:
                        e.matmul(pbz[hsl, p, :], winT[hsl, p, :],
                                 qkv[hsl, p, ci, 0:128], start=True, stop=True,
                                 skip_group_check=True, tile_position=(off, off))),
                  waits=[ck["upd_win"], ck["qkv_ready"],
                         ck["b0"] if p < 4 else ck["b1"]])
    tok_z = t_tok

    for p in range(NP):
        t_tok = P("t", (lambda e, p=p:
                        e.transpose(pb5w[:, p, :], qkv[:, p, ci, 64:192],
                                    ident[:])),
                  waits=[ck["b5"]])
    tok_trkv = t_tok
    tok_kvt = P("v", lambda e: e.tensor_copy(kvT_sb[:], pb5w[:, 0:8, :]),
                waits=[tok_trkv, ck["kvt"]])
    ck["b5"] = tok_kvt

    for p, hh, hsl, off in heads():
        t_tok = P("t", (lambda e, p=p, hh=hh, hsl=hsl, off=off:
                        e.matmul(pbv[2][hsl, p, :], woutT[hsl, p, :],
                                 qkv[hsl, p, ci, 128:192], start=True, stop=True,
                                 skip_group_check=True, tile_position=(off, off))),
                  waits=[ck["wt_evac"], ck["b2"]])
    tok_amm = t_tok

    tok_eqb = P("s", (lambda e: e.activation(E_sb[:], pb[0][:, :],
                                             ACT.Exp, bias=-16.0)),
                waits=[tok_z, ck["eq"]])
    ck["b0"] = tok_eqb
    ck["b1"] = tok_eqb
    tok_sums = P("v", (lambda e:
                       e.reduce_sum(SUMS[:], E_sb.rearrange("p a (b c) -> p (a b) c", c=64),
                                    axis=AX.X)),
                 waits=[tok_eqb])
    tok_rec = P("v", lambda e: e.reciprocal(RECIP[:], SUMS[:]),
                waits=[tok_sums])
    r2 = RECIP.rearrange("p (a b) -> p a b", b=2)
    rq_b = r2[:, :, 0:1].broadcast_to((128, 8, 64))
    rk_b = r2[:, :, 1:2].broadcast_to((128, 8, 64))
    rk_cols = r2[:, :, 1]
    Ek_view = E_sb[:, :, 64:128]
    Eq_view = E_sb[:, :, 0:64]
    tok_cli = P("v", (lambda e:
                      e.tensor_tensor(CLI[:], rk_cols,
                                      lrwdB[:, 0, :, ci], OP.mult)),
                waits=[ck["lrwdB_ready"], tok_rec])
    tok_clo = P("v", (lambda e:
                      e.tensor_tensor(CLO[:], rk_cols,
                                      lrwdB[:, 1, :, ci], OP.mult)))
    cli_b = CLI.rearrange("p (a b) -> p a b", b=1).broadcast_to((128, 8, 64))
    clo_b = CLO.rearrange("p (a b) -> p a b", b=1).broadcast_to((128, 8, 64))
    tok_pq = P("v", (lambda e: e.tensor_tensor(p_q[:], Eq_view, rq_b, OP.mult)),
               waits=[ck["pq"], tok_rec])
    tok_pli = P("v", (lambda e: e.tensor_tensor(p_li[:], Ek_view, cli_b, OP.mult)),
                waits=[ck["pli"], tok_clo])
    tok_plo = P("v", (lambda e: e.tensor_tensor(gzp[:, :, 64:128], Ek_view,
                                                clo_b, OP.mult)),
                waits=[ck["plo"]])
    m2_b = tb["mask2"].ap().rearrange("p (a b) -> p a b", a=1).broadcast_to((128, 8, 2))
    tok_rk2 = P("v", (lambda e: e.tensor_tensor(rk2[:], m2_b,
                                                rk_b[:, :, 0:2], OP.mult)),
                waits=[ck["rk2"]])
    tok_eqfree = tok_rk2

    for p, hh, hsl, off in heads():
        t_tok = P("t", (lambda e, p=p, hh=hh, hsl=hsl, off=off:
                        e.matmul(pbv[3][hsl, p, :], wout[hsl, p, :],
                                 p_q[hsl, p, :], start=True, stop=True,
                                 skip_group_check=True, tile_position=(off, off))),
                  waits=[ck["upd_wout"], ck["b3"], tok_pq])
    tok_omm = t_tok
    ck["pq"] = tok_omm
    tok_oev = P("s", (lambda e, ci=ci:
                      e.activation(V["o_seg"][:, :, ci * 64:(ci + 1) * 64],
                                   pbv[3][:, 0:8, :], ACT.Copy)),
                waits=[tok_omm, ck["oseg"]])
    ck["b3"] = tok_oev
    ck["o_evac"] = tok_oev

    tok_aev = P("s", lambda e: e.activation(a_sb[:], pbv[2][:, 0:8, :], ACT.Copy),
                waits=[tok_amm, ck["a"]])
    tok_m = P("v", (lambda e:
                    e.tensor_tensor(m_sb[:],
                                    E_sb[:, :, 64:128], pbv[2][:, 0:8, :],
                                    OP.mult)),
              waits=[tok_eqb, tok_aev, ck["m"]])
    ck["b2"] = tok_m
    ck["eq"] = tok_eqfree
    ck["ek"] = tok_m
    for p in range(NP):
        t_tok = P("t", (lambda e, p=p:
                        e.matmul(pbv[4][0:2, p, :], rk2[:, p, :], m_sb[:, p, :],
                                 start=True, stop=True, skip_group_check=True)),
                  waits=[tok_m, tok_rk2, ck["b4"]])
    tok_s2mm = t_tok
    ck["rk2"] = tok_s2mm
    ck["m"] = tok_s2mm
    tok_s2ev = P("s", lambda e: e.activation(s2_sb[:], pbv[4][0:2, 0:8, :],
                                             ACT.Copy),
                 waits=[tok_s2mm, ck["s2"]])
    tok_sb = P("t", (lambda e:
                     e.matmul(pb[4][:, :], tb["sel2"][:],
                              s2_sb.rearrange("p a b -> p (a b)"),
                              start=True, stop=True, skip_group_check=True)),
              waits=[tok_s2ev])
    ck["s2"] = tok_sb
    tok_t = P("v", lambda e: e.tensor_tensor(t_sb[:], pb[4].ap()
                                             .rearrange("p (a b) -> p a b", b=64),
                                             a_sb[:], OP.subtract),
              waits=[tok_sb, tok_aev])
    ck["b4"] = tok_t
    ck["a"] = tok_t
    tok_gz = P("v", (lambda e:
                     e.tensor_tensor(gzp[:, :, 0:64],
                                     t_sb[:], p_li[:], OP.mult)),
               waits=[tok_pli, tok_t, ck["gz"]])
    ck["pli"] = tok_gz

    for p in range(NP):
        t_tok = P("t", (lambda e, p=p:
                        e.transpose(pb6w[:, p, :], gzp[:, p, :], ident[:])),
                  waits=[tok_gz, tok_plo, ck["b6"]])
    tok_trgzp = t_tok
    ck["gz"] = tok_trgzp
    ck["plo"] = tok_trgzp
    tok_gzpt = P("s", lambda e: e.activation(gzpT_sb[:], pb6w[:, 0:8, :],
                                             ACT.Copy),
                 waits=[tok_trgzp, ck["gzpt"]])
    ck["b6"] = tok_gzpt

    for p, hh, hsl, off in heads():
        t_tok = P("t", (lambda e, p=p, hh=hh, hsl=hsl, off=off:
                        e.matmul(pbv[0][hsl, p, :],
                                 gzpT_sb[64:128, p, hh * 64:(hh + 1) * 64],
                                 kvT_sb[64:128, p, hh * 64:(hh + 1) * 64],
                                 start=True, stop=True, skip_group_check=True,
                                 tile_position=(64, off))),
                  waits=[tok_gzpt, tok_kvt, ck["b0"]])
    tok_g11 = t_tok
    for p, hh, hsl, off in heads():
        t_tok = P("t", (lambda e, p=p, hh=hh, hsl=hsl, off=off:
                        e.matmul(pbv[1][hsl, p, :],
                                 kvT_sb[0:64, p, hh * 64:(hh + 1) * 64],
                                 gzpT_sb[0:64, p, hh * 64:(hh + 1) * 64],
                                 start=True, stop=True, skip_group_check=True,
                                 tile_position=(0, off))),
                  waits=[ck["b1"]])
    tok_g13 = t_tok
    ck["kvt"] = tok_g13
    ck["gzpt"] = tok_g13

    for p in range(NP):
        tok_ui = P("v", (lambda e, p=p:
                         e.scalar_tensor_tensor(winT[:, p, :], winT[:, p, :],
                                                lrwdB[:, 2, p, ci:ci + 1],
                                                pbv[1][:, p, :], op0=OP.mult,
                                                op1=OP.subtract)),
                   waits=[tok_g13, tok_z, ck["upd_win"], ck["lrwdB_ready"]])
    for p in range(NP):
        tok_uo = P("v", (lambda e, p=p:
                         e.scalar_tensor_tensor(wout[:, p, :], wout[:, p, :],
                                                lrwdB[:, 3, p, ci:ci + 1],
                                                pbv[0][:, p, :], op0=OP.mult,
                                                op1=OP.add)),
                   waits=[tok_g11, tok_omm, ck["upd_wout"]])
    ck["upd_win"] = tok_ui
    ck["upd_wout"] = tok_uo
    ck["b0"] = tok_uo
    ck["b1"] = tok_ui

    for p, hh, hsl, off in heads():
        t_tok = P("t", (lambda e, p=p, hh=hh, hsl=hsl, off=off:
                        e.transpose(pb5WT[hsl, p, :], wout[hsl, p, :],
                                    ident[hsl, hsl], tile_position=(off, off))),
                  waits=[tok_uo, ck["b5"]])
    tok_trw = t_tok
    tok_wt = P("s", lambda e: e.activation(woutT[:], pb5WT[:, 0:8, :], ACT.Copy),
               waits=[tok_trw])
    ck["b5"] = tok_wt
    ck["wt_evac"] = tok_wt


def w2_tile_src_g(dr, c):
    rt, cc = c // 4, c % 4
    return dr["w2"][rt * 128:(rt + 1) * 128, cc * 1024:(cc + 1) * 1024]


def t1_src(dr, i):
    if i < 32:
        return dr["w1"][i * 128:(i + 1) * 128, :]
    return w2_tile_src_g(dr, i - 32)


def emit_t1_prefetch(plan, P, dr, V, ck, n):
    wstream = V["wstream"]
    for _ in range(n):
        i = ck["t1_pf"]
        if i >= 64:
            return
        ck["t1_pf"] += 1
        par = i % 4
        srcap = t1_src(dr, i)
        ck["t1_dma"][par] = P("y", (lambda e, srcap=srcap, par=par:
                                    e.dma_start(wstream[:, par, :], srcap)),
                              waits=[ck["t1_ws"][par], ck["t1_dma"][par]],
                              dma=True)


def emit_t1_compute(plan, P, dr, V, ck, n):
    wstream, tabs = V["wstream"], V["tabs"]
    for _ in range(n):
        i = ck["t1_i"]
        if i >= 64:
            return
        ck["t1_i"] += 1
        par = i % 4
        acc = V["acc1"] if i < 32 else V["acc2"]
        tok_tb = P("v", (lambda e, par=par:
                         e.reduce_sum(tabs[:], wstream[:, par, :], axis=AX.X,
                                      apply_absolute_value=True)),
                   waits=[ck["t1_dma"][par], ck["t1_tabs"]])
        if i % 32 == 0:
            tok_a = P("v", (lambda e, acc=acc: e.tensor_copy(acc[:], tabs[:])),
                      waits=[tok_tb])
        else:
            tok_a = P("v", (lambda e, acc=acc:
                            e.tensor_tensor(acc[:], acc[:], tabs[:], OP.add)),
                      waits=[tok_tb])
        ck["t1_ws"][par] = tok_a
        ck["t1_tabs"] = tok_a
        emit_t1_prefetch(plan, P, dr, V, ck, 1)
    return


def pbj(pb, j):
    if j == 0:
        return pb[0][:, 0:512]
    if j == 1:
        return pb[0][:, 512:1024]
    return pb[2 + (j - 2)][:, :]


def emit_passB(nc, plan, P, dr, dbg_dr, tb, A, A4, bar_toks, L,
               b2bf, b1cols, ident, identf, ck, t1bufs):
    pb = tb["pb"]
    xt = tb["xt"]
    outt = tb["outt"]
    pb5w = pb[5].ap().rearrange("p (a b) -> p a b", b=128)
    pb6w = pb[6].ap().rearrange("p (a b) -> p a b", b=128)
    NBT = L // SEGB
    wstream = t1bufs["wstream"]
    acc1, acc2, tabs = t1bufs["acc1"], t1bufs["acc2"], t1bufs["tabs"]

    for e in Plan.ENGS:
        plan.op(e, lambda _e: None, waits=bar_toks, inc=False)

    w1T = A.alloc("w1T", [128, 8, FF])
    w2T = A.alloc("w2T", [128, 32, 1024])
    A.mark("tern_tmp")
    w12scr = A.alloc("w12scr", [128, 1024])
    sgA = A.alloc("sgA", [128, 1024])
    A.reset_to("tern_tmp")
    hT = A.alloc("hT", [128, 2, SEGB])
    xnB = A.alloc("xnB", [128, 2, 8, SEGB])
    s_rd = A.alloc("s_rd", [128, 2, 1024])
    g_sb = A.alloc("g_sb", [128, 2, 1024])

    tot = A4.alloc("totw", [1, 1])
    inv1 = A4.alloc("invw1", [128, 1])
    inv2 = A4.alloc("invw2", [128, 1])
    ones_col = A4.alloc("ones_col", [128, 1])
    ones_row = A4.alloc("ones_row", [1, 128])


    tok_o1 = P("v", lambda e: e.memset(ones_col[:], 1.0))
    tok_o2 = P("v", lambda e: e.memset(ones_row[:], 1.0))
    tabs_free = ck["t1_tabs"]
    for acc, inv in ((acc1, inv1), (acc2, inv2)):
        t_tok = P("t", (lambda e, acc=acc:
                        e.matmul(pb[7][0:1, 0:1], ones_col[:], acc[:],
                                 start=True, stop=True, skip_group_check=True)),
                  waits=[ck["t1_tabs"], tok_o1])
        tok_cp = P("v", lambda e: e.tensor_copy(tot[0:1, :], pb[7][0:1, 0:1]),
                   waits=[t_tok])
        t_tok = P("t", (lambda e:
                        e.matmul(pb[7][0:128, 1:2], ones_row[:], tot[0:1, :],
                                 start=True, stop=True, skip_group_check=True)),
                  waits=[tok_cp, tok_o2])
        tok_t1 = P("v", (lambda e: e.tensor_scalar(tabs[:], pb[7][0:128, 1:2],
                                                   1.0 / (FF * D), None,
                                                   op0=OP.mult)),
                   waits=[t_tok, tabs_free])
        tok_t2 = P("v", (lambda e: e.tensor_scalar(tabs[:], tabs[:], 1e-8, None,
                                                   op0=OP.max)), waits=[tok_t1])
        tok_i = P("v", (lambda e, inv=inv: e.reciprocal(inv[:], tabs[:])),
                  waits=[tok_t2])
        tabs_free = tok_i

    ws_free = list(ck["t1_ws"])
    ws_dma = list(ck["t1_dma"])
    scr_free = Z0
    tr_banks = [pb5w, pb6w]
    bank_ev = [Z0, Z0]
    tok_trev = Z0
    for wname, wT, inv in (("w1", w1T, inv1), ("w2", w2T, inv2)):
        for c in range(32):
            par = (c + (32 if wname == "w2" else 0)) % 4
            srcap = (dr["w1"][c * 128:(c + 1) * 128, :] if wname == "w1"
                     else w2_tile_src_g(dr, c))
            tok_l = P("y", (lambda e, srcap=srcap, par=par:
                            e.dma_start(wstream[:, par, :], srcap)),
                      waits=[ws_free[par], ws_dma[par]], dma=True)
            ws_dma[par] = tok_l
            tok_sa = P("s", (lambda e, par=par, inv=inv:
                             e.activation(w12scr[:], wstream[:, par, :], ACT.Sign,
                                          bias=-0.5, scale=inv[:])),
                       waits=[tok_l, tok_i, scr_free])
            tok_sb_ = P("s", (lambda e, par=par, inv=inv:
                              e.activation(sgA[:], wstream[:, par, :], ACT.Sign,
                                           bias=0.5, scale=inv[:])))
            ws_free[par] = tok_sb_
            tok_w = P("v", lambda e: e.tensor_tensor(w12scr[:], w12scr[:], sgA[:],
                                                     OP.add),
                      waits=[tok_sb_])
            pbx = tr_banks[par % 2]
            for j in range(8):
                t_tok = P("t", (lambda e, j=j, pbx=pbx:
                                e.transpose(pbx[:, j, :],
                                            w12scr[:, j * 128:(j + 1) * 128],
                                            ident[:])),
                          waits=[tok_w, bank_ev[par % 2]])
            scr_free = t_tok
            if wname == "w1":
                dst = w1T[:, :, c * 128:(c + 1) * 128]
            else:
                rt, cc = c // 4, c % 4
                dst = w2T[:, cc * 8:(cc + 1) * 8, rt * 128:(rt + 1) * 128]
            tok_trev = P("v", (lambda e, dst=dst, pbx=pbx:
                               e.tensor_copy(dst, pbx[:, :, :])),
                         waits=[t_tok])
            bank_ev[par % 2] = tok_trev
    tok_tern = tok_trev
    tok_tern_t = ("t", plan.cnt["t"])

    xt_free = list(ck["xt"])
    xdmaB = list(ck["xdma"])
    h_free = [Z0, Z0]
    bank_free = {i: Z0 for i in range(4)}
    mm1_free = {0: Z0, 1: tabs_free}
    g_free = [Z0, Z0]
    srd_free = Z0
    out_free = [Z0, Z0]
    xnB_free = [Z0, Z0]
    xnB_tok = [Z0, Z0]
    srd_tok = Z0

    def dma_xnB(bt):
        par = bt % 2
        tok = P("y", (lambda e, bt=bt, par=par:
                      e.dma_start(xnB[:, par, :, :],
                                  dr["xn_sp"][:, :, bt * SEGB:(bt + 1) * SEGB]
                                  .rearrange("a p t -> p a t"))),
                waits=[xnB_free[par], xnB_tok[par], tok_tern], dma=True)
        xnB_tok[par] = tok

    dma_xnB(0)
    for bt in range(NBT):
        par = bt % 2
        if bt + 1 < NBT:
            dma_xnB(bt + 1)
        srd_tok = P("y", (lambda e, bt=bt:
                          e.dma_start(s_rd[:, 0:2, :],
                                      dr["s_sp"][bt * SEGB:(bt + 1) * SEGB, :]
                                      .rearrange("(a p) f -> p a f", p=128))),
                    waits=[srd_free, srd_tok, tok_tern], dma=True)
        for tsub in range(2):
            gidx = bt * 2 + tsub
            slot = gidx % 4
            xt_tok = P("y", (lambda e, gidx=gidx, slot=slot:
                             e.dma_start(xt[:, slot, :],
                                         dr["x"][gidx * 128:(gidx + 1) * 128, :])),
                       waits=[xt_free[slot], xdmaB[slot]], dma=True)
            xdmaB[slot] = xt_tok
            if tsub == 0:
                xt_tok0 = xt_tok
            else:
                xt_tok1 = xt_tok
        for f1c in range(32):
            hpar = f1c % 2
            bank = pb[4] if hpar == 0 else pb[7]
            for fc in range(8):
                t_tok = P("t", (lambda e, f1c=f1c, fc=fc, bank=bank, par=par:
                                e.matmul(bank[:, 0:SEGB],
                                         w1T[:, fc, f1c * 128:(f1c + 1) * 128],
                                         xnB[:, par, fc, :], start=(fc == 0),
                                         stop=(fc == 7), skip_group_check=True)),
                          waits=[xnB_tok[par], mm1_free[hpar], tok_tern_t])
            tok_mm1 = t_tok
            if f1c > 0:
                pf1c, phpar = f1c - 1, (f1c - 1) % 2
                for tsub in range(2):
                    for half in range(2):
                        j = tsub * 2 + half
                        t_tok = P("t", (lambda e, pf1c=pf1c, phpar=phpar,
                                        tsub=tsub, half=half, j=j:
                                        e.matmul(pbj(pb, j),
                                                 hT[:, phpar, tsub * 128:(tsub + 1) * 128],
                                                 w2T[:, pf1c, half * 512:(half + 1) * 512],
                                                 start=(pf1c == 0), stop=False,
                                                 skip_group_check=True)),
                                  waits=[tok_gelu_prev, bank_free[j]])
                mm1_free[phpar] = t_tok
                h_free[phpar] = t_tok
            tok_gelu_prev = P("s", (lambda e, hpar=hpar, f1c=f1c, bank=bank:
                                    e.activation(hT[:, hpar, :], bank[:, 0:SEGB],
                                                 ACT.Gelu, scale=0.5,
                                                 bias=b1cols[:, f1c:f1c + 1])),
                              waits=[tok_mm1, h_free[hpar], tok_tern])
        pf1c, phpar = 31, 31 % 2
        for tsub in range(2):
            for half in range(2):
                j = tsub * 2 + half
                t_tok = P("t", (lambda e, pf1c=pf1c, phpar=phpar, tsub=tsub,
                                half=half, j=j:
                                e.matmul(pbj(pb, j),
                                         hT[:, phpar, tsub * 128:(tsub + 1) * 128],
                                         w2T[:, pf1c, half * 512:(half + 1) * 512],
                                         start=False, stop=False,
                                         skip_group_check=True)),
                          waits=[tok_gelu_prev, bank_free[j]])
        mm1_free[phpar] = t_tok
        h_free[phpar] = t_tok
        for tsub in range(2):
            for half in range(2):
                j = tsub * 2 + half
                t_tok = P("t", (lambda e, j=j, half=half:
                                e.matmul(pbj(pb, j), tb["two_row"][:],
                                         b2bf[0:1, half * 512:(half + 1) * 512],
                                         start=False, stop=True,
                                         skip_group_check=True)))
                tok_g = P("s", (lambda e, j=j, tsub=tsub, half=half:
                                e.activation(g_sb[:, tsub, half * 512:(half + 1) * 512],
                                             pbj(pb, j), ACT.Silu, scale=0.5)),
                          waits=[t_tok, g_free[tsub]])
                bank_free[j] = tok_g
            gidx = bt * 2 + tsub
            slot = gidx % 4
            opar = gidx % 2
            xt_tok = xt_tok0 if tsub == 0 else xt_tok1
            tok_c1 = P("v", (lambda e, tsub=tsub, opar=opar:
                             e.tensor_tensor(outt[:, opar, :], s_rd[:, tsub, :],
                                             g_sb[:, tsub, :], OP.mult)),
                       waits=[tok_g, srd_tok, out_free[opar]])
            tok_c2 = P("v", (lambda e, slot=slot, opar=opar:
                             e.tensor_tensor(outt[:, opar, :], outt[:, opar, :],
                                             xt[:, slot, :], OP.add)),
                       waits=[tok_c1, xt_tok])
            g_free[tsub] = tok_c2
            xt_free[slot] = tok_c2
            tok_od = P("g", (lambda e, gidx=gidx, opar=opar:
                             e.dma_start(dr["out"][gidx * 128:(gidx + 1) * 128, :],
                                         outt[:, opar, :])),
                       waits=[tok_c2], dma=True)
            out_free[opar] = tok_od
            if tsub == 1:
                srd_free = tok_c1
        xnB_free[par] = ("t", plan.cnt["t"])


_NC_CACHE = {}


def _get_program(L=4096):
    if L not in _NC_CACHE:
        _NC_CACHE[L] = build_program(L=L)
    return _NC_CACHE[L]


def kernel(L0_attn_x, L0_attn_Wq, L0_attn_Wk, L0_attn_Wv, L0_attn_Wlr,
           L0_attn_Wbeta, L0_attn_Wo, L0_attn_W_in_init, L0_attn_W_out_init,
           L0_ffn_W1, L0_ffn_b1, L0_ffn_W2, L0_ffn_b2):
    x = np.ascontiguousarray(np.asarray(L0_attn_x, dtype=np.float32))
    B, L, _ = x.shape
    base = {
        "wq": np.ascontiguousarray(np.asarray(L0_attn_Wq, np.float32)),
        "wk": np.ascontiguousarray(np.asarray(L0_attn_Wk, np.float32)),
        "wv": np.ascontiguousarray(np.asarray(L0_attn_Wv, np.float32)),
        "wo": np.ascontiguousarray(np.asarray(L0_attn_Wo, np.float32)),
        "wlr": np.ascontiguousarray(np.asarray(L0_attn_Wlr, np.float32)),
        "wbeta": np.ascontiguousarray(np.asarray(L0_attn_Wbeta, np.float32)),
        "winit": np.ascontiguousarray(
            np.asarray(L0_attn_W_in_init, np.float32).reshape(64, 16 * 64)),
        "woutit": np.ascontiguousarray(
            np.asarray(L0_attn_W_out_init, np.float32).reshape(64, 16 * 64)),
        "w1": np.ascontiguousarray(np.asarray(L0_ffn_W1, np.float32)),
        "b1": np.ascontiguousarray(np.asarray(L0_ffn_b1, np.float32)),
        "w2": np.ascontiguousarray(np.asarray(L0_ffn_W2, np.float32)),
        "b2": np.ascontiguousarray(np.asarray(L0_ffn_b2, np.float32)),
    }
    base.update(host_constants())
    nc = _get_program(L=L)
    in_maps = [dict(base, x=x[i]) for i in range(B)]
    res = run_bass_kernel_spmd(nc, in_maps, core_ids=list(range(B)))
    return np.stack([res.results[i]["out"] for i in range(B)], axis=0)


# version 06
# speedup vs baseline: 1.2276x; 1.1228x over previous
import numpy as np
import ml_dtypes
import concourse.bass as bass
import concourse.mybir as mybir
from concourse.bass_utils import run_bass_kernel_spmd
from concourse.bass_isa_rust import ReduceOp

F32 = mybir.dt.float32
BF16 = mybir.dt.bfloat16
AX = mybir.AxisListType
OP = mybir.AluOpType
ACT = mybir.ActivationFunctionType

D = 1024
NCORE = 8
CH = 64
NP = 8
FF = 4096
SEG = 256
SEGB = 256
BASE_LR = 0.001
BASE_WD = 0.9


class Plan:

    NROT = 8
    ENGS = ("t", "v", "s", "g", "y")
    SEMS = tuple(["t", "v", "s", "g"]
                 + [f"dy{i}" for i in range(8)] + [f"dg{i}" for i in range(8)])

    def __init__(self):
        self.ops = {e: [] for e in self.ENGS}
        self.cnt = {sm: 0 for sm in self.SEMS}
        self.dma_i = {"y": 0, "g": 0}
        self._waited = {e: {sm: 0 for sm in self.SEMS} for e in self.ENGS}

    def op(self, eng, fn, waits=(), inc=True, dma=False):
        w = list(waits)
        if dma:
            i = self.dma_i[eng]
            self.dma_i[eng] += 1
            sem_name = f"d{eng}{i % self.NROT}"
            amt = 16
            if self.cnt[sem_name] > 0:
                w.append((sem_name, self.cnt[sem_name]))
        else:
            sem_name = eng
            amt = 1
        tok = None
        if inc:
            self.cnt[sem_name] += amt
            tok = self.cnt[sem_name]
        ww = []
        for (sm, c) in w:
            if c is None or c <= 0:
                continue
            if self._waited[eng][sm] >= c:
                continue
            self._waited[eng][sm] = c
            ww.append((sm, c))
        self.ops[eng].append((fn, tuple(ww), sem_name if inc else None, amt))
        return tok

    def barrier_tokens(self):
        return dict(self.cnt)


def emit_engine(plan, eng, engobj, sems):
    for fn, waits, inc_sem, amt in plan.ops[eng]:
        for (sm, c) in waits:
            engobj.wait_ge(sems[sm], c)
        inst = fn(engobj)
        if inc_sem is not None:
            assert inst is not None, f"op on {eng} must return instruction to inc"
            inst.then_inc(sems[inc_sem], amt)


def host_constants():
    c = {}
    c["c_ident"] = np.eye(128, dtype=ml_dtypes.bfloat16)
    c["c_identf"] = np.eye(128, dtype=np.float32)
    sel2 = np.zeros((2, 128), dtype=ml_dtypes.bfloat16)
    sel2[0, 0:64] = 1.0
    sel2[1, 64:128] = 1.0
    c["c_sel2"] = sel2
    mask2 = np.zeros((128, 2), dtype=ml_dtypes.bfloat16)
    mask2[0:64, 0] = 1.0
    mask2[64:128, 1] = 1.0
    c["c_mask2"] = mask2
    lsc = np.zeros((64, 1), dtype=np.float32)
    lsc[0:32, 0] = BASE_LR / CH
    lsc[32:64, 0] = BASE_WD / CH
    c["c_lrwdscale"] = lsc
    c["c_two"] = np.full((1, 128), 2.0, dtype=ml_dtypes.bfloat16)
    c["c_pool"] = np.tile(np.array([[1e-5, -0.5, 0.5, 1.0 / (FF * D), 1e-8,
                                     -16.0]], dtype=np.float32), (128, 1))
    selbig = np.zeros((64, 32, 128), dtype=ml_dtypes.bfloat16)
    for p in range(NP):
        for q in range(4):
            idx = q * 8 + p
            selbig[q * 16 + 2 * p, idx, 0:64] = 1.0
            selbig[q * 16 + 2 * p + 1, idx, 64:128] = 1.0
    c["c_selbig"] = selbig.reshape(64, 32 * 128)
    return c


def build_program(L=4096, dbg=(), skip_passB=False):
    NSEG = L // SEG
    NCH = SEG // CH
    nc = bass.Bass("TRN2", target_bir_lowering=False)

    dr = {}
    dr["x"] = nc.dram_tensor("x", [L, D], F32, kind="ExternalInput").ap()
    for nm, shp in [("wq", [D, D]), ("wk", [D, D]), ("wv", [D, D]),
                    ("wo", [D, D]), ("wlr", [32, D]), ("wbeta", [32, D]),
                    ("winit", [64, 16 * 64]), ("woutit", [64, 16 * 64]),
                    ("w1", [FF, D]), ("b1", [FF]), ("w2", [D, FF]), ("b2", [D])]:
        dr[nm] = nc.dram_tensor(nm, shp, F32, kind="ExternalInput").ap()
    consts = host_constants()
    for nm, arr in consts.items():
        dt = BF16 if arr.dtype == ml_dtypes.bfloat16 else F32
        dr[nm] = nc.dram_tensor(nm, list(arr.shape), dt, kind="ExternalInput").ap()
    dr["out"] = nc.dram_tensor("out", [L, D], F32, kind="ExternalOutput").ap()
    dr["s_sp"] = nc.dram_tensor("s_sp", [L, D], BF16, kind="Internal").ap()
    dr["xn_sp"] = nc.dram_tensor("xn_sp", [8, 128, L], BF16, kind="Internal").ap()
    dbg_dr = {}
    for nm, shp, dt in dbg:
        dbg_dr[nm] = nc.dram_tensor(nm, shp, dt, kind="ExternalOutput").ap()

    plan = Plan()

    import contextlib
    with contextlib.ExitStack() as st:
        def T(*args, **kw):
            return st.enter_context(nc.sbuf_tensor(*args, **kw))

        def PS(*args, **kw):
            return st.enter_context(nc.psum_tensor(*args, **kw))

        ident = T("ident", [128, 128], BF16)
        identf = T("identf", [128, 128], F32)
        sel2 = T("sel2", [2, 128], BF16)
        mask2 = T("mask2", [128, 2], BF16)
        lrwdscale = T("lrwdscale", [64, 1], F32)
        two_row = T("two_row", [1, 128], BF16)
        cpool = T("cpool", [128, 8], F32)
        nc.const_aps.register(F32, 1e-5, cpool[:, 0:1])
        nc.const_aps.register(F32, -0.5, cpool[:, 1:2])
        nc.const_aps.register(F32, 0.5, cpool[:, 2:3])
        nc.const_aps.register(F32, 1.0 / (FF * D), cpool[:, 3:4])
        nc.const_aps.register(F32, 1e-8, cpool[:, 4:5])
        nc.const_aps.register(F32, -16.0, cpool[:, 5:6])
        xt = T("xt", [128, 4, D], F32)
        outt = T("outt", [128, 2, D], F32)
        abf = T("abf", [128, 76 * 1024], BF16)
        af32 = T("af32", [128, 6 * 1024], F32)
        pb0big = PS("pb0", [128, 1024], F32)
        pb = [pb0big, pb0big,
              PS("pb2", [128, 512], F32), PS("pb3", [128, 512], F32),
              PS("pb4", [128, 512], F32), PS("pb5", [128, 1024], BF16),
              PS("pb6", [128, 1024], BF16), PS("pb7", [128, 512], F32)]
        sems = {nm: st.enter_context(nc.semaphore("sem_" + nm))
                for nm in Plan.SEMS}
        block = st.enter_context(nc.Block())
        tb = dict(ident=ident, identf=identf, sel2=sel2, mask2=mask2,
                  lrwdscale=lrwdscale, two_row=two_row, cpool=cpool,
                  xt=xt, outt=outt, abf=abf, af32=af32, pb=pb)

        emit_all(nc, plan, dr, dbg_dr, tb, L, NSEG, NCH, skip_passB)

        @block.tensor
        def _(t):
            emit_engine(plan, "t", t, sems)

        @block.vector
        def _(v):
            emit_engine(plan, "v", v, sems)

        @block.scalar
        def _(s):
            emit_engine(plan, "s", s, sems)

        @block.gpsimd
        def _(g):
            emit_engine(plan, "g", g, sems)

        @block.sync
        def _(y):
            emit_engine(plan, "y", y, sems)

    return nc


class Arena:

    def __init__(self, tensor, n_elems):
        self.t = tensor
        self.n = n_elems
        self.off = 0
        self.marks = {}
        self.slots = {}

    def alloc(self, name, shape):
        n = int(np.prod(shape[1:]))
        assert self.off + n <= self.n, f"arena overflow at {name}: {self.off}+{n}>{self.n}"
        base = self.off
        self.off += n
        self.slots[name] = (base, shape)
        return self.view(name)

    def view(self, name):
        base, shape = self.slots[name]
        ap = self.t[:shape[0], base:base + int(np.prod(shape[1:]))]
        if len(shape) > 2:
            pat = "p (" + " ".join(f"d{i}" for i in range(1, len(shape))) + ") -> p " + \
                  " ".join(f"d{i}" for i in range(1, len(shape)))
            kw = {f"d{i}": shape[i] for i in range(2, len(shape))}
            ap = ap.rearrange(pat, **kw)
        return ap

    def mark(self, tag):
        self.marks[tag] = self.off

    def reset_to(self, tag):
        self.off = self.marks[tag]



Z0 = ("v", 0)


DEBUG_ARENAS = {}


def emit_all(nc, plan, dr, dbg_dr, tb, L, NSEG, NCH, skip_passB=False):
    def P(eng, fn, waits=(), dma=False):
        if dma:
            sem_name = f"d{eng}{plan.dma_i[eng] % plan.NROT}"
            tok = plan.op(eng, fn, waits=waits, inc=True, dma=True)
            return (sem_name, tok)
        tok = plan.op(eng, fn, waits=waits, inc=True, dma=False)
        return (eng, tok)

    pb = tb["pb"]
    A = Arena(tb["abf"], 76 * 1024)
    A4 = Arena(tb["af32"], 6 * 1024)
    DEBUG_ARENAS["A"] = A
    DEBUG_ARENAS["A4"] = A4
    ident = tb["ident"]
    identf = tb["identf"]

    b2bf = A.alloc("b2bf", [1, 1024])
    A.mark("postperm")
    selbig = A.alloc("selbig", [64, 32, 128])
    wqT = A.alloc("wqT", [128, 8, 1024])
    wkT = A.alloc("wkT", [128, 8, 1024])
    wvT = A.alloc("wvT", [128, 8, 1024])
    woT = A.alloc("woT", [128, 8, 1024])
    wlrbT = A.alloc("wlrbT", [128, 8, 64])
    winT = A.alloc("winT", [128, 8, 64])
    wout = A.alloc("wout", [128, 8, 64])
    woutT = A.alloc("woutT", [128, 8, 64])
    A.mark("postAW")
    wnat = A.alloc("wnat", [128, 8, 1024])
    winit_nat = A.alloc("winit_nat", [64, 1024])
    woutit_nat = A.alloc("woutit_nat", [64, 1024])
    wlrb_nat = A.alloc("wlrb_nat", [64, 1024])
    A.reset_to("postAW")

    b1nat = A4.alloc("b1nat", [32, 128])
    b1cols = A4.alloc("b1cols", [128, 32])
    lrwd_f = A4.alloc("lrwd_f", [64, 2, 256])
    lrwdc = A4.alloc("lrwdc", [64, 2, 4])
    lrwdB = A4.alloc("lrwdB", [128, 2, 4, 8, 4])
    SUMS = A4.alloc("SUMS", [128, 16])
    RECIP = A4.alloc("RECIP", [128, 16])
    CLI = A4.alloc("CLI", [128, 8])
    CLO = A4.alloc("CLO", [128, 8])
    nmq = A4.alloc("nmq", [128, 2])
    bnst = A4.alloc("bnst", [128, 12])
    mv = A4.alloc("mv", [128, 2])
    rstd = A4.alloc("rstd", [128, 1])
    lntmp = A4.alloc("lntmp", [128, 1])
    xt = tb["xt"]
    outt = tb["outt"]

    pb5w = pb[5].ap().rearrange("p (a b) -> p a b", b=128)
    pb5v = pb[5].ap().rearrange("p (a b) -> p a b", b=64)
    pb6w = pb[6].ap().rearrange("p (a b) -> p a b", b=128)
    pb6v = pb[6].ap().rearrange("p (a b) -> p a b", b=64)
    pb7v = pb[7].ap().rearrange("p (a b) -> p a b", b=64)

    dstmap = {"c_ident": ident, "c_identf": identf, "c_sel2": tb["sel2"],
              "c_mask2": tb["mask2"], "c_lrwdscale": tb["lrwdscale"],
              "c_two": tb["two_row"]}
    cy_toks = []
    for nm, dst in dstmap.items():
        cy_toks.append(P("y", lambda e, dst=dst, nm=nm: e.dma_start(dst[:], dr[nm]),
                         dma=True))
    cy_toks.append(P("y", lambda e: e.dma_start(tb["cpool"][:, 0:6], dr["c_pool"]),
                    dma=True))
    cy_toks.append(P("y", lambda e: e.dma_start(selbig.rearrange("p a b -> p (a b)"),
                                                dr["c_selbig"]), dma=True))
    cy_toks.append(P("y", lambda e: e.dma_start(b1nat[:],
                                                dr["b1"].rearrange("(a b) -> a b", b=128)),
                    dma=True))
    cg_toks2 = [
        P("g", lambda e: e.dma_start(winit_nat[:], dr["winit"]), dma=True),
        P("g", lambda e: e.dma_start(woutit_nat[:], dr["woutit"]), dma=True),
        P("g", lambda e: e.dma_start(wlrb_nat[0:32, :], dr["wlr"]), dma=True),
        P("g", lambda e: e.dma_start(wlrb_nat[32:64, :], dr["wbeta"]), dma=True),
    ]
    cg_toks = [P("g", lambda e: e.dma_start(b2bf[:],
                                            dr["b2"].rearrange("(a b) -> a b", a=1)),
                dma=True)]

    t_tok = P("t", lambda e: e.transpose(pb[7][0:128, 0:32], b1nat[:],
                                         identf[0:32, 0:32]),
              waits=cy_toks)
    tok_b1 = P("v", lambda e: e.tensor_copy(b1cols[:], pb[7][0:128, 0:32]),
               waits=[t_tok])

    for h in range(16):
        p_, hh = h // 2, h % 2
        sl = slice(h * 64, (h + 1) * 64)
        hsl = slice(hh * 64, (hh + 1) * 64)
        t_tok = P("t", (lambda e, sl=sl, p_=p_, hh=hh, hsl=hsl:
                        e.transpose(pb5v[hsl, p_, :], winit_nat[:, sl],
                                    ident[0:64, 0:64], tile_position=(0, 64 * hh))),
                  waits=cg_toks + cg_toks2 + cy_toks + [tok_b1])
        t_tok = P("t", (lambda e, sl=sl, p_=p_, hh=hh, hsl=hsl:
                        e.transpose(pb5v[hsl, 8 + p_, :], woutit_nat[:, sl],
                                    ident[0:64, 0:64], tile_position=(0, 64 * hh))))
        t_tok = P("t", (lambda e, sl=sl, p_=p_, hh=hh, hsl=hsl:
                        e.matmul(pb7v[hsl, p_, :], ident[0:64, 0:64],
                                 woutit_nat[:, sl], start=True, stop=True,
                                 skip_group_check=True, tile_position=(0, 64 * hh))))
    P("v", lambda e: e.tensor_copy(winT[:], pb5v[:, 0:8, :]), waits=[t_tok])
    P("v", lambda e: e.tensor_copy(woutT[:], pb5v[:, 8:16, :]))
    tok_state0 = P("v", lambda e: e.tensor_copy(wout[:], pb7v[:, 0:8, :]))

    for fc in range(8):
        t_tok = P("t", (lambda e, fc=fc:
                        e.transpose(pb6v[0:128, 8 + fc, :],
                                    wlrb_nat[:, fc * 128:(fc + 1) * 128],
                                    ident[0:64, 0:64])),
                  waits=[tok_state0])
    tok_evac = P("v", lambda e: e.tensor_copy(wlrbT[:], pb6v[:, 8:16, :]),
                 waits=[t_tok])

    tok_wload = Z0
    for wname, wT in (("wq", wqT), ("wk", wkT), ("wv", wvT), ("wo", woT)):
        wnat_free = ("t", plan.cnt["t"])
        tok_wload = P("g", (lambda e, wname=wname:
                            e.dma_start(wnat[:],
                                        dr[wname].rearrange("(c p) f -> p c f", p=128))),
                      waits=[wnat_free, tok_wload], dma=True)
        for fc in range(8):
            for c in range(8):
                t_tok = P("t", (lambda e, fc=fc, c=c:
                                e.transpose(pb5w[:, c, :],
                                            wnat[:, c, fc * 128:(fc + 1) * 128],
                                            ident[:])),
                          waits=[tok_wload, tok_evac])
            tok_evac = P("v", (lambda e, wT=wT, fc=fc:
                               e.tensor_copy(wT[:, fc, :],
                                             pb5w.rearrange("p a b -> p (a b)"))),
                         waits=[t_tok])

    qkv = A.alloc("qkv", [128, 2, 8, NCH, 192])
    xn_seg = A.alloc("xn_seg", [128, 2, 8, SEG])
    o_seg = A.alloc("o_seg", [128, 2, 8, SEG])
    s_seg = A.alloc("s_seg", [128, 2, 2, 1024])
    E_sb = A.alloc("E_sb", [128, 8, 128])
    p_q = A.alloc("p_q", [128, 8, 64])
    p_li = A.alloc("p_li", [128, 8, 64])
    m_sb = A.alloc("m_sb", [128, 8, 64])
    a_sb = A.alloc("a_sb", [128, 8, 64])
    t_sb = A.alloc("t_sb", [128, 8, 64])
    gzp = A.alloc("gzp", [128, 8, 128])
    s2_sb = A.alloc("s2_sb", [2, 8, 64])
    rk2 = A.alloc("rk2", [128, 8, 2])
    kvT_sb = A.alloc("kvT_sb", [128, 8, 128])
    gzpT_sb = A.alloc("gzpT_sb", [128, 8, 128])
    xn_tok = A.alloc("xn_tok", [128, 2, 1024])
    lrwdc_bf = A.alloc("lrwdc_bf", [64, 2, 4])
    A.mark("passA_end")

    z16 = pb[0].ap().rearrange("p (a b) -> p a b", b=64)
    pbv = [z16[:, 0:8, :], z16[:, 8:16, :],
           pb[2].ap().rearrange("p (a b) -> p a b", b=64),
           pb[3].ap().rearrange("p (a b) -> p a b", b=64),
           pb[4].ap().rearrange("p (a b) -> p a b", b=64)]
    views = dict(qkv=qkv, xn_seg=xn_seg, o_seg=o_seg, s_seg=s_seg, E_sb=E_sb,
                 p_q=p_q, p_li=p_li, m_sb=m_sb, a_sb=a_sb, t_sb=t_sb,
                 gzp=gzp, s2_sb=s2_sb, rk2=rk2, kvT_sb=kvT_sb,
                 gzpT_sb=gzpT_sb, winT=winT, wout=wout, woutT=woutT,
                 SUMS=SUMS, RECIP=RECIP, CLI=CLI, CLO=CLO, nmq=nmq,
                 lrwdB=lrwdB, pbv=pbv, pb5w=pb5w, pb6w=pb6w,
                 wqT=wqT, wkT=wkT, wvT=wvT, woT=woT, wlrbT=wlrbT,
                 lrwd_f=lrwd_f, lrwdc=lrwdc, bnst=bnst, mv=mv, rstd=rstd,
                 lntmp=lntmp, xn_tok=xn_tok, selbig=selbig,
                 lrwdc_bf=lrwdc_bf)

    wstream = A4.alloc("wstream", [128, 4, 1024])
    acc1 = A4.alloc("accw1", [128, 1])
    acc2 = A4.alloc("accw2", [128, 1])
    tabs = A4.alloc("tabsw", [128, 1])
    t1bufs = dict(wstream=wstream, acc1=acc1, acc2=acc2, tabs=tabs)

    ck = {
        "t1_i": 0, "t1_pf": 0, "t1_ws": [Z0, Z0, Z0, Z0],
        "t1_dma": [Z0, Z0, Z0, Z0], "t1_tabs": Z0, "xdma": [Z0, Z0, Z0, Z0],
        "upd_win": tok_state0, "upd_wout": tok_state0, "wt_evac": tok_state0,
        "b0": Z0, "b1": Z0, "b2": Z0, "b3": tok_b1, "b4": Z0,
        "b5": tok_evac, "b6": tok_evac, "b7": tok_b1,
        "eq": Z0, "ek": Z0, "pq": Z0, "pli": Z0, "plo": Z0, "m": Z0, "a": Z0,
        "gz": Z0, "rk2": Z0, "s2": Z0, "kvt": Z0, "gzpt": Z0,
        "nm": Z0, "oseg": Z0, "xt": [Z0, Z0, Z0, Z0], "xntok": [Z0, Z0],
        "mv": Z0, "ln_ev": Z0, "proj_last": ("s", 0), "lrwd_lc": Z0,
        "proj_pp": 0,
        "lrwdc_r": Z0, "o_evac": ("s", 0), "qkv_ready": ("s", 0),
        "lrwdB_ready": Z0,
        "lrwdB_ready_p": [Z0, Z0], "lrwdB_r_p": [Z0, Z0],
        "oseg_p": [Z0, Z0], "xnseg_p": [Z0, Z0], "qkv_w": [Z0, Z0],
        "sseg_p": [Z0, Z0], "lrwdf_p": [Z0, Z0], "xnsp_dma_p": [Z0, Z0],
    }

    views.update(t1bufs)
    for stage in range(NSEG + 1):
        emit_stage(plan, P, dr, tb, views, ck, stage, NSEG, NCH)

    barA = plan.barrier_tokens()
    bar_toks = [(sm, c) for sm, c in barA.items() if c > 0]
    if not skip_passB:
        A.reset_to("postperm")
        emit_passB(nc, plan, P, dr, dbg_dr, tb, A, A4, bar_toks, L,
                   b2bf, b1cols, ident, identf, ck, t1bufs)

    fin = [(sm, plan.cnt[sm]) for sm in Plan.SEMS
           if sm.startswith("d") and plan.cnt[sm] > 0]
    plan.op("y", lambda e: None, waits=fin, inc=False)


def emit_stage(plan, P, dr, tb, V, ck, stage, NSEG, NCH):
    pb = tb["pb"]
    pb5w = V["pb5w"]
    ident = tb["ident"]
    xt = tb["xt"]

    fillers = []
    if stage < NSEG:
        seg = stage
        par = seg % 2
        qkv = V["qkv"][:, par]
        xn_seg = V["xn_seg"][:, par]

        for tsub in range(2):
            gidx = seg * 2 + tsub
            slot = gidx % 4
            ck["xdma"][slot] = P("y", (lambda e, gidx=gidx, slot=slot:
                                       e.dma_start(xt[:, slot, :],
                                                   dr["x"][gidx * 128:(gidx + 1) * 128, :])),
                                 waits=[ck["xt"][slot], ck["xdma"][slot]],
                                 dma=True)
        if stage == 0:
            emit_t1_prefetch(plan, P, dr, V, ck, 4)

        def ln_sub(tsub, seg=seg, par=par, xn_seg=xn_seg):
            gidx = seg * 2 + tsub
            xpar = gidx % 4
            tok_x = ck["xdma"][xpar]
            tok_bn = P("v", (lambda e, xpar=xpar:
                             e.bn_stats(V["bnst"][:, 0:6], xt[:, xpar, 0:512])),
                       waits=[tok_x, ck["mv"]])
            tok_bn = P("v", (lambda e, xpar=xpar:
                             e.bn_stats(V["bnst"][:, 6:12], xt[:, xpar, 512:1024])))
            tok_mv = P("v", lambda e: e.bn_aggr(V["mv"][:], V["bnst"][:, 0:12]),
                       waits=[tok_bn])
            tok_ln = P("s", (lambda e: e.activation(V["lntmp"][:], V["mv"][:, 1:2],
                                                    ACT.Ln, bias=1e-5)),
                      waits=[tok_mv])
            tok_rstd = P("s", (lambda e: e.activation(V["rstd"][:], V["lntmp"][:],
                                                      ACT.Exp, scale=-0.5)),
                         waits=[tok_ln])
            ck["mv"] = tok_rstd
            npar = gidx % 2
            tok_xntok = P("v", (lambda e, xpar=xpar, npar=npar:
                                e.tensor_scalar(V["xn_tok"][:, npar, :],
                                                xt[:, xpar, :],
                                                V["mv"][:, 0:1], V["rstd"][:],
                                                op0=OP.subtract, op1=OP.mult)),
                          waits=[tok_rstd, tok_mv, ck["xntok"][npar]])
            ck["xt"][xpar] = tok_xntok
            for fc in range(8):
                t_tok = P("t", (lambda e, npar=npar, fc=fc:
                                e.transpose(pb5w[:, fc, :],
                                            V["xn_tok"][:, npar, fc * 128:(fc + 1) * 128],
                                            ident[:])),
                          waits=[tok_xntok, ck["b5"]])
            ck["xntok"][npar] = t_tok
            tok_ev = P("v", (lambda e, tsub=tsub, xn_seg=xn_seg:
                             e.tensor_copy(xn_seg[:, :, tsub * 128:(tsub + 1) * 128],
                                           pb5w[:, :, :])),
                       waits=[t_tok, ck["xnseg_p"][par], ck["xnsp_dma_p"][par]])
            ck["b5"] = tok_ev
            ck["ln_ev"] = tok_ev

        fillers.append(lambda: ln_sub(0))
        fillers.append(lambda: ln_sub(1))

        def proj_tile(wTname, m, coff, actf, seg=seg, par=par, qkv=qkv,
                      xn_seg=xn_seg):
            wT = V[wTname]
            bsel = ck["proj_pp"] % 2
            ck["proj_pp"] += 1
            bank = pb[3] if bsel == 0 else pb[7]
            bkey = "b3" if bsel == 0 else "b7"
            for fc in range(8):
                t_tok = P("t", (lambda e, wT=wT, m=m, fc=fc, bank=bank:
                                e.matmul(bank[:, 0:SEG],
                                         wT[:, fc, m * 128:(m + 1) * 128],
                                         xn_seg[:, fc, :],
                                         start=(fc == 0), stop=(fc == 7),
                                         skip_group_check=True)),
                          waits=[ck["ln_ev"], ck[bkey]])
            bview = bank.ap().rearrange("p (a b) -> p a b", b=64)
            tok_evac = P("s", (lambda e, bview=bview, m=m, coff=coff, actf=actf,
                               qkv=qkv:
                               e.activation(qkv[:, m, :, coff:coff + 64],
                                            bview[:, 0:NCH, :], actf)),
                         waits=[t_tok, ck["qkv_w"][par]])
            ck[bkey] = tok_evac
            ck["proj_last"] = tok_evac

        for wTname, coff, actf in (("wqT", 0, ACT.Silu), ("wkT", 64, ACT.Silu),
                                   ("wvT", 128, ACT.Copy)):
            for m in range(8):
                fillers.append(lambda wTname=wTname, m=m, coff=coff, actf=actf:
                               proj_tile(wTname, m, coff, actf))

        def lrwd_proj(seg=seg, par=par, xn_seg=xn_seg):
            for fc in range(8):
                t_tok = P("t", (lambda e, fc=fc:
                                e.matmul(pb[3][0:64, 0:SEG], V["wlrbT"][:, fc, :],
                                         xn_seg[:, fc, :], start=(fc == 0),
                                         stop=(fc == 7), skip_group_check=True)),
                          waits=[ck["ln_ev"], ck["b3"]])
            tok_sg = P("s", (lambda e, par=par:
                             e.activation(V["lrwd_f"][:, par, :], pb[3][0:64, 0:SEG],
                                          ACT.Sigmoid)),
                       waits=[t_tok, ck["lrwdf_p"][par]])
            ck["b3"] = tok_sg
            tok_rd = P("v", (lambda e, par=par:
                             e.reduce_sum(V["lrwdc"][:, par, :],
                                          V["lrwd_f"][:, par]
                                          .rearrange("p (a b) -> p a b", b=64),
                                          axis=AX.X)),
                      waits=[tok_sg, ck["lrwdc_r"]])
            ck["lrwdf_p"][par] = tok_rd
            tok_lc = P("v", (lambda e, par=par:
                             e.tensor_scalar(V["lrwdc_bf"][:, par, :],
                                             V["lrwdc"][:, par, :],
                                             tb["lrwdscale"][:], None,
                                             op0=OP.mult)), waits=[tok_rd])
            ck["lrwd_lc"] = tok_lc
            ck["xnseg_p"][par] = ("t", plan.cnt["t"])

        def lrwd_bcast(seg=seg, par=par):
            pb3b = pb[3].ap().rearrange("p (a b) -> p a b", b=4)
            for idx in range(32):
                t_tok = P("t", (lambda e, idx=idx, par=par:
                                e.matmul(pb3b[:, idx, :],
                                         V["selbig"][:, idx, :],
                                         V["lrwdc_bf"][:, par, :], start=True,
                                         stop=True, skip_group_check=True)),
                          waits=[ck["lrwd_lc"], ck["b3"]])
            tok_lB = P("v", (lambda e, par=par:
                             e.tensor_copy(V["lrwdB"][:, par]
                                           .rearrange("p a b c -> p (a b c)"),
                                           pb[3][:, 0:128])),
                       waits=[t_tok, ck["lrwdB_r_p"][par]])
            ck["b3"] = tok_lB
            ck["lrwdB_ready_p"][par] = tok_lB
            ck["lrwdc_r"] = ck["lrwd_lc"]

        fillers.append(lrwd_proj)
        fillers.append(lrwd_bcast)

        def xn_spill(seg=seg, par=par, xn_seg=xn_seg):
            tok = P("g", (lambda e, seg=seg, xn_seg=xn_seg:
                          e.dma_start(dr["xn_sp"][:, :, seg * SEG:(seg + 1) * SEG]
                                      .rearrange("a p t -> p a t"),
                                      xn_seg[:])),
                    waits=[ck["ln_ev"]], dma=True)
            ck["xnsp_dma_p"][par] = tok

        fillers.append(xn_spill)
        n_t1 = (64 + NSEG - 1) // NSEG
        for _ in range(n_t1):
            fillers.append(lambda: emit_t1_compute(plan, P, dr, V, ck, 1))

    if stage == 0:
        for f in fillers:
            f()
        ck["qkv_ready"] = ck["proj_last"]
        return

    seg = stage - 1
    par = seg % 2
    Vp = dict(V)
    Vp["qkv"] = V["qkv"][:, par]
    Vp["o_seg"] = V["o_seg"][:, par]
    Vp["lrwdB"] = V["lrwdB"][:, par]
    ck["lrwdB_ready"] = ck["lrwdB_ready_p"][par]
    ck["oseg"] = ck["oseg_p"][par]
    nf = len(fillers)
    npoints = 2 * NCH
    per = max(1, (nf + npoints - 1) // npoints) if nf else 0
    fstate = {"i": 0}

    def filler_cb(point):
        for _ in range(per):
            if fstate["i"] < nf:
                fillers[fstate["i"]]()
                fstate["i"] += 1

    for ci in range(NCH):
        emit_scan_chunk(plan, P, tb, Vp, ck, ci, filler_cb)
    while fstate["i"] < nf:
        fillers[fstate["i"]]()
        fstate["i"] += 1
    if stage < NSEG:
        ck["qkv_ready"] = ck["proj_last"]
    ck["qkv_w"][par] = ("t", plan.cnt["t"])
    ck["lrwdB_r_p"][par] = ck["upd_win"]

    o_seg = V["o_seg"][:, par]
    s_seg = V["s_seg"][:, par]
    tok_sev = ("s", 0)
    for tsub in range(2):
        for half in range(2):
            for fc in range(8):
                t_tok = P("t", (lambda e, tsub=tsub, half=half, fc=fc,
                                o_seg=o_seg:
                                e.matmul(pb[3][:, :],
                                         o_seg[:, fc, tsub * 128:(tsub + 1) * 128],
                                         V["woT"][:, fc, half * 512:(half + 1) * 512],
                                         start=(fc == 0), stop=(fc == 7),
                                         skip_group_check=True)),
                          waits=[ck["o_evac"], ck["b3"]])
            tok_sev = P("s", (lambda e, tsub=tsub, half=half, s_seg=s_seg:
                              e.activation(s_seg[:, tsub, half * 512:(half + 1) * 512],
                                           pb[3][:, :], ACT.Copy)),
                        waits=[t_tok, ck["sseg_p"][par]])
            ck["b3"] = tok_sev
    ck["oseg_p"][par] = ("t", plan.cnt["t"])
    tok_sdma = P("g", (lambda e, seg=seg, s_seg=s_seg:
                       e.dma_start(dr["s_sp"][seg * SEG:(seg + 1) * SEG, :]
                                   .rearrange("(a p) f -> p a f", p=128),
                                   s_seg[:])),
                 waits=[tok_sev], dma=True)
    ck["sseg_p"][par] = tok_sdma


def emit_scan_chunk(plan, P, tb, V, ck, ci, filler_cb=None):
    pb = tb["pb"]
    pbv = V["pbv"]
    pb5w, pb6w = V["pb5w"], V["pb6w"]
    ident = tb["ident"]
    qkv, winT, wout, woutT = V["qkv"], V["winT"], V["wout"], V["woutT"]
    E_sb, p_q, p_li = V["E_sb"], V["p_q"], V["p_li"]
    m_sb, a_sb, t_sb, gzp = V["m_sb"], V["a_sb"], V["t_sb"], V["gzp"]
    s2_sb, rk2 = V["s2_sb"], V["rk2"]
    kvT_sb, gzpT_sb = V["kvT_sb"], V["gzpT_sb"]
    SUMS, RECIP, CLI, CLO = V["SUMS"], V["RECIP"], V["CLI"], V["CLO"]
    nmq, lrwdB = V["nmq"], V["lrwdB"]
    pb5WT = pb[5].ap().rearrange("p (a b) -> p a b", b=64)
    pbz = pb[0].ap().rearrange("p (a b) -> p a b", b=128)

    def heads():
        for p in range(NP):
            for hh in range(2):
                yield p, hh, slice(hh * 64, (hh + 1) * 64), (0 if hh == 0 else 64)

    for p, hh, hsl, off in heads():
        t_tok = P("t", (lambda e, p=p, hh=hh, hsl=hsl, off=off:
                        e.matmul(pbz[hsl, p, :], winT[hsl, p, :],
                                 qkv[hsl, p, ci, 0:128], start=True, stop=True,
                                 skip_group_check=True, tile_position=(off, off))),
                  waits=[ck["upd_win"], ck["qkv_ready"],
                         ck["b0"] if p < 4 else ck["b1"]])
    tok_z = t_tok

    for p in range(NP):
        t_tok = P("t", (lambda e, p=p:
                        e.transpose(pb5w[:, p, :], qkv[:, p, ci, 64:192],
                                    ident[:])),
                  waits=[ck["b5"]])
    tok_trkv = t_tok
    tok_kvt = P("v", lambda e: e.tensor_copy(kvT_sb[:], pb5w[:, 0:8, :]),
                waits=[tok_trkv, ck["kvt"]])
    ck["b5"] = tok_kvt

    if filler_cb:
        filler_cb(1)
    for p, hh, hsl, off in heads():
        t_tok = P("t", (lambda e, p=p, hh=hh, hsl=hsl, off=off:
                        e.matmul(pbv[2][hsl, p, :], woutT[hsl, p, :],
                                 qkv[hsl, p, ci, 128:192], start=True, stop=True,
                                 skip_group_check=True, tile_position=(off, off))),
                  waits=[ck["wt_evac"], ck["b2"]])
    tok_amm = t_tok

    tok_eqb = P("s", (lambda e: e.activation(E_sb[:], pb[0][:, :],
                                             ACT.Exp, bias=-16.0)),
                waits=[tok_z, ck["eq"]])
    ck["b0"] = tok_eqb
    ck["b1"] = tok_eqb
    tok_sums = P("v", (lambda e:
                       e.reduce_sum(SUMS[:], E_sb.rearrange("p a (b c) -> p (a b) c", c=64),
                                    axis=AX.X)),
                 waits=[tok_eqb])
    tok_rec = P("v", lambda e: e.reciprocal(RECIP[:], SUMS[:]),
                waits=[tok_sums])
    r2 = RECIP.rearrange("p (a b) -> p a b", b=2)
    rq_b = r2[:, :, 0:1].broadcast_to((128, 8, 64))
    rk_b = r2[:, :, 1:2].broadcast_to((128, 8, 64))
    rk_cols = r2[:, :, 1]
    Ek_view = E_sb[:, :, 64:128]
    Eq_view = E_sb[:, :, 0:64]
    tok_cli = P("v", (lambda e:
                      e.tensor_tensor(CLI[:], rk_cols,
                                      lrwdB[:, 0, :, ci], OP.mult)),
                waits=[ck["lrwdB_ready"], tok_rec])
    tok_clo = P("v", (lambda e:
                      e.tensor_tensor(CLO[:], rk_cols,
                                      lrwdB[:, 1, :, ci], OP.mult)))
    cli_b = CLI.rearrange("p (a b) -> p a b", b=1).broadcast_to((128, 8, 64))
    clo_b = CLO.rearrange("p (a b) -> p a b", b=1).broadcast_to((128, 8, 64))
    tok_pq = P("v", (lambda e: e.tensor_tensor(p_q[:], Eq_view, rq_b, OP.mult)),
               waits=[ck["pq"], tok_rec])
    tok_pli = P("v", (lambda e: e.tensor_tensor(p_li[:], Ek_view, cli_b, OP.mult)),
                waits=[ck["pli"], tok_clo])
    tok_plo = P("v", (lambda e: e.tensor_tensor(gzp[:, :, 64:128], Ek_view,
                                                clo_b, OP.mult)),
                waits=[ck["plo"]])
    m2_b = tb["mask2"].ap().rearrange("p (a b) -> p a b", a=1).broadcast_to((128, 8, 2))
    tok_rk2 = P("v", (lambda e: e.tensor_tensor(rk2[:], m2_b,
                                                rk_b[:, :, 0:2], OP.mult)),
                waits=[ck["rk2"]])
    tok_eqfree = tok_rk2

    tok_aev = P("s", lambda e: e.activation(a_sb[:], pbv[2][:, 0:8, :], ACT.Copy),
                waits=[tok_amm, ck["a"]])
    tok_m = P("v", (lambda e:
                    e.tensor_tensor(m_sb[:],
                                    E_sb[:, :, 64:128], pbv[2][:, 0:8, :],
                                    OP.mult)),
              waits=[tok_eqb, tok_aev, ck["m"]])
    ck["eq"] = tok_eqfree
    ck["ek"] = tok_m

    for p, hh, hsl, off in heads():
        t_tok = P("t", (lambda e, p=p, hh=hh, hsl=hsl, off=off:
                        e.matmul(pbv[2][hsl, p, :], wout[hsl, p, :],
                                 p_q[hsl, p, :], start=True, stop=True,
                                 skip_group_check=True, tile_position=(off, off))),
                  waits=[ck["upd_wout"], tok_m, tok_aev, tok_pq])
    tok_omm = t_tok
    ck["pq"] = tok_omm
    tok_oev = P("s", (lambda e, ci=ci:
                      e.activation(V["o_seg"][:, :, ci * 64:(ci + 1) * 64],
                                   pbv[2][:, 0:8, :], ACT.Copy)),
                waits=[tok_omm, ck["oseg"]])
    ck["b2"] = tok_oev
    ck["o_evac"] = tok_oev
    for p in range(NP):
        t_tok = P("t", (lambda e, p=p:
                        e.matmul(pbv[4][0:2, p, :], rk2[:, p, :], m_sb[:, p, :],
                                 start=True, stop=True, skip_group_check=True)),
                  waits=[tok_m, tok_rk2, ck["b4"]])
    tok_s2mm = t_tok
    ck["rk2"] = tok_s2mm
    ck["m"] = tok_s2mm
    tok_s2ev = P("s", lambda e: e.activation(s2_sb[:], pbv[4][0:2, 0:8, :],
                                             ACT.Copy),
                 waits=[tok_s2mm, ck["s2"]])
    tok_sb = P("t", (lambda e:
                     e.matmul(pb[4][:, :], tb["sel2"][:],
                              s2_sb.rearrange("p a b -> p (a b)"),
                              start=True, stop=True, skip_group_check=True)),
              waits=[tok_s2ev])
    ck["s2"] = tok_sb
    tok_t = P("v", lambda e: e.tensor_tensor(t_sb[:], pb[4].ap()
                                             .rearrange("p (a b) -> p a b", b=64),
                                             a_sb[:], OP.subtract),
              waits=[tok_sb, tok_aev])
    ck["b4"] = tok_t
    ck["a"] = tok_t
    tok_gz = P("v", (lambda e:
                     e.tensor_tensor(gzp[:, :, 0:64],
                                     t_sb[:], p_li[:], OP.mult)),
               waits=[tok_pli, tok_t, ck["gz"]])
    ck["pli"] = tok_gz

    for p in range(NP):
        t_tok = P("t", (lambda e, p=p:
                        e.transpose(pb6w[:, p, :], gzp[:, p, :], ident[:])),
                  waits=[tok_gz, tok_plo, ck["b6"]])
    tok_trgzp = t_tok
    ck["gz"] = tok_trgzp
    ck["plo"] = tok_trgzp
    tok_gzpt = P("s", lambda e: e.activation(gzpT_sb[:], pb6w[:, 0:8, :],
                                             ACT.Copy),
                 waits=[tok_trgzp, ck["gzpt"]])
    ck["b6"] = tok_gzpt

    for p, hh, hsl, off in heads():
        t_tok = P("t", (lambda e, p=p, hh=hh, hsl=hsl, off=off:
                        e.matmul(pbv[0][hsl, p, :],
                                 gzpT_sb[64:128, p, hh * 64:(hh + 1) * 64],
                                 kvT_sb[64:128, p, hh * 64:(hh + 1) * 64],
                                 start=True, stop=True, skip_group_check=True,
                                 tile_position=(64, off))),
                  waits=[tok_gzpt, tok_kvt, ck["b0"]])
    tok_g11 = t_tok
    for p, hh, hsl, off in heads():
        t_tok = P("t", (lambda e, p=p, hh=hh, hsl=hsl, off=off:
                        e.matmul(pbv[1][hsl, p, :],
                                 kvT_sb[0:64, p, hh * 64:(hh + 1) * 64],
                                 gzpT_sb[0:64, p, hh * 64:(hh + 1) * 64],
                                 start=True, stop=True, skip_group_check=True,
                                 tile_position=(0, off))),
                  waits=[ck["b1"]])
    tok_g13 = t_tok
    ck["kvt"] = tok_g13
    ck["gzpt"] = tok_g13
    if filler_cb:
        filler_cb(2)

    for p in range(NP):
        tok_ui = P("v", (lambda e, p=p:
                         e.scalar_tensor_tensor(winT[:, p, :], winT[:, p, :],
                                                lrwdB[:, 2, p, ci:ci + 1],
                                                pbv[1][:, p, :], op0=OP.mult,
                                                op1=OP.subtract)),
                   waits=[tok_g13, tok_z, ck["upd_win"], ck["lrwdB_ready"]])
    for p in range(NP):
        tok_uo = P("v", (lambda e, p=p:
                         e.scalar_tensor_tensor(wout[:, p, :], wout[:, p, :],
                                                lrwdB[:, 3, p, ci:ci + 1],
                                                pbv[0][:, p, :], op0=OP.mult,
                                                op1=OP.add)),
                   waits=[tok_g11, tok_omm, ck["upd_wout"]])
    ck["upd_win"] = tok_ui
    ck["upd_wout"] = tok_uo
    ck["b0"] = tok_uo
    ck["b1"] = tok_ui

    for p, hh, hsl, off in heads():
        t_tok = P("t", (lambda e, p=p, hh=hh, hsl=hsl, off=off:
                        e.transpose(pb5WT[hsl, p, :], wout[hsl, p, :],
                                    ident[hsl, hsl], tile_position=(off, off))),
                  waits=[tok_uo, ck["b5"]])
    tok_trw = t_tok
    tok_wt = P("s", lambda e: e.activation(woutT[:], pb5WT[:, 0:8, :], ACT.Copy),
               waits=[tok_trw])
    ck["b5"] = tok_wt
    ck["wt_evac"] = tok_wt


def w2_tile_src_g(dr, c):
    rt, cc = c // 4, c % 4
    return dr["w2"][rt * 128:(rt + 1) * 128, cc * 1024:(cc + 1) * 1024]


def t1_src(dr, i):
    if i < 32:
        return dr["w1"][i * 128:(i + 1) * 128, :]
    return w2_tile_src_g(dr, i - 32)


def emit_t1_prefetch(plan, P, dr, V, ck, n):
    wstream = V["wstream"]
    for _ in range(n):
        i = ck["t1_pf"]
        if i >= 64:
            return
        ck["t1_pf"] += 1
        par = i % 4
        srcap = t1_src(dr, i)
        ck["t1_dma"][par] = P("y", (lambda e, srcap=srcap, par=par:
                                    e.dma_start(wstream[:, par, :], srcap)),
                              waits=[ck["t1_ws"][par], ck["t1_dma"][par]],
                              dma=True)


def emit_t1_compute(plan, P, dr, V, ck, n):
    wstream, tabs = V["wstream"], V["tabs"]
    for _ in range(n):
        i = ck["t1_i"]
        if i >= 64:
            return
        ck["t1_i"] += 1
        par = i % 4
        acc = V["acc1"] if i < 32 else V["acc2"]
        tok_tb = P("v", (lambda e, par=par:
                         e.reduce_sum(tabs[:], wstream[:, par, :], axis=AX.X,
                                      apply_absolute_value=True)),
                   waits=[ck["t1_dma"][par], ck["t1_tabs"]])
        if i % 32 == 0:
            tok_a = P("v", (lambda e, acc=acc: e.tensor_copy(acc[:], tabs[:])),
                      waits=[tok_tb])
        else:
            tok_a = P("v", (lambda e, acc=acc:
                            e.tensor_tensor(acc[:], acc[:], tabs[:], OP.add)),
                      waits=[tok_tb])
        ck["t1_ws"][par] = tok_a
        ck["t1_tabs"] = tok_a
        emit_t1_prefetch(plan, P, dr, V, ck, 1)
    return


def pbj(pb, j):
    if j == 0:
        return pb[0][:, 0:512]
    if j == 1:
        return pb[0][:, 512:1024]
    return pb[2 + (j - 2)][:, :]


def emit_passB(nc, plan, P, dr, dbg_dr, tb, A, A4, bar_toks, L,
               b2bf, b1cols, ident, identf, ck, t1bufs):
    pb = tb["pb"]
    xt = tb["xt"]
    outt = tb["outt"]
    pb5w = pb[5].ap().rearrange("p (a b) -> p a b", b=128)
    pb6w = pb[6].ap().rearrange("p (a b) -> p a b", b=128)
    NBT = L // SEGB
    wstream = t1bufs["wstream"]
    acc1, acc2, tabs = t1bufs["acc1"], t1bufs["acc2"], t1bufs["tabs"]

    for e in Plan.ENGS:
        plan.op(e, lambda _e: None, waits=bar_toks, inc=False)

    w1T = A.alloc("w1T", [128, 8, FF])
    w2T = A.alloc("w2T", [128, 32, 1024])
    A.mark("tern_tmp")
    w12scr = A.alloc("w12scr", [128, 1024])
    sgA = A.alloc("sgA", [128, 1024])
    A.reset_to("tern_tmp")
    hT = A.alloc("hT", [128, 2, SEGB])
    xnB = A.alloc("xnB", [128, 2, 8, SEGB])
    s_rd = A.alloc("s_rd", [128, 2, 1024])
    g_sb = A.alloc("g_sb", [128, 2, 1024])

    tot = A4.alloc("totw", [1, 1])
    inv1 = A4.alloc("invw1", [128, 1])
    inv2 = A4.alloc("invw2", [128, 1])
    ones_col = A4.alloc("ones_col", [128, 1])
    ones_row = A4.alloc("ones_row", [1, 128])


    tok_o1 = P("v", lambda e: e.memset(ones_col[:], 1.0))
    tok_o2 = P("v", lambda e: e.memset(ones_row[:], 1.0))
    tabs_free = ck["t1_tabs"]
    for acc, inv in ((acc1, inv1), (acc2, inv2)):
        t_tok = P("t", (lambda e, acc=acc:
                        e.matmul(pb[7][0:1, 0:1], ones_col[:], acc[:],
                                 start=True, stop=True, skip_group_check=True)),
                  waits=[ck["t1_tabs"], tok_o1])
        tok_cp = P("v", lambda e: e.tensor_copy(tot[0:1, :], pb[7][0:1, 0:1]),
                   waits=[t_tok])
        t_tok = P("t", (lambda e:
                        e.matmul(pb[7][0:128, 1:2], ones_row[:], tot[0:1, :],
                                 start=True, stop=True, skip_group_check=True)),
                  waits=[tok_cp, tok_o2])
        tok_t1 = P("v", (lambda e: e.tensor_scalar(tabs[:], pb[7][0:128, 1:2],
                                                   1.0 / (FF * D), None,
                                                   op0=OP.mult)),
                   waits=[t_tok, tabs_free])
        tok_t2 = P("v", (lambda e: e.tensor_scalar(tabs[:], tabs[:], 1e-8, None,
                                                   op0=OP.max)), waits=[tok_t1])
        tok_i = P("v", (lambda e, inv=inv: e.reciprocal(inv[:], tabs[:])),
                  waits=[tok_t2])
        tabs_free = tok_i

    ws_free = list(ck["t1_ws"])
    ws_dma = list(ck["t1_dma"])
    scr_free = Z0
    tr_banks = [pb5w, pb6w]
    bank_ev = [Z0, Z0]
    tok_trev = Z0
    for wname, wT, inv in (("w1", w1T, inv1), ("w2", w2T, inv2)):
        for c in range(32):
            par = (c + (32 if wname == "w2" else 0)) % 4
            srcap = (dr["w1"][c * 128:(c + 1) * 128, :] if wname == "w1"
                     else w2_tile_src_g(dr, c))
            tok_l = P("y", (lambda e, srcap=srcap, par=par:
                            e.dma_start(wstream[:, par, :], srcap)),
                      waits=[ws_free[par], ws_dma[par]], dma=True)
            ws_dma[par] = tok_l
            tok_sa = P("s", (lambda e, par=par, inv=inv:
                             e.activation(w12scr[:], wstream[:, par, :], ACT.Sign,
                                          bias=-0.5, scale=inv[:])),
                       waits=[tok_l, tok_i, scr_free])
            tok_sb_ = P("s", (lambda e, par=par, inv=inv:
                              e.activation(sgA[:], wstream[:, par, :], ACT.Sign,
                                           bias=0.5, scale=inv[:])))
            ws_free[par] = tok_sb_
            tok_w = P("v", lambda e: e.tensor_tensor(w12scr[:], w12scr[:], sgA[:],
                                                     OP.add),
                      waits=[tok_sb_])
            pbx = tr_banks[par % 2]
            for j in range(8):
                t_tok = P("t", (lambda e, j=j, pbx=pbx:
                                e.transpose(pbx[:, j, :],
                                            w12scr[:, j * 128:(j + 1) * 128],
                                            ident[:])),
                          waits=[tok_w, bank_ev[par % 2]])
            scr_free = t_tok
            if wname == "w1":
                dst = w1T[:, :, c * 128:(c + 1) * 128]
            else:
                rt, cc = c // 4, c % 4
                dst = w2T[:, cc * 8:(cc + 1) * 8, rt * 128:(rt + 1) * 128]
            tok_trev = P("v", (lambda e, dst=dst, pbx=pbx:
                               e.tensor_copy(dst, pbx[:, :, :])),
                         waits=[t_tok])
            bank_ev[par % 2] = tok_trev
    tok_tern = tok_trev
    tok_tern_t = ("t", plan.cnt["t"])

    xt_free = list(ck["xt"])
    xdmaB = list(ck["xdma"])
    h_free = [Z0, Z0]
    bank_free = {i: Z0 for i in range(4)}
    mm1_free = {0: Z0, 1: tabs_free}
    g_free = [Z0, Z0]
    srd_free = Z0
    out_free = [Z0, Z0]
    xnB_free = [Z0, Z0]
    xnB_tok = [Z0, Z0]
    srd_tok = Z0

    def dma_xnB(bt):
        par = bt % 2
        tok = P("y", (lambda e, bt=bt, par=par:
                      e.dma_start(xnB[:, par, :, :],
                                  dr["xn_sp"][:, :, bt * SEGB:(bt + 1) * SEGB]
                                  .rearrange("a p t -> p a t"))),
                waits=[xnB_free[par], xnB_tok[par], tok_tern], dma=True)
        xnB_tok[par] = tok

    dma_xnB(0)
    for bt in range(NBT):
        par = bt % 2
        if bt + 1 < NBT:
            dma_xnB(bt + 1)
        srd_tok = P("y", (lambda e, bt=bt:
                          e.dma_start(s_rd[:, 0:2, :],
                                      dr["s_sp"][bt * SEGB:(bt + 1) * SEGB, :]
                                      .rearrange("(a p) f -> p a f", p=128))),
                    waits=[srd_free, srd_tok, tok_tern], dma=True)
        for tsub in range(2):
            gidx = bt * 2 + tsub
            slot = gidx % 4
            xt_tok = P("y", (lambda e, gidx=gidx, slot=slot:
                             e.dma_start(xt[:, slot, :],
                                         dr["x"][gidx * 128:(gidx + 1) * 128, :])),
                       waits=[xt_free[slot], xdmaB[slot]], dma=True)
            xdmaB[slot] = xt_tok
            if tsub == 0:
                xt_tok0 = xt_tok
            else:
                xt_tok1 = xt_tok
        for f1c in range(32):
            hpar = f1c % 2
            bank = pb[4] if hpar == 0 else pb[7]
            for fc in range(8):
                t_tok = P("t", (lambda e, f1c=f1c, fc=fc, bank=bank, par=par:
                                e.matmul(bank[:, 0:SEGB],
                                         w1T[:, fc, f1c * 128:(f1c + 1) * 128],
                                         xnB[:, par, fc, :], start=(fc == 0),
                                         stop=(fc == 7), skip_group_check=True)),
                          waits=[xnB_tok[par], mm1_free[hpar], tok_tern_t])
            tok_mm1 = t_tok
            if f1c > 0:
                pf1c, phpar = f1c - 1, (f1c - 1) % 2
                for tsub in range(2):
                    for half in range(2):
                        j = tsub * 2 + half
                        t_tok = P("t", (lambda e, pf1c=pf1c, phpar=phpar,
                                        tsub=tsub, half=half, j=j:
                                        e.matmul(pbj(pb, j),
                                                 hT[:, phpar, tsub * 128:(tsub + 1) * 128],
                                                 w2T[:, pf1c, half * 512:(half + 1) * 512],
                                                 start=(pf1c == 0), stop=False,
                                                 skip_group_check=True)),
                                  waits=[tok_gelu_prev, bank_free[j]])
                mm1_free[phpar] = t_tok
                h_free[phpar] = t_tok
            tok_gelu_prev = P("s", (lambda e, hpar=hpar, f1c=f1c, bank=bank:
                                    e.activation(hT[:, hpar, :], bank[:, 0:SEGB],
                                                 ACT.Gelu, scale=0.5,
                                                 bias=b1cols[:, f1c:f1c + 1])),
                              waits=[tok_mm1, h_free[hpar], tok_tern])
        pf1c, phpar = 31, 31 % 2
        for tsub in range(2):
            for half in range(2):
                j = tsub * 2 + half
                t_tok = P("t", (lambda e, pf1c=pf1c, phpar=phpar, tsub=tsub,
                                half=half, j=j:
                                e.matmul(pbj(pb, j),
                                         hT[:, phpar, tsub * 128:(tsub + 1) * 128],
                                         w2T[:, pf1c, half * 512:(half + 1) * 512],
                                         start=False, stop=False,
                                         skip_group_check=True)),
                          waits=[tok_gelu_prev, bank_free[j]])
        mm1_free[phpar] = t_tok
        h_free[phpar] = t_tok
        for tsub in range(2):
            for half in range(2):
                j = tsub * 2 + half
                t_tok = P("t", (lambda e, j=j, half=half:
                                e.matmul(pbj(pb, j), tb["two_row"][:],
                                         b2bf[0:1, half * 512:(half + 1) * 512],
                                         start=False, stop=True,
                                         skip_group_check=True)))
                tok_g = P("s", (lambda e, j=j, tsub=tsub, half=half:
                                e.activation(g_sb[:, tsub, half * 512:(half + 1) * 512],
                                             pbj(pb, j), ACT.Silu, scale=0.5)),
                          waits=[t_tok, g_free[tsub]])
                bank_free[j] = tok_g
            gidx = bt * 2 + tsub
            slot = gidx % 4
            opar = gidx % 2
            xt_tok = xt_tok0 if tsub == 0 else xt_tok1
            tok_c1 = P("v", (lambda e, tsub=tsub, opar=opar:
                             e.tensor_tensor(outt[:, opar, :], s_rd[:, tsub, :],
                                             g_sb[:, tsub, :], OP.mult)),
                       waits=[tok_g, srd_tok, out_free[opar]])
            tok_c2 = P("v", (lambda e, slot=slot, opar=opar:
                             e.tensor_tensor(outt[:, opar, :], outt[:, opar, :],
                                             xt[:, slot, :], OP.add)),
                       waits=[tok_c1, xt_tok])
            g_free[tsub] = tok_c2
            xt_free[slot] = tok_c2
            tok_od = P("g", (lambda e, gidx=gidx, opar=opar:
                             e.dma_start(dr["out"][gidx * 128:(gidx + 1) * 128, :],
                                         outt[:, opar, :])),
                       waits=[tok_c2], dma=True)
            out_free[opar] = tok_od
            if tsub == 1:
                srd_free = tok_c1
        xnB_free[par] = ("t", plan.cnt["t"])


_NC_CACHE = {}


def _get_program(L=4096):
    if L not in _NC_CACHE:
        _NC_CACHE[L] = build_program(L=L)
    return _NC_CACHE[L]


def kernel(L0_attn_x, L0_attn_Wq, L0_attn_Wk, L0_attn_Wv, L0_attn_Wlr,
           L0_attn_Wbeta, L0_attn_Wo, L0_attn_W_in_init, L0_attn_W_out_init,
           L0_ffn_W1, L0_ffn_b1, L0_ffn_W2, L0_ffn_b2):
    x = np.ascontiguousarray(np.asarray(L0_attn_x, dtype=np.float32))
    B, L, _ = x.shape
    base = {
        "wq": np.ascontiguousarray(np.asarray(L0_attn_Wq, np.float32)),
        "wk": np.ascontiguousarray(np.asarray(L0_attn_Wk, np.float32)),
        "wv": np.ascontiguousarray(np.asarray(L0_attn_Wv, np.float32)),
        "wo": np.ascontiguousarray(np.asarray(L0_attn_Wo, np.float32)),
        "wlr": np.ascontiguousarray(np.asarray(L0_attn_Wlr, np.float32)),
        "wbeta": np.ascontiguousarray(np.asarray(L0_attn_Wbeta, np.float32)),
        "winit": np.ascontiguousarray(
            np.asarray(L0_attn_W_in_init, np.float32).reshape(64, 16 * 64)),
        "woutit": np.ascontiguousarray(
            np.asarray(L0_attn_W_out_init, np.float32).reshape(64, 16 * 64)),
        "w1": np.ascontiguousarray(np.asarray(L0_ffn_W1, np.float32)),
        "b1": np.ascontiguousarray(np.asarray(L0_ffn_b1, np.float32)),
        "w2": np.ascontiguousarray(np.asarray(L0_ffn_W2, np.float32)),
        "b2": np.ascontiguousarray(np.asarray(L0_ffn_b2, np.float32)),
    }
    base.update(host_constants())
    nc = _get_program(L=L)
    in_maps = [dict(base, x=x[i]) for i in range(B)]
    res = run_bass_kernel_spmd(nc, in_maps, core_ids=list(range(B)))
    return np.stack([res.results[i]["out"] for i in range(B)], axis=0)


# version 07
# speedup vs baseline: 1.3242x; 1.0787x over previous
import numpy as np
import ml_dtypes
import concourse.bass as bass
import concourse.mybir as mybir
from concourse.bass_utils import run_bass_kernel_spmd
from concourse.bass_isa_rust import ReduceOp

F32 = mybir.dt.float32
BF16 = mybir.dt.bfloat16
AX = mybir.AxisListType
OP = mybir.AluOpType
ACT = mybir.ActivationFunctionType

D = 1024
NCORE = 8
CH = 64
NP = 8
FF = 4096
SEG = 256
SEGB = 256
BASE_LR = 0.001
BASE_WD = 0.9


class Plan:

    NROT = 8
    ENGS = ("t", "v", "s", "g", "y")
    SEMS = tuple(["t", "v", "s", "g"]
                 + [f"dy{i}" for i in range(8)] + [f"dg{i}" for i in range(8)])

    def __init__(self):
        self.ops = {e: [] for e in self.ENGS}
        self.cnt = {sm: 0 for sm in self.SEMS}
        self.dma_i = {"y": 0, "g": 0}
        self._waited = {e: {sm: 0 for sm in self.SEMS} for e in self.ENGS}

    def op(self, eng, fn, waits=(), inc=True, dma=False):
        w = list(waits)
        if dma:
            i = self.dma_i[eng]
            self.dma_i[eng] += 1
            sem_name = f"d{eng}{i % self.NROT}"
            amt = 16
            if self.cnt[sem_name] > 0:
                w.append((sem_name, self.cnt[sem_name]))
        else:
            sem_name = eng
            amt = 1
        tok = None
        if inc:
            self.cnt[sem_name] += amt
            tok = self.cnt[sem_name]
        ww = []
        for (sm, c) in w:
            if c is None or c <= 0:
                continue
            if self._waited[eng][sm] >= c:
                continue
            self._waited[eng][sm] = c
            ww.append((sm, c))
        self.ops[eng].append((fn, tuple(ww), sem_name if inc else None, amt))
        return tok

    def barrier_tokens(self):
        return dict(self.cnt)


def emit_engine(plan, eng, engobj, sems):
    for fn, waits, inc_sem, amt in plan.ops[eng]:
        for (sm, c) in waits:
            engobj.wait_ge(sems[sm], c)
        inst = fn(engobj)
        if inc_sem is not None:
            assert inst is not None, f"op on {eng} must return instruction to inc"
            inst.then_inc(sems[inc_sem], amt)


def host_constants():
    c = {}
    c["c_ident"] = np.eye(128, dtype=ml_dtypes.bfloat16)
    c["c_identf"] = np.eye(128, dtype=np.float32)
    sel2 = np.zeros((2, 128), dtype=ml_dtypes.bfloat16)
    sel2[0, 0:64] = 1.0
    sel2[1, 64:128] = 1.0
    c["c_sel2"] = sel2
    mask2 = np.zeros((128, 2), dtype=ml_dtypes.bfloat16)
    mask2[0:64, 0] = 1.0
    mask2[64:128, 1] = 1.0
    c["c_mask2"] = mask2
    lsc = np.zeros((64, 1), dtype=np.float32)
    lsc[0:32, 0] = BASE_LR / CH
    lsc[32:64, 0] = BASE_WD / CH
    c["c_lrwdscale"] = lsc
    c["c_two"] = np.full((1, 128), 2.0, dtype=ml_dtypes.bfloat16)
    c["c_pool"] = np.tile(np.array([[1e-5, -0.5, 0.5, 1.0 / (FF * D), 1e-8,
                                     -16.0]], dtype=np.float32), (128, 1))
    selbig = np.zeros((64, 32, 128), dtype=ml_dtypes.bfloat16)
    for p in range(NP):
        for q in range(4):
            idx = q * 8 + p
            selbig[q * 16 + 2 * p, idx, 0:64] = 1.0
            selbig[q * 16 + 2 * p + 1, idx, 64:128] = 1.0
    c["c_selbig"] = selbig.reshape(64, 32 * 128)
    return c


def build_program(L=4096, dbg=(), skip_passB=False):
    NSEG = L // SEG
    NCH = SEG // CH
    nc = bass.Bass("TRN2", target_bir_lowering=False)

    dr = {}
    dr["x"] = nc.dram_tensor("x", [L, D], F32, kind="ExternalInput").ap()
    for nm, shp in [("wq", [D, D]), ("wk", [D, D]), ("wv", [D, D]),
                    ("wo", [D, D]), ("wlr", [32, D]), ("wbeta", [32, D]),
                    ("winit", [64, 16 * 64]), ("woutit", [64, 16 * 64]),
                    ("w1", [FF, D]), ("b1", [FF]), ("w2", [D, FF]), ("b2", [D])]:
        dr[nm] = nc.dram_tensor(nm, shp, F32, kind="ExternalInput").ap()
    consts = host_constants()
    for nm, arr in consts.items():
        dt = BF16 if arr.dtype == ml_dtypes.bfloat16 else F32
        dr[nm] = nc.dram_tensor(nm, list(arr.shape), dt, kind="ExternalInput").ap()
    dr["out"] = nc.dram_tensor("out", [L, D], F32, kind="ExternalOutput").ap()
    dr["s_sp"] = nc.dram_tensor("s_sp", [L, D], BF16, kind="Internal").ap()
    dr["xn_sp"] = nc.dram_tensor("xn_sp", [8, 128, L], BF16, kind="Internal").ap()
    dbg_dr = {}
    for nm, shp, dt in dbg:
        dbg_dr[nm] = nc.dram_tensor(nm, shp, dt, kind="ExternalOutput").ap()

    plan = Plan()

    import contextlib
    with contextlib.ExitStack() as st:
        def T(*args, **kw):
            return st.enter_context(nc.sbuf_tensor(*args, **kw))

        def PS(*args, **kw):
            return st.enter_context(nc.psum_tensor(*args, **kw))

        ident = T("ident", [128, 128], BF16)
        identf = T("identf", [128, 128], F32)
        sel2 = T("sel2", [2, 128], BF16)
        mask2 = T("mask2", [128, 2], BF16)
        lrwdscale = T("lrwdscale", [64, 1], F32)
        two_row = T("two_row", [1, 128], BF16)
        cpool = T("cpool", [128, 8], F32)
        nc.const_aps.register(F32, 1e-5, cpool[:, 0:1])
        nc.const_aps.register(F32, -0.5, cpool[:, 1:2])
        nc.const_aps.register(F32, 0.5, cpool[:, 2:3])
        nc.const_aps.register(F32, 1.0 / (FF * D), cpool[:, 3:4])
        nc.const_aps.register(F32, 1e-8, cpool[:, 4:5])
        nc.const_aps.register(F32, -16.0, cpool[:, 5:6])
        xt = T("xt", [128, 4, D], F32)
        outt = T("outt", [128, 2, D], F32)
        abf = T("abf", [128, 76 * 1024], BF16)
        af32 = T("af32", [128, 6 * 1024], F32)
        pb0big = PS("pb0", [128, 1024], F32)
        pb = [pb0big, pb0big,
              PS("pb2", [128, 512], F32), PS("pb3", [128, 512], F32),
              PS("pb4", [128, 512], F32), PS("pb5", [128, 1024], BF16),
              PS("pb6", [128, 1024], BF16), PS("pb7", [128, 512], F32)]
        sems = {nm: st.enter_context(nc.semaphore("sem_" + nm))
                for nm in Plan.SEMS}
        block = st.enter_context(nc.Block())
        tb = dict(ident=ident, identf=identf, sel2=sel2, mask2=mask2,
                  lrwdscale=lrwdscale, two_row=two_row, cpool=cpool,
                  xt=xt, outt=outt, abf=abf, af32=af32, pb=pb)

        emit_all(nc, plan, dr, dbg_dr, tb, L, NSEG, NCH, skip_passB)

        @block.tensor
        def _(t):
            emit_engine(plan, "t", t, sems)

        @block.vector
        def _(v):
            emit_engine(plan, "v", v, sems)

        @block.scalar
        def _(s):
            emit_engine(plan, "s", s, sems)

        @block.gpsimd
        def _(g):
            emit_engine(plan, "g", g, sems)

        @block.sync
        def _(y):
            emit_engine(plan, "y", y, sems)

    return nc


class Arena:

    def __init__(self, tensor, n_elems):
        self.t = tensor
        self.n = n_elems
        self.off = 0
        self.marks = {}
        self.slots = {}

    def alloc(self, name, shape):
        n = int(np.prod(shape[1:]))
        assert self.off + n <= self.n, f"arena overflow at {name}: {self.off}+{n}>{self.n}"
        base = self.off
        self.off += n
        self.slots[name] = (base, shape)
        return self.view(name)

    def view(self, name):
        base, shape = self.slots[name]
        ap = self.t[:shape[0], base:base + int(np.prod(shape[1:]))]
        if len(shape) > 2:
            pat = "p (" + " ".join(f"d{i}" for i in range(1, len(shape))) + ") -> p " + \
                  " ".join(f"d{i}" for i in range(1, len(shape)))
            kw = {f"d{i}": shape[i] for i in range(2, len(shape))}
            ap = ap.rearrange(pat, **kw)
        return ap

    def mark(self, tag):
        self.marks[tag] = self.off

    def reset_to(self, tag):
        self.off = self.marks[tag]



Z0 = ("v", 0)


DEBUG_ARENAS = {}


def emit_all(nc, plan, dr, dbg_dr, tb, L, NSEG, NCH, skip_passB=False):
    def P(eng, fn, waits=(), dma=False):
        if dma:
            sem_name = f"d{eng}{plan.dma_i[eng] % plan.NROT}"
            tok = plan.op(eng, fn, waits=waits, inc=True, dma=True)
            return (sem_name, tok)
        tok = plan.op(eng, fn, waits=waits, inc=True, dma=False)
        return (eng, tok)

    pb = tb["pb"]
    A = Arena(tb["abf"], 76 * 1024)
    A4 = Arena(tb["af32"], 6 * 1024)
    DEBUG_ARENAS["A"] = A
    DEBUG_ARENAS["A4"] = A4
    ident = tb["ident"]
    identf = tb["identf"]

    b2bf = A.alloc("b2bf", [1, 1024])
    A.mark("postperm")
    selbig = A.alloc("selbig", [64, 32, 128])
    wqT = A.alloc("wqT", [128, 8, 1024])
    wkT = A.alloc("wkT", [128, 8, 1024])
    wvT = A.alloc("wvT", [128, 8, 1024])
    woT = A.alloc("woT", [128, 8, 1024])
    wlrbT = A.alloc("wlrbT", [128, 8, 64])
    winT = A.alloc("winT", [128, 8, 64])
    wout = A.alloc("wout", [128, 8, 64])
    woutT = A.alloc("woutT", [128, 8, 64])
    A.mark("postAW")
    wnat = A.alloc("wnat", [128, 8, 1024])
    winit_nat = A.alloc("winit_nat", [64, 1024])
    woutit_nat = A.alloc("woutit_nat", [64, 1024])
    wlrb_nat = A.alloc("wlrb_nat", [64, 1024])
    A.reset_to("postAW")

    b1nat = A4.alloc("b1nat", [32, 128])
    b1cols = A4.alloc("b1cols", [128, 32])
    lrwd_f = A4.alloc("lrwd_f", [64, 2, 256])
    lrwdc = A4.alloc("lrwdc", [64, 2, 4])
    lrwdB = A4.alloc("lrwdB", [128, 2, 4, 8, 4])
    SUMS = A4.alloc("SUMS", [128, 16])
    RECIP = A4.alloc("RECIP", [128, 16])
    CLI = A4.alloc("CLI", [128, 8])
    CLO = A4.alloc("CLO", [128, 8])
    nmq = A4.alloc("nmq", [128, 2])
    bnst = A4.alloc("bnst", [128, 12])
    mv = A4.alloc("mv", [128, 2])
    rstd = A4.alloc("rstd", [128, 1])
    lntmp = A4.alloc("lntmp", [128, 1])
    xt = tb["xt"]
    outt = tb["outt"]

    pb5w = pb[5].ap().rearrange("p (a b) -> p a b", b=128)
    pb5v = pb[5].ap().rearrange("p (a b) -> p a b", b=64)
    pb6w = pb[6].ap().rearrange("p (a b) -> p a b", b=128)
    pb6v = pb[6].ap().rearrange("p (a b) -> p a b", b=64)
    pb7v = pb[7].ap().rearrange("p (a b) -> p a b", b=64)

    dstmap = {"c_ident": ident, "c_identf": identf, "c_sel2": tb["sel2"],
              "c_mask2": tb["mask2"], "c_lrwdscale": tb["lrwdscale"],
              "c_two": tb["two_row"]}
    cy_toks = []
    for nm, dst in dstmap.items():
        cy_toks.append(P("y", lambda e, dst=dst, nm=nm: e.dma_start(dst[:], dr[nm]),
                         dma=True))
    cy_toks.append(P("y", lambda e: e.dma_start(tb["cpool"][:, 0:6], dr["c_pool"]),
                    dma=True))
    cy_toks.append(P("y", lambda e: e.dma_start(selbig.rearrange("p a b -> p (a b)"),
                                                dr["c_selbig"]), dma=True))
    cy_toks.append(P("y", lambda e: e.dma_start(b1nat[:],
                                                dr["b1"].rearrange("(a b) -> a b", b=128)),
                    dma=True))
    cg_toks2 = [
        P("g", lambda e: e.dma_start(winit_nat[:], dr["winit"]), dma=True),
        P("g", lambda e: e.dma_start(woutit_nat[:], dr["woutit"]), dma=True),
        P("g", lambda e: e.dma_start(wlrb_nat[0:32, :], dr["wlr"]), dma=True),
        P("g", lambda e: e.dma_start(wlrb_nat[32:64, :], dr["wbeta"]), dma=True),
    ]
    cg_toks = [P("g", lambda e: e.dma_start(b2bf[:],
                                            dr["b2"].rearrange("(a b) -> a b", a=1)),
                dma=True)]

    t_tok = P("t", lambda e: e.transpose(pb[7][0:128, 0:32], b1nat[:],
                                         identf[0:32, 0:32]),
              waits=cy_toks)
    tok_b1 = P("v", lambda e: e.tensor_copy(b1cols[:], pb[7][0:128, 0:32]),
               waits=[t_tok])

    for h in range(16):
        p_, hh = h // 2, h % 2
        sl = slice(h * 64, (h + 1) * 64)
        hsl = slice(hh * 64, (hh + 1) * 64)
        t_tok = P("t", (lambda e, sl=sl, p_=p_, hh=hh, hsl=hsl:
                        e.transpose(pb5v[hsl, p_, :], winit_nat[:, sl],
                                    ident[0:64, 0:64], tile_position=(0, 64 * hh))),
                  waits=cg_toks + cg_toks2 + cy_toks + [tok_b1])
        t_tok = P("t", (lambda e, sl=sl, p_=p_, hh=hh, hsl=hsl:
                        e.transpose(pb5v[hsl, 8 + p_, :], woutit_nat[:, sl],
                                    ident[0:64, 0:64], tile_position=(0, 64 * hh))))
        t_tok = P("t", (lambda e, sl=sl, p_=p_, hh=hh, hsl=hsl:
                        e.matmul(pb7v[hsl, p_, :], ident[0:64, 0:64],
                                 woutit_nat[:, sl], start=True, stop=True,
                                 skip_group_check=True, tile_position=(0, 64 * hh))))
    P("v", lambda e: e.tensor_copy(winT[:], pb5v[:, 0:8, :]), waits=[t_tok])
    P("v", lambda e: e.tensor_copy(woutT[:], pb5v[:, 8:16, :]))
    tok_state0 = P("v", lambda e: e.tensor_copy(wout[:], pb7v[:, 0:8, :]))

    for fc in range(8):
        t_tok = P("t", (lambda e, fc=fc:
                        e.transpose(pb6v[0:128, 8 + fc, :],
                                    wlrb_nat[:, fc * 128:(fc + 1) * 128],
                                    ident[0:64, 0:64])),
                  waits=[tok_state0])
    tok_evac = P("v", lambda e: e.tensor_copy(wlrbT[:], pb6v[:, 8:16, :]),
                 waits=[t_tok])

    tok_wload = Z0
    for wname, wT in (("wq", wqT), ("wk", wkT), ("wv", wvT), ("wo", woT)):
        wnat_free = ("t", plan.cnt["t"])
        tok_wload = P("g", (lambda e, wname=wname:
                            e.dma_start(wnat[:],
                                        dr[wname].rearrange("(c p) f -> p c f", p=128))),
                      waits=[wnat_free, tok_wload], dma=True)
        for fc in range(8):
            for c in range(8):
                t_tok = P("t", (lambda e, fc=fc, c=c:
                                e.transpose(pb5w[:, c, :],
                                            wnat[:, c, fc * 128:(fc + 1) * 128],
                                            ident[:])),
                          waits=[tok_wload, tok_evac])
            tok_evac = P("v", (lambda e, wT=wT, fc=fc:
                               e.tensor_copy(wT[:, fc, :],
                                             pb5w.rearrange("p a b -> p (a b)"))),
                         waits=[t_tok])

    qkv = A.alloc("qkv", [128, 2, 8, NCH, 192])
    xn_seg = A.alloc("xn_seg", [128, 2, 8, SEG])
    o_seg = A.alloc("o_seg", [128, 2, 8, SEG])
    s_seg = A.alloc("s_seg", [128, 2, 2, 1024])
    E_sb = A.alloc("E_sb", [128, 8, 128])
    p_q = A.alloc("p_q", [128, 8, 64])
    p_li = A.alloc("p_li", [128, 8, 64])
    m_sb = A.alloc("m_sb", [128, 8, 64])
    a_sb = A.alloc("a_sb", [128, 8, 64])
    t_sb = A.alloc("t_sb", [128, 8, 64])
    gzp = A.alloc("gzp", [128, 8, 128])
    s2_sb = A.alloc("s2_sb", [2, 8, 64])
    rk2 = A.alloc("rk2", [128, 8, 2])
    kvT_sb = A.alloc("kvT_sb", [128, 8, 128])
    gzpT_sb = A.alloc("gzpT_sb", [128, 8, 128])
    xn_tok = A.alloc("xn_tok", [128, 2, 1024])
    lrwdc_bf = A.alloc("lrwdc_bf", [64, 2, 4])
    A.mark("passA_end")

    z16 = pb[0].ap().rearrange("p (a b) -> p a b", b=64)
    pbv = [z16[:, 0:8, :], z16[:, 8:16, :],
           pb[2].ap().rearrange("p (a b) -> p a b", b=64),
           pb[3].ap().rearrange("p (a b) -> p a b", b=64),
           pb[4].ap().rearrange("p (a b) -> p a b", b=64)]
    views = dict(qkv=qkv, xn_seg=xn_seg, o_seg=o_seg, s_seg=s_seg, E_sb=E_sb,
                 p_q=p_q, p_li=p_li, m_sb=m_sb, a_sb=a_sb, t_sb=t_sb,
                 gzp=gzp, s2_sb=s2_sb, rk2=rk2, kvT_sb=kvT_sb,
                 gzpT_sb=gzpT_sb, winT=winT, wout=wout, woutT=woutT,
                 SUMS=SUMS, RECIP=RECIP, CLI=CLI, CLO=CLO, nmq=nmq,
                 lrwdB=lrwdB, pbv=pbv, pb5w=pb5w, pb6w=pb6w,
                 wqT=wqT, wkT=wkT, wvT=wvT, woT=woT, wlrbT=wlrbT,
                 lrwd_f=lrwd_f, lrwdc=lrwdc, bnst=bnst, mv=mv, rstd=rstd,
                 lntmp=lntmp, xn_tok=xn_tok, selbig=selbig,
                 lrwdc_bf=lrwdc_bf)

    wstream = A4.alloc("wstream", [128, 4, 1024])
    acc1 = A4.alloc("accw1", [128, 1])
    acc2 = A4.alloc("accw2", [128, 1])
    tabs = A4.alloc("tabsw", [128, 1])
    t1bufs = dict(wstream=wstream, acc1=acc1, acc2=acc2, tabs=tabs)

    ck = {
        "t1_i": 0, "t1_pf": 0, "t1_ws": [Z0, Z0, Z0, Z0],
        "t1_dma": [Z0, Z0, Z0, Z0], "t1_tabs": Z0, "xdma": [Z0, Z0, Z0, Z0],
        "upd_win": tok_state0, "upd_wout": tok_state0, "wt_evac": tok_state0,
        "b0": Z0, "b1": Z0, "b2": Z0, "b3": tok_b1, "b4": Z0,
        "b5": tok_evac, "b6": tok_evac, "b7": tok_b1,
        "eq": Z0, "ek": Z0, "pq": Z0, "pli": Z0, "plo": Z0, "m": Z0, "a": Z0,
        "gz": Z0, "rk2": Z0, "s2": Z0, "kvt": Z0, "gzpt": Z0,
        "nm": Z0, "oseg": Z0, "xt": [Z0, Z0, Z0, Z0], "xntok": [Z0, Z0],
        "mv": Z0, "ln_ev": Z0, "proj_last": ("s", 0), "lrwd_lc": Z0,
        "proj_pp": 0, "wo_fill": [], "wo_done": [("s", 0), ("s", 0)],
        "wo_t": [Z0, Z0],
        "lrwdc_r": Z0, "o_evac": ("s", 0), "qkv_ready": ("s", 0),
        "lrwdB_ready": Z0,
        "lrwdB_ready_p": [Z0, Z0], "lrwdB_r_p": [Z0, Z0],
        "oseg_p": [Z0, Z0], "xnseg_p": [Z0, Z0], "qkv_w": [Z0, Z0],
        "sseg_p": [Z0, Z0], "lrwdf_p": [Z0, Z0], "xnsp_dma_p": [Z0, Z0],
    }

    views.update(t1bufs)
    for stage in range(NSEG + 1):
        emit_stage(plan, P, dr, tb, views, ck, stage, NSEG, NCH)

    barA = plan.barrier_tokens()
    bar_toks = [(sm, c) for sm, c in barA.items() if c > 0]
    if not skip_passB:
        A.reset_to("postperm")
        emit_passB(nc, plan, P, dr, dbg_dr, tb, A, A4, bar_toks, L,
                   b2bf, b1cols, ident, identf, ck, t1bufs)

    fin = [(sm, plan.cnt[sm]) for sm in Plan.SEMS
           if sm.startswith("d") and plan.cnt[sm] > 0]
    plan.op("y", lambda e: None, waits=fin, inc=False)


def emit_stage(plan, P, dr, tb, V, ck, stage, NSEG, NCH):
    pb = tb["pb"]
    pb5w = V["pb5w"]
    ident = tb["ident"]
    xt = tb["xt"]

    fillers = list(ck["wo_fill"])
    ck["wo_fill"] = []
    if stage < NSEG:
        seg = stage
        par = seg % 2
        qkv = V["qkv"][:, par]
        xn_seg = V["xn_seg"][:, par]

        for tsub in range(2):
            gidx = seg * 2 + tsub
            slot = gidx % 4
            ck["xdma"][slot] = P("y", (lambda e, gidx=gidx, slot=slot:
                                       e.dma_start(xt[:, slot, :],
                                                   dr["x"][gidx * 128:(gidx + 1) * 128, :])),
                                 waits=[ck["xt"][slot], ck["xdma"][slot]],
                                 dma=True)
        if stage == 0:
            emit_t1_prefetch(plan, P, dr, V, ck, 4)

        def ln_sub(tsub, seg=seg, par=par, xn_seg=xn_seg):
            gidx = seg * 2 + tsub
            xpar = gidx % 4
            tok_x = ck["xdma"][xpar]
            tok_bn = P("v", (lambda e, xpar=xpar:
                             e.bn_stats(V["bnst"][:, 0:6], xt[:, xpar, 0:512])),
                       waits=[tok_x, ck["mv"]])
            tok_bn = P("v", (lambda e, xpar=xpar:
                             e.bn_stats(V["bnst"][:, 6:12], xt[:, xpar, 512:1024])))
            tok_mv = P("v", lambda e: e.bn_aggr(V["mv"][:], V["bnst"][:, 0:12]),
                       waits=[tok_bn])
            tok_ln = P("s", (lambda e: e.activation(V["lntmp"][:], V["mv"][:, 1:2],
                                                    ACT.Ln, bias=1e-5)),
                      waits=[tok_mv])
            tok_rstd = P("s", (lambda e: e.activation(V["rstd"][:], V["lntmp"][:],
                                                      ACT.Exp, scale=-0.5)),
                         waits=[tok_ln])
            ck["mv"] = tok_rstd
            npar = gidx % 2
            tok_xntok = P("v", (lambda e, xpar=xpar, npar=npar:
                                e.tensor_scalar(V["xn_tok"][:, npar, :],
                                                xt[:, xpar, :],
                                                V["mv"][:, 0:1], V["rstd"][:],
                                                op0=OP.subtract, op1=OP.mult)),
                          waits=[tok_rstd, tok_mv, ck["xntok"][npar]])
            ck["xt"][xpar] = tok_xntok
            for fc in range(8):
                t_tok = P("t", (lambda e, npar=npar, fc=fc:
                                e.transpose(pb5w[:, fc, :],
                                            V["xn_tok"][:, npar, fc * 128:(fc + 1) * 128],
                                            ident[:])),
                          waits=[tok_xntok, ck["b5"]])
            ck["xntok"][npar] = t_tok
            tok_ev = P("v", (lambda e, tsub=tsub, xn_seg=xn_seg:
                             e.tensor_copy(xn_seg[:, :, tsub * 128:(tsub + 1) * 128],
                                           pb5w[:, :, :])),
                       waits=[t_tok, ck["xnseg_p"][par], ck["xnsp_dma_p"][par]])
            ck["b5"] = tok_ev
            ck["ln_ev"] = tok_ev

        fillers.append(lambda: ln_sub(0))
        fillers.append(lambda: ln_sub(1))

        def proj_tile(wTname, m, coff, actf, seg=seg, par=par, qkv=qkv,
                      xn_seg=xn_seg):
            wT = V[wTname]
            bsel = ck["proj_pp"] % 2
            ck["proj_pp"] += 1
            bank = pb[3] if bsel == 0 else pb[7]
            bkey = "b3" if bsel == 0 else "b7"
            for fc in range(8):
                t_tok = P("t", (lambda e, wT=wT, m=m, fc=fc, bank=bank:
                                e.matmul(bank[:, 0:SEG],
                                         wT[:, fc, m * 128:(m + 1) * 128],
                                         xn_seg[:, fc, :],
                                         start=(fc == 0), stop=(fc == 7),
                                         skip_group_check=True)),
                          waits=[ck["ln_ev"], ck[bkey]])
            bview = bank.ap().rearrange("p (a b) -> p a b", b=64)
            tok_evac = P("s", (lambda e, bview=bview, m=m, coff=coff, actf=actf,
                               qkv=qkv:
                               e.activation(qkv[:, m, :, coff:coff + 64],
                                            bview[:, 0:NCH, :], actf)),
                         waits=[t_tok, ck["qkv_w"][par]])
            ck[bkey] = tok_evac
            ck["proj_last"] = tok_evac

        for wTname, coff, actf in (("wqT", 0, ACT.Silu), ("wkT", 64, ACT.Silu),
                                   ("wvT", 128, ACT.Copy)):
            for m in range(8):
                fillers.append(lambda wTname=wTname, m=m, coff=coff, actf=actf:
                               proj_tile(wTname, m, coff, actf))

        def lrwd_proj(seg=seg, par=par, xn_seg=xn_seg):
            for fc in range(8):
                t_tok = P("t", (lambda e, fc=fc:
                                e.matmul(pb[3][0:64, 0:SEG], V["wlrbT"][:, fc, :],
                                         xn_seg[:, fc, :], start=(fc == 0),
                                         stop=(fc == 7), skip_group_check=True)),
                          waits=[ck["ln_ev"], ck["b3"]])
            tok_sg = P("s", (lambda e, par=par:
                             e.activation(V["lrwd_f"][:, par, :], pb[3][0:64, 0:SEG],
                                          ACT.Sigmoid)),
                       waits=[t_tok, ck["lrwdf_p"][par]])
            ck["b3"] = tok_sg
            tok_rd = P("v", (lambda e, par=par:
                             e.reduce_sum(V["lrwdc"][:, par, :],
                                          V["lrwd_f"][:, par]
                                          .rearrange("p (a b) -> p a b", b=64),
                                          axis=AX.X)),
                      waits=[tok_sg, ck["lrwdc_r"]])
            ck["lrwdf_p"][par] = tok_rd
            tok_lc = P("v", (lambda e, par=par:
                             e.tensor_scalar(V["lrwdc_bf"][:, par, :],
                                             V["lrwdc"][:, par, :],
                                             tb["lrwdscale"][:], None,
                                             op0=OP.mult)), waits=[tok_rd])
            ck["lrwd_lc"] = tok_lc
            ck["xnseg_p"][par] = ("t", plan.cnt["t"])

        def lrwd_bcast(seg=seg, par=par):
            pb3b = pb[3].ap().rearrange("p (a b) -> p a b", b=4)
            for idx in range(32):
                t_tok = P("t", (lambda e, idx=idx, par=par:
                                e.matmul(pb3b[:, idx, :],
                                         V["selbig"][:, idx, :],
                                         V["lrwdc_bf"][:, par, :], start=True,
                                         stop=True, skip_group_check=True)),
                          waits=[ck["lrwd_lc"], ck["b3"]])
            tok_lB = P("v", (lambda e, par=par:
                             e.tensor_copy(V["lrwdB"][:, par]
                                           .rearrange("p a b c -> p (a b c)"),
                                           pb[3][:, 0:128])),
                       waits=[t_tok, ck["lrwdB_r_p"][par]])
            ck["b3"] = tok_lB
            ck["lrwdB_ready_p"][par] = tok_lB
            ck["lrwdc_r"] = ck["lrwd_lc"]

        fillers.append(lrwd_proj)
        fillers.append(lrwd_bcast)

        def xn_spill(seg=seg, par=par, xn_seg=xn_seg):
            tok = P("g", (lambda e, seg=seg, xn_seg=xn_seg:
                          e.dma_start(dr["xn_sp"][:, :, seg * SEG:(seg + 1) * SEG]
                                      .rearrange("a p t -> p a t"),
                                      xn_seg[:])),
                    waits=[ck["ln_ev"]], dma=True)
            ck["xnsp_dma_p"][par] = tok

        fillers.append(xn_spill)
        n_t1 = (64 + NSEG - 1) // NSEG
        for _ in range(n_t1):
            fillers.append(lambda: emit_t1_compute(plan, P, dr, V, ck, 1))

    if stage == 0:
        for f in fillers:
            f()
        ck["qkv_ready"] = ck["proj_last"]
        return

    seg = stage - 1
    par = seg % 2
    Vp = dict(V)
    Vp["qkv"] = V["qkv"][:, par]
    Vp["o_seg"] = V["o_seg"][:, par]
    Vp["lrwdB"] = V["lrwdB"][:, par]
    ck["lrwdB_ready"] = ck["lrwdB_ready_p"][par]
    ck["oseg"] = ck["oseg_p"][par]
    nf = len(fillers)
    npoints = 2 * NCH
    per = max(1, (nf + npoints - 1) // npoints) if nf else 0
    fstate = {"i": 0}

    def filler_cb(point):
        for _ in range(per):
            if fstate["i"] < nf:
                fillers[fstate["i"]]()
                fstate["i"] += 1

    for ci in range(NCH):
        emit_scan_chunk(plan, P, tb, Vp, ck, ci, filler_cb)
    while fstate["i"] < nf:
        fillers[fstate["i"]]()
        fstate["i"] += 1
    if stage < NSEG:
        ck["qkv_ready"] = ck["proj_last"]
    ck["qkv_w"][par] = ("t", plan.cnt["t"])
    ck["lrwdB_r_p"][par] = ck["upd_win"]

    o_evac_tok = ck["o_evac"]

    def wo_tile(tsub, half, seg=seg, par=par, o_evac_tok=o_evac_tok):
        o_seg = V["o_seg"][:, par]
        s_seg = V["s_seg"][:, par]
        bsel = ck["proj_pp"] % 2
        ck["proj_pp"] += 1
        bank = pb[3] if bsel == 0 else pb[7]
        bkey = "b3" if bsel == 0 else "b7"
        for fc in range(8):
            t_tok = P("t", (lambda e, tsub=tsub, half=half, fc=fc,
                            o_seg=o_seg, bank=bank:
                            e.matmul(bank[:, :],
                                     o_seg[:, fc, tsub * 128:(tsub + 1) * 128],
                                     V["woT"][:, fc, half * 512:(half + 1) * 512],
                                     start=(fc == 0), stop=(fc == 7),
                                     skip_group_check=True)),
                      waits=[o_evac_tok, ck[bkey]])
        tok_sev = P("s", (lambda e, tsub=tsub, half=half, s_seg=s_seg, bank=bank:
                          e.activation(s_seg[:, tsub, half * 512:(half + 1) * 512],
                                       bank[:, :], ACT.Copy)),
                    waits=[t_tok, ck["sseg_p"][par]])
        ck[bkey] = tok_sev
        ck["wo_done"][par] = tok_sev
        ck["wo_t"][par] = t_tok

    def s_spill(seg=seg, par=par):
        s_seg = V["s_seg"][:, par]
        tok_sdma = P("g", (lambda e, seg=seg, s_seg=s_seg:
                           e.dma_start(dr["s_sp"][seg * SEG:(seg + 1) * SEG, :]
                                       .rearrange("(a p) f -> p a f", p=128),
                                       s_seg[:])),
                     waits=[ck["wo_done"][par], ck["sseg_p"][par]], dma=True)
        ck["sseg_p"][par] = tok_sdma
        ck["oseg_p"][par] = ck["wo_t"][par]

    for tsub in range(2):
        for half in range(2):
            ck["wo_fill"].append(lambda tsub=tsub, half=half: wo_tile(tsub, half))
    ck["wo_fill"].append(s_spill)
    if stage == NSEG:
        for f in ck["wo_fill"]:
            f()
        ck["wo_fill"] = []


def emit_scan_chunk(plan, P, tb, V, ck, ci, filler_cb=None):
    pb = tb["pb"]
    pbv = V["pbv"]
    pb5w, pb6w = V["pb5w"], V["pb6w"]
    ident = tb["ident"]
    qkv, winT, wout, woutT = V["qkv"], V["winT"], V["wout"], V["woutT"]
    E_sb, p_q, p_li = V["E_sb"], V["p_q"], V["p_li"]
    m_sb, a_sb, t_sb, gzp = V["m_sb"], V["a_sb"], V["t_sb"], V["gzp"]
    s2_sb, rk2 = V["s2_sb"], V["rk2"]
    kvT_sb, gzpT_sb = V["kvT_sb"], V["gzpT_sb"]
    SUMS, RECIP, CLI, CLO = V["SUMS"], V["RECIP"], V["CLI"], V["CLO"]
    nmq, lrwdB = V["nmq"], V["lrwdB"]
    pb5WT = pb[5].ap().rearrange("p (a b) -> p a b", b=64)
    pbz = pb[0].ap().rearrange("p (a b) -> p a b", b=128)

    def heads():
        for p in range(NP):
            for hh in range(2):
                yield p, hh, slice(hh * 64, (hh + 1) * 64), (0 if hh == 0 else 64)

    for p, hh, hsl, off in heads():
        t_tok = P("t", (lambda e, p=p, hh=hh, hsl=hsl, off=off:
                        e.matmul(pbz[hsl, p, :], winT[hsl, p, :],
                                 qkv[hsl, p, ci, 0:128], start=True, stop=True,
                                 skip_group_check=True, tile_position=(off, off))),
                  waits=[ck["upd_win"], ck["qkv_ready"],
                         ck["b0"] if p < 4 else ck["b1"]])
    tok_z = t_tok

    for p in range(NP):
        t_tok = P("t", (lambda e, p=p:
                        e.transpose(pb5w[:, p, :], qkv[:, p, ci, 64:192],
                                    ident[:])),
                  waits=[ck["b5"]])
    tok_trkv = t_tok
    tok_kvt = P("v", lambda e: e.tensor_copy(kvT_sb[:], pb5w[:, 0:8, :]),
                waits=[tok_trkv, ck["kvt"]])
    ck["b5"] = tok_kvt

    if filler_cb:
        filler_cb(1)
    for p, hh, hsl, off in heads():
        t_tok = P("t", (lambda e, p=p, hh=hh, hsl=hsl, off=off:
                        e.matmul(pbv[2][hsl, p, :], woutT[hsl, p, :],
                                 qkv[hsl, p, ci, 128:192], start=True, stop=True,
                                 skip_group_check=True, tile_position=(off, off))),
                  waits=[ck["wt_evac"], ck["b2"]])
    tok_amm = t_tok

    tok_eqb = P("s", (lambda e: e.activation(E_sb[:], pb[0][:, :],
                                             ACT.Exp, bias=-16.0)),
                waits=[tok_z, ck["eq"]])
    ck["b0"] = tok_eqb
    ck["b1"] = tok_eqb
    tok_sums = P("v", (lambda e:
                       e.reduce_sum(SUMS[:], E_sb.rearrange("p a (b c) -> p (a b) c", c=64),
                                    axis=AX.X)),
                 waits=[tok_eqb])
    tok_rec = P("v", lambda e: e.reciprocal(RECIP[:], SUMS[:]),
                waits=[tok_sums])
    r2 = RECIP.rearrange("p (a b) -> p a b", b=2)
    rq_b = r2[:, :, 0:1].broadcast_to((128, 8, 64))
    rk_b = r2[:, :, 1:2].broadcast_to((128, 8, 64))
    rk_cols = r2[:, :, 1]
    Ek_view = E_sb[:, :, 64:128]
    Eq_view = E_sb[:, :, 0:64]
    tok_cli = P("v", (lambda e:
                      e.tensor_tensor(CLI[:], rk_cols,
                                      lrwdB[:, 0, :, ci], OP.mult)),
                waits=[ck["lrwdB_ready"], tok_rec])
    tok_clo = P("v", (lambda e:
                      e.tensor_tensor(CLO[:], rk_cols,
                                      lrwdB[:, 1, :, ci], OP.mult)))
    cli_b = CLI.rearrange("p (a b) -> p a b", b=1).broadcast_to((128, 8, 64))
    clo_b = CLO.rearrange("p (a b) -> p a b", b=1).broadcast_to((128, 8, 64))
    tok_pq = P("v", (lambda e: e.tensor_tensor(p_q[:], Eq_view, rq_b, OP.mult)),
               waits=[ck["pq"], tok_rec])
    tok_pli = P("v", (lambda e: e.tensor_tensor(p_li[:], Ek_view, cli_b, OP.mult)),
                waits=[ck["pli"], tok_clo])
    tok_plo = P("v", (lambda e: e.tensor_tensor(gzp[:, :, 64:128], Ek_view,
                                                clo_b, OP.mult)),
                waits=[ck["plo"]])
    m2_b = tb["mask2"].ap().rearrange("p (a b) -> p a b", a=1).broadcast_to((128, 8, 2))
    tok_rk2 = P("v", (lambda e: e.tensor_tensor(rk2[:], m2_b,
                                                rk_b[:, :, 0:2], OP.mult)),
                waits=[ck["rk2"]])
    tok_eqfree = tok_rk2

    tok_aev = P("s", lambda e: e.activation(a_sb[:], pbv[2][:, 0:8, :], ACT.Copy),
                waits=[tok_amm, ck["a"]])
    tok_m = P("v", (lambda e:
                    e.tensor_tensor(m_sb[:],
                                    E_sb[:, :, 64:128], pbv[2][:, 0:8, :],
                                    OP.mult)),
              waits=[tok_eqb, tok_aev, ck["m"]])
    ck["eq"] = tok_eqfree
    ck["ek"] = tok_m

    for p, hh, hsl, off in heads():
        t_tok = P("t", (lambda e, p=p, hh=hh, hsl=hsl, off=off:
                        e.matmul(pbv[2][hsl, p, :], wout[hsl, p, :],
                                 p_q[hsl, p, :], start=True, stop=True,
                                 skip_group_check=True, tile_position=(off, off))),
                  waits=[ck["upd_wout"], tok_m, tok_aev, tok_pq])
    tok_omm = t_tok
    ck["pq"] = tok_omm
    tok_oev = P("s", (lambda e, ci=ci:
                      e.activation(V["o_seg"][:, :, ci * 64:(ci + 1) * 64],
                                   pbv[2][:, 0:8, :], ACT.Copy)),
                waits=[tok_omm, ck["oseg"]])
    ck["b2"] = tok_oev
    ck["o_evac"] = tok_oev
    for p in range(NP):
        t_tok = P("t", (lambda e, p=p:
                        e.matmul(pbv[4][0:2, p, :], rk2[:, p, :], m_sb[:, p, :],
                                 start=True, stop=True, skip_group_check=True)),
                  waits=[tok_m, tok_rk2, ck["b4"]])
    tok_s2mm = t_tok
    ck["rk2"] = tok_s2mm
    ck["m"] = tok_s2mm
    tok_s2ev = P("s", lambda e: e.activation(s2_sb[:], pbv[4][0:2, 0:8, :],
                                             ACT.Copy),
                 waits=[tok_s2mm, ck["s2"]])
    tok_sb = P("t", (lambda e:
                     e.matmul(pb[4][:, :], tb["sel2"][:],
                              s2_sb.rearrange("p a b -> p (a b)"),
                              start=True, stop=True, skip_group_check=True)),
              waits=[tok_s2ev])
    ck["s2"] = tok_sb
    tok_t = P("v", lambda e: e.tensor_tensor(t_sb[:], pb[4].ap()
                                             .rearrange("p (a b) -> p a b", b=64),
                                             a_sb[:], OP.subtract),
              waits=[tok_sb, tok_aev])
    ck["b4"] = tok_t
    ck["a"] = tok_t
    tok_gz = P("v", (lambda e:
                     e.tensor_tensor(gzp[:, :, 0:64],
                                     t_sb[:], p_li[:], OP.mult)),
               waits=[tok_pli, tok_t, ck["gz"]])
    ck["pli"] = tok_gz

    for p in range(NP):
        t_tok = P("t", (lambda e, p=p:
                        e.transpose(pb6w[:, p, :], gzp[:, p, :], ident[:])),
                  waits=[tok_gz, tok_plo, ck["b6"]])
    tok_trgzp = t_tok
    ck["gz"] = tok_trgzp
    ck["plo"] = tok_trgzp
    tok_gzpt = P("s", lambda e: e.activation(gzpT_sb[:], pb6w[:, 0:8, :],
                                             ACT.Copy),
                 waits=[tok_trgzp, ck["gzpt"]])
    ck["b6"] = tok_gzpt

    for p, hh, hsl, off in heads():
        t_tok = P("t", (lambda e, p=p, hh=hh, hsl=hsl, off=off:
                        e.matmul(pbv[0][hsl, p, :],
                                 gzpT_sb[64:128, p, hh * 64:(hh + 1) * 64],
                                 kvT_sb[64:128, p, hh * 64:(hh + 1) * 64],
                                 start=True, stop=True, skip_group_check=True,
                                 tile_position=(64, off))),
                  waits=[tok_gzpt, tok_kvt, ck["b0"]])
    tok_g11 = t_tok
    for p, hh, hsl, off in heads():
        t_tok = P("t", (lambda e, p=p, hh=hh, hsl=hsl, off=off:
                        e.matmul(pbv[1][hsl, p, :],
                                 kvT_sb[0:64, p, hh * 64:(hh + 1) * 64],
                                 gzpT_sb[0:64, p, hh * 64:(hh + 1) * 64],
                                 start=True, stop=True, skip_group_check=True,
                                 tile_position=(0, off))),
                  waits=[ck["b1"]])
    tok_g13 = t_tok
    ck["kvt"] = tok_g13
    ck["gzpt"] = tok_g13
    if filler_cb:
        filler_cb(2)

    for p in range(NP):
        tok_ui = P("v", (lambda e, p=p:
                         e.scalar_tensor_tensor(winT[:, p, :], winT[:, p, :],
                                                lrwdB[:, 2, p, ci:ci + 1],
                                                pbv[1][:, p, :], op0=OP.mult,
                                                op1=OP.subtract)),
                   waits=[tok_g13, tok_z, ck["upd_win"], ck["lrwdB_ready"]])
    for p in range(NP):
        tok_uo = P("v", (lambda e, p=p:
                         e.scalar_tensor_tensor(wout[:, p, :], wout[:, p, :],
                                                lrwdB[:, 3, p, ci:ci + 1],
                                                pbv[0][:, p, :], op0=OP.mult,
                                                op1=OP.add)),
                   waits=[tok_g11, tok_omm, ck["upd_wout"]])
    ck["upd_win"] = tok_ui
    ck["upd_wout"] = tok_uo
    ck["b0"] = tok_uo
    ck["b1"] = tok_ui

    for p, hh, hsl, off in heads():
        t_tok = P("t", (lambda e, p=p, hh=hh, hsl=hsl, off=off:
                        e.transpose(pb5WT[hsl, p, :], wout[hsl, p, :],
                                    ident[hsl, hsl], tile_position=(off, off))),
                  waits=[tok_uo, ck["b5"]])
    tok_trw = t_tok
    tok_wt = P("s", lambda e: e.activation(woutT[:], pb5WT[:, 0:8, :], ACT.Copy),
               waits=[tok_trw])
    ck["b5"] = tok_wt
    ck["wt_evac"] = tok_wt


def w2_tile_src_g(dr, c):
    rt, cc = c // 4, c % 4
    return dr["w2"][rt * 128:(rt + 1) * 128, cc * 1024:(cc + 1) * 1024]


def t1_src(dr, i):
    if i < 32:
        return dr["w1"][i * 128:(i + 1) * 128, :]
    return w2_tile_src_g(dr, i - 32)


def emit_t1_prefetch(plan, P, dr, V, ck, n):
    wstream = V["wstream"]
    for _ in range(n):
        i = ck["t1_pf"]
        if i >= 64:
            return
        ck["t1_pf"] += 1
        par = i % 4
        srcap = t1_src(dr, i)
        ck["t1_dma"][par] = P("y", (lambda e, srcap=srcap, par=par:
                                    e.dma_start(wstream[:, par, :], srcap)),
                              waits=[ck["t1_ws"][par], ck["t1_dma"][par]],
                              dma=True)


def emit_t1_compute(plan, P, dr, V, ck, n):
    wstream, tabs = V["wstream"], V["tabs"]
    for _ in range(n):
        i = ck["t1_i"]
        if i >= 64:
            return
        ck["t1_i"] += 1
        par = i % 4
        acc = V["acc1"] if i < 32 else V["acc2"]
        tok_tb = P("v", (lambda e, par=par:
                         e.reduce_sum(tabs[:], wstream[:, par, :], axis=AX.X,
                                      apply_absolute_value=True)),
                   waits=[ck["t1_dma"][par], ck["t1_tabs"]])
        if i % 32 == 0:
            tok_a = P("v", (lambda e, acc=acc: e.tensor_copy(acc[:], tabs[:])),
                      waits=[tok_tb])
        else:
            tok_a = P("v", (lambda e, acc=acc:
                            e.tensor_tensor(acc[:], acc[:], tabs[:], OP.add)),
                      waits=[tok_tb])
        ck["t1_ws"][par] = tok_a
        ck["t1_tabs"] = tok_a
        emit_t1_prefetch(plan, P, dr, V, ck, 1)
    return


def pbj(pb, j):
    if j == 0:
        return pb[0][:, 0:512]
    if j == 1:
        return pb[0][:, 512:1024]
    return pb[2 + (j - 2)][:, :]


def emit_passB(nc, plan, P, dr, dbg_dr, tb, A, A4, bar_toks, L,
               b2bf, b1cols, ident, identf, ck, t1bufs):
    pb = tb["pb"]
    xt = tb["xt"]
    outt = tb["outt"]
    pb5w = pb[5].ap().rearrange("p (a b) -> p a b", b=128)
    pb6w = pb[6].ap().rearrange("p (a b) -> p a b", b=128)
    NBT = L // SEGB
    wstream = t1bufs["wstream"]
    acc1, acc2, tabs = t1bufs["acc1"], t1bufs["acc2"], t1bufs["tabs"]

    for e in Plan.ENGS:
        plan.op(e, lambda _e: None, waits=bar_toks, inc=False)

    w1T = A.alloc("w1T", [128, 8, FF])
    w2T = A.alloc("w2T", [128, 32, 1024])
    A.mark("tern_tmp")
    w12scr = A.alloc("w12scr", [128, 1024])
    sgA = A.alloc("sgA", [128, 1024])
    A.reset_to("tern_tmp")
    hT = A.alloc("hT", [128, 2, SEGB])
    xnB = A.alloc("xnB", [128, 2, 8, SEGB])
    s_rd = A.alloc("s_rd", [128, 2, 1024])
    g_sb = A.alloc("g_sb", [128, 2, 1024])

    tot = A4.alloc("totw", [1, 1])
    inv1 = A4.alloc("invw1", [128, 1])
    inv2 = A4.alloc("invw2", [128, 1])
    ones_col = A4.alloc("ones_col", [128, 1])
    ones_row = A4.alloc("ones_row", [1, 128])


    tok_o1 = P("v", lambda e: e.memset(ones_col[:], 1.0))
    tok_o2 = P("v", lambda e: e.memset(ones_row[:], 1.0))
    tabs_free = ck["t1_tabs"]
    for acc, inv in ((acc1, inv1), (acc2, inv2)):
        t_tok = P("t", (lambda e, acc=acc:
                        e.matmul(pb[7][0:1, 0:1], ones_col[:], acc[:],
                                 start=True, stop=True, skip_group_check=True)),
                  waits=[ck["t1_tabs"], tok_o1])
        tok_cp = P("v", lambda e: e.tensor_copy(tot[0:1, :], pb[7][0:1, 0:1]),
                   waits=[t_tok])
        t_tok = P("t", (lambda e:
                        e.matmul(pb[7][0:128, 1:2], ones_row[:], tot[0:1, :],
                                 start=True, stop=True, skip_group_check=True)),
                  waits=[tok_cp, tok_o2])
        tok_t1 = P("v", (lambda e: e.tensor_scalar(tabs[:], pb[7][0:128, 1:2],
                                                   1.0 / (FF * D), None,
                                                   op0=OP.mult)),
                   waits=[t_tok, tabs_free])
        tok_t2 = P("v", (lambda e: e.tensor_scalar(tabs[:], tabs[:], 1e-8, None,
                                                   op0=OP.max)), waits=[tok_t1])
        tok_i = P("v", (lambda e, inv=inv: e.reciprocal(inv[:], tabs[:])),
                  waits=[tok_t2])
        tabs_free = tok_i

    ws_free = list(ck["t1_ws"])
    ws_dma = list(ck["t1_dma"])
    scr_free = Z0
    tr_banks = [pb5w, pb6w]
    bank_ev = [Z0, Z0]
    tok_trev = Z0
    for wname, wT, inv in (("w1", w1T, inv1), ("w2", w2T, inv2)):
        for c in range(32):
            par = (c + (32 if wname == "w2" else 0)) % 4
            srcap = (dr["w1"][c * 128:(c + 1) * 128, :] if wname == "w1"
                     else w2_tile_src_g(dr, c))
            tok_l = P("y", (lambda e, srcap=srcap, par=par:
                            e.dma_start(wstream[:, par, :], srcap)),
                      waits=[ws_free[par], ws_dma[par]], dma=True)
            ws_dma[par] = tok_l
            tok_sa = P("s", (lambda e, par=par, inv=inv:
                             e.activation(w12scr[:], wstream[:, par, :], ACT.Sign,
                                          bias=-0.5, scale=inv[:])),
                       waits=[tok_l, tok_i, scr_free])
            tok_sb_ = P("s", (lambda e, par=par, inv=inv:
                              e.activation(sgA[:], wstream[:, par, :], ACT.Sign,
                                           bias=0.5, scale=inv[:])))
            ws_free[par] = tok_sb_
            tok_w = P("v", lambda e: e.tensor_tensor(w12scr[:], w12scr[:], sgA[:],
                                                     OP.add),
                      waits=[tok_sb_])
            pbx = tr_banks[par % 2]
            for j in range(8):
                t_tok = P("t", (lambda e, j=j, pbx=pbx:
                                e.transpose(pbx[:, j, :],
                                            w12scr[:, j * 128:(j + 1) * 128],
                                            ident[:])),
                          waits=[tok_w, bank_ev[par % 2]])
            scr_free = t_tok
            if wname == "w1":
                dst = w1T[:, :, c * 128:(c + 1) * 128]
            else:
                rt, cc = c // 4, c % 4
                dst = w2T[:, cc * 8:(cc + 1) * 8, rt * 128:(rt + 1) * 128]
            tok_trev = P("v", (lambda e, dst=dst, pbx=pbx:
                               e.tensor_copy(dst, pbx[:, :, :])),
                         waits=[t_tok])
            bank_ev[par % 2] = tok_trev
    tok_tern = tok_trev
    tok_tern_t = ("t", plan.cnt["t"])

    xt_free = list(ck["xt"])
    xdmaB = list(ck["xdma"])
    h_free = [Z0, Z0]
    bank_free = {i: Z0 for i in range(4)}
    mm1_free = {0: Z0, 1: tabs_free}
    g_free = [Z0, Z0]
    srd_free = Z0
    out_free = [Z0, Z0]
    xnB_free = [Z0, Z0]
    xnB_tok = [Z0, Z0]
    srd_tok = Z0

    def dma_xnB(bt):
        par = bt % 2
        tok = P("y", (lambda e, bt=bt, par=par:
                      e.dma_start(xnB[:, par, :, :],
                                  dr["xn_sp"][:, :, bt * SEGB:(bt + 1) * SEGB]
                                  .rearrange("a p t -> p a t"))),
                waits=[xnB_free[par], xnB_tok[par], tok_tern], dma=True)
        xnB_tok[par] = tok

    dma_xnB(0)
    for bt in range(NBT):
        par = bt % 2
        if bt + 1 < NBT:
            dma_xnB(bt + 1)
        srd_tok = P("y", (lambda e, bt=bt:
                          e.dma_start(s_rd[:, 0:2, :],
                                      dr["s_sp"][bt * SEGB:(bt + 1) * SEGB, :]
                                      .rearrange("(a p) f -> p a f", p=128))),
                    waits=[srd_free, srd_tok, tok_tern], dma=True)
        for tsub in range(2):
            gidx = bt * 2 + tsub
            slot = gidx % 4
            xt_tok = P("y", (lambda e, gidx=gidx, slot=slot:
                             e.dma_start(xt[:, slot, :],
                                         dr["x"][gidx * 128:(gidx + 1) * 128, :])),
                       waits=[xt_free[slot], xdmaB[slot]], dma=True)
            xdmaB[slot] = xt_tok
            if tsub == 0:
                xt_tok0 = xt_tok
            else:
                xt_tok1 = xt_tok
        for f1c in range(32):
            hpar = f1c % 2
            bank = pb[4] if hpar == 0 else pb[7]
            for fc in range(8):
                t_tok = P("t", (lambda e, f1c=f1c, fc=fc, bank=bank, par=par:
                                e.matmul(bank[:, 0:SEGB],
                                         w1T[:, fc, f1c * 128:(f1c + 1) * 128],
                                         xnB[:, par, fc, :], start=(fc == 0),
                                         stop=(fc == 7), skip_group_check=True)),
                          waits=[xnB_tok[par], mm1_free[hpar], tok_tern_t])
            tok_mm1 = t_tok
            if f1c > 0:
                pf1c, phpar = f1c - 1, (f1c - 1) % 2
                for tsub in range(2):
                    for half in range(2):
                        j = tsub * 2 + half
                        t_tok = P("t", (lambda e, pf1c=pf1c, phpar=phpar,
                                        tsub=tsub, half=half, j=j:
                                        e.matmul(pbj(pb, j),
                                                 hT[:, phpar, tsub * 128:(tsub + 1) * 128],
                                                 w2T[:, pf1c, half * 512:(half + 1) * 512],
                                                 start=(pf1c == 0), stop=False,
                                                 skip_group_check=True)),
                                  waits=[tok_gelu_prev, bank_free[j]])
                mm1_free[phpar] = t_tok
                h_free[phpar] = t_tok
            tok_gelu_prev = P("s", (lambda e, hpar=hpar, f1c=f1c, bank=bank:
                                    e.activation(hT[:, hpar, :], bank[:, 0:SEGB],
                                                 ACT.Gelu, scale=0.5,
                                                 bias=b1cols[:, f1c:f1c + 1])),
                              waits=[tok_mm1, h_free[hpar], tok_tern])
        pf1c, phpar = 31, 31 % 2
        for tsub in range(2):
            for half in range(2):
                j = tsub * 2 + half
                t_tok = P("t", (lambda e, pf1c=pf1c, phpar=phpar, tsub=tsub,
                                half=half, j=j:
                                e.matmul(pbj(pb, j),
                                         hT[:, phpar, tsub * 128:(tsub + 1) * 128],
                                         w2T[:, pf1c, half * 512:(half + 1) * 512],
                                         start=False, stop=False,
                                         skip_group_check=True)),
                          waits=[tok_gelu_prev, bank_free[j]])
        mm1_free[phpar] = t_tok
        h_free[phpar] = t_tok
        for tsub in range(2):
            for half in range(2):
                j = tsub * 2 + half
                t_tok = P("t", (lambda e, j=j, half=half:
                                e.matmul(pbj(pb, j), tb["two_row"][:],
                                         b2bf[0:1, half * 512:(half + 1) * 512],
                                         start=False, stop=True,
                                         skip_group_check=True)))
                tok_g = P("s", (lambda e, j=j, tsub=tsub, half=half:
                                e.activation(g_sb[:, tsub, half * 512:(half + 1) * 512],
                                             pbj(pb, j), ACT.Silu, scale=0.5)),
                          waits=[t_tok, g_free[tsub]])
                bank_free[j] = tok_g
            gidx = bt * 2 + tsub
            slot = gidx % 4
            opar = gidx % 2
            xt_tok = xt_tok0 if tsub == 0 else xt_tok1
            tok_c1 = P("v", (lambda e, tsub=tsub, opar=opar:
                             e.tensor_tensor(outt[:, opar, :], s_rd[:, tsub, :],
                                             g_sb[:, tsub, :], OP.mult)),
                       waits=[tok_g, srd_tok, out_free[opar]])
            tok_c2 = P("v", (lambda e, slot=slot, opar=opar:
                             e.tensor_tensor(outt[:, opar, :], outt[:, opar, :],
                                             xt[:, slot, :], OP.add)),
                       waits=[tok_c1, xt_tok])
            g_free[tsub] = tok_c2
            xt_free[slot] = tok_c2
            tok_od = P("g", (lambda e, gidx=gidx, opar=opar:
                             e.dma_start(dr["out"][gidx * 128:(gidx + 1) * 128, :],
                                         outt[:, opar, :])),
                       waits=[tok_c2], dma=True)
            out_free[opar] = tok_od
            if tsub == 1:
                srd_free = tok_c1
        xnB_free[par] = ("t", plan.cnt["t"])


_NC_CACHE = {}


def _get_program(L=4096):
    if L not in _NC_CACHE:
        _NC_CACHE[L] = build_program(L=L)
    return _NC_CACHE[L]


def kernel(L0_attn_x, L0_attn_Wq, L0_attn_Wk, L0_attn_Wv, L0_attn_Wlr,
           L0_attn_Wbeta, L0_attn_Wo, L0_attn_W_in_init, L0_attn_W_out_init,
           L0_ffn_W1, L0_ffn_b1, L0_ffn_W2, L0_ffn_b2):
    x = np.ascontiguousarray(np.asarray(L0_attn_x, dtype=np.float32))
    B, L, _ = x.shape
    base = {
        "wq": np.ascontiguousarray(np.asarray(L0_attn_Wq, np.float32)),
        "wk": np.ascontiguousarray(np.asarray(L0_attn_Wk, np.float32)),
        "wv": np.ascontiguousarray(np.asarray(L0_attn_Wv, np.float32)),
        "wo": np.ascontiguousarray(np.asarray(L0_attn_Wo, np.float32)),
        "wlr": np.ascontiguousarray(np.asarray(L0_attn_Wlr, np.float32)),
        "wbeta": np.ascontiguousarray(np.asarray(L0_attn_Wbeta, np.float32)),
        "winit": np.ascontiguousarray(
            np.asarray(L0_attn_W_in_init, np.float32).reshape(64, 16 * 64)),
        "woutit": np.ascontiguousarray(
            np.asarray(L0_attn_W_out_init, np.float32).reshape(64, 16 * 64)),
        "w1": np.ascontiguousarray(np.asarray(L0_ffn_W1, np.float32)),
        "b1": np.ascontiguousarray(np.asarray(L0_ffn_b1, np.float32)),
        "w2": np.ascontiguousarray(np.asarray(L0_ffn_W2, np.float32)),
        "b2": np.ascontiguousarray(np.asarray(L0_ffn_b2, np.float32)),
    }
    base.update(host_constants())
    nc = _get_program(L=L)
    in_maps = [dict(base, x=x[i]) for i in range(B)]
    res = run_bass_kernel_spmd(nc, in_maps, core_ids=list(range(B)))
    return np.stack([res.results[i]["out"] for i in range(B)], axis=0)
